# version 13
# baseline (speedup 1.0000x reference)
import numpy as np
import ml_dtypes
import concourse.bass as bass
import concourse.mybir as mybir
from concourse.bass_utils import run_bass_kernel_spmd

F32 = mybir.dt.float32
F32R = mybir.dt.float32r
BF16 = mybir.dt.bfloat16
U8 = mybir.dt.uint8
AF = mybir.ActivationFunctionType
ALU = mybir.AluOpType
AX = mybir.AxisListType.X

D = 1024
NH = 16
HD = 64
CH = 128
GRP = 256
HID = 4096
SEQ = 2048
NSEQ = 2
NGRP = SEQ // GRP
MASK_RAW = -262144.0
EXP_SHIFT = -12.0

ENG = ("T", "V", "S", "Q0", "Q1")
NPOOL = {"Q0": 12, "Q1": 8, "Q2": 8}
DIMS = (["T", "V", "S", "G"]
        + [(q, k) for q in ("Q0", "Q1", "Q2") for k in range(NPOOL[q])])


class Prog:

    def __init__(self, nc):
        self.nc = nc
        self.thunks = {e: [] for e in ENG}
        self.cnt = {e: 0 for e in ("T", "V", "S", "G")}
        self.dcnt = {q: 0 for q in ("Q0", "Q1", "Q2")}
        self.vc = {e: {d: 0 for d in DIMS} for e in ENG}
        self.snap = {d: [None] for d in DIMS}
        self.bufs = {}
        self.sems = {}
        self.pool_uses = {q: [0] * NPOOL[q] for q in ("Q0", "Q1", "Q2")}
        self.pend = {}

    def make_sems(self, estack):
        for e in ("T", "V", "S", "G"):
            self.sems[e] = estack.enter_context(self.nc.semaphore("sem" + e))
        for q in ("Q0", "Q1", "Q2"):
            for k in range(NPOOL[q]):
                self.sems[(q, k)] = estack.enter_context(
                    self.nc.semaphore("sem%s_%d" % (q, k)))

    def _need_for(self, reads, writes):
        need = {}
        for key in reads:
            b = self.bufs.get(key)
            if b and b[0] is not None:
                e, i = b[0]
                if i > need.get(e, 0):
                    need[e] = i
        for key in writes:
            b = self.bufs.get(key)
            if b:
                if b[0] is not None:
                    e, i = b[0]
                    if i > need.get(e, 0):
                        need[e] = i
                for e, i in b[1].items():
                    if i > need.get(e, 0):
                        need[e] = i
        return need

    def _emit_waits(self, eng, need):
        myvc = self.vc[eng]
        for d in sorted(need, key=str):
            i = need[d]
            if myvc[d] >= i:
                continue
            if isinstance(d, tuple):
                self.thunks[eng].append(("w", self.sems[d], 16 * i))
            else:
                self.thunks[eng].append(("w", self.sems[d], i))
            sn = self.snap[d][i]
            for dd, v in sn.items():
                if v > myvc[dd]:
                    myvc[dd] = v
            if i > myvc[d]:
                myvc[d] = i

    def _marker(self, eng):
        self.cnt[eng] += 1
        i = self.cnt[eng]
        sn = dict(self.vc[eng])
        sn[eng] = i
        self.snap[eng].append(sn)
        return (eng, i)

    def _register(self, reads, writes, marker):
        for key in writes:
            self.bufs[key] = [marker, {}]
        e, i = marker
        for key in reads:
            b = self.bufs.get(key)
            if b is None:
                b = [None, {}]
                self.bufs[key] = b
            if i > b[1].get(e, 0):
                b[1][e] = i

    def op(self, eng, emit, reads=(), writes=()):
        stream = "Q1" if eng == "G" else eng
        self._emit_waits(stream, self._need_for(reads, writes))
        self.cnt[eng] += 1
        i = self.cnt[eng]
        sn = dict(self.vc[stream])
        sn[eng] = i
        self.snap[eng].append(sn)
        self.thunks[stream].append(("i", emit, self.sems[eng], 1))
        self._register(reads, writes, (eng, i))

    def gop(self, eng, emit, reads=(), gkey=None, first=False, close=False,
            final=False):
        writes = (gkey,) if first else ()
        self._emit_waits(eng, self._need_for(reads, writes))
        pend = self.pend.setdefault((eng, gkey), set())
        pend.update(reads)
        if close:
            m = self._marker(eng)
            self.thunks[eng].append(("i", emit, self.sems[eng], 1))
            self._register(tuple(pend), (gkey,) if final else (), m)
            pend.clear()
            if final:
                self.pend.pop((eng, gkey), None)
        else:
            self.thunks[eng].append(("x", emit))

    def dma(self, q, out_ap, in_ap, reads=(), writes=()):
        stream = "S" if q == "Q2" else q
        R = NPOOL[q]
        idx = self.dcnt[q]
        self.dcnt[q] = idx + 1
        k = idx % R
        d = (q, k)
        prev_uses = self.pool_uses[q][k]
        if prev_uses > 0 and self.vc[stream][d] < prev_uses:
            self.thunks[stream].append(("w", self.sems[d], 16 * prev_uses))
            sn = self.snap[d][prev_uses]
            myvc = self.vc[stream]
            for dd, v in sn.items():
                if v > myvc[dd]:
                    myvc[dd] = v
        self._emit_waits(stream, self._need_for(reads, writes))
        self.thunks[stream].append(("d", out_ap, in_ap, self.sems[d]))
        u = prev_uses + 1
        self.pool_uses[q][k] = u
        sn = dict(self.vc[stream])
        sn[d] = u
        self.snap[d].append(sn)
        self._register(reads, writes, (d, u))

    def finalize_queues(self):
        for q in ("Q0", "Q1", "Q2"):
            stream = "S" if q == "Q2" else q
            for k in range(NPOOL[q]):
                u = self.pool_uses[q][k]
                if u > 0:
                    self.thunks[stream].append(("w", self.sems[(q, k)], 16 * u))

    def render(self):
        nc = self.nc

        def runner(e):
            lst = self.thunks[e]
            def run(engine_obj):
                for t in lst:
                    tag = t[0]
                    if tag == "w":
                        engine_obj.wait_ge(t[1], t[2])
                    elif tag == "i":
                        t[1](engine_obj).then_inc(t[2], t[3])
                    elif tag == "x":
                        t[1](engine_obj)
                    else:
                        engine_obj.dma_start(out=t[1], in_=t[2]).then_inc(t[3], 16)
            return run

        with nc.Block() as block:
            block.tensor(runner("T"))
            block.vector(runner("V"))
            block.scalar(runner("S"))
            block.sync(runner("Q0"))
            block.gpsimd(runner("Q1"))



def build_program(n_seq=NSEQ, n_grp=NGRP):
    import contextlib
    nc = bass.Bass("TRN2", target_bir_lowering=False, debug=False)

    x_d = nc.dram_tensor("x", [NSEQ * SEQ, D], F32, kind="ExternalInput").ap()
    y_d = nc.dram_tensor("y", [NSEQ * SEQ, D], F32, kind="ExternalOutput").ap()
    wt_d = {}
    for nm, t in (("wq", "wqt"), ("wk", "wkt"), ("wv", "wvt"), ("wo", "wot")):
        wt_d[nm] = nc.dram_tensor(t, [D, D], F32, kind="ExternalInput").ap()
    k0_d = nc.dram_tensor("k0", [2 * D, HID], U8, kind="ExternalInput").ap()
    k1s_d = nc.dram_tensor("k1s", [HID, HID], U8, kind="ExternalInput").ap()
    k2_d = nc.dram_tensor("k2", [HID, D], U8, kind="ExternalInput").ap()
    scal_d = nc.dram_tensor("scal", [128, 4], F32, kind="ExternalInput").ap()
    cmsk_d = nc.dram_tensor("cmsk", [128, 256], BF16, kind="ExternalInput").ap()
    cmsk0_d = nc.dram_tensor("cmsk0", [128, 256], BF16, kind="ExternalInput").ap()
    cid_d = nc.dram_tensor("cid", [128, 128], BF16, kind="ExternalInput").ap()
    cones_d = nc.dram_tensor("cones", [128, 1], BF16, kind="ExternalInput").ap()
    conesr_d = nc.dram_tensor("conesr", [1, 128], BF16, kind="ExternalInput").ap()
    cn12_d = nc.dram_tensor("cn12", [128, 1], F32, kind="ExternalInput").ap()
    wsp_d = {}
    for nm in ("wq", "wk", "wv", "wo"):
        wsp_d[nm] = nc.dram_tensor(
            "%s_hl" % nm, [D, 2 * D], BF16, kind="Internal").ap()
    w2pm_d = nc.dram_tensor("w2pm", [HID, D], F32R, kind="Internal").ap()
    w1pm_d = nc.dram_tensor("w1pm", [HID, HID], F32R, kind="Internal").ap()

    estack = contextlib.ExitStack()
    with estack:
        A = nc.alloc_sbuf_tensor
        s_cid = A("s_cid", [128, 128], BF16)
        s_msk = A("s_msk", [128, 256], BF16)
        s_msk0 = A("s_msk0", [128, 256], BF16)
        s_ones = A("s_ones", [128, 1], BF16)
        s_onesr = A("s_onesr", [1, 128], BF16)
        s_scal = A("s_scal", [128, 4], F32)
        s_n12 = A("s_n12", [128, 1], F32)
        s_S0 = A("s_S0", [128, 1], F32)
        s_S1 = A("s_S1", [128, 1], F32)
        s_S2 = A("s_S2", [128, 1], F32)
        s_d0 = A("s_d0", [128, 8 * HID], BF16)
        s_kcar = {p: A("s_kcar_" + p, [128, 8 * 128], BF16) for p in ("hi", "lo")}
        s_vcar = {p: A("s_vcar_" + p, [128, D], BF16) for p in ("hi", "lo")}
        s_xacc = [A("s_xacc%d" % b, [128, 2 * D], F32) for b in range(2)]
        s_xnf = A("s_xnf", [128, D], F32)
        s_xnt = {p: A("s_xnt_" + p, [128, D], BF16) for p in ("hi", "lo")}
        s_xn = {p: A("s_xn_" + p, [128, 8 * GRP], BF16) for p in ("hi", "lo")}
        s_q = {p: A("s_q_" + p, [128, 8 * GRP], BF16) for p in ("hi", "lo")}
        s_k = {p: A("s_k_" + p, [128, 8 * GRP], BF16) for p in ("hi", "lo")}
        s_v = {p: A("s_v_" + p, [128, 2 * D], BF16) for p in ("hi", "lo")}
        s_O = {p: A("s_O_" + p, [128, 8 * GRP], BF16) for p in ("hi", "lo")}
        s_P0 = A("s_P0", [128, 2 * 256], F32)
        s_Ph = A("s_Ph", [128, 2 * 256], BF16)
        s_Pl = A("s_Pl", [128, 2 * 256], BF16)
        s_PT = A("s_PT", [128, 2 * 512], BF16)
        s_sums = A("s_sums", [128, NH], F32)
        s_rcp = A("s_rcp", [128, NH], F32)
        s_wbuf = [A("s_wbuf%d" % b, [128, 2 * D], BF16) for b in range(2)]
        s_a0 = A("s_a0", [128, 32 * GRP], F32R)
        s_a1 = A("s_a1", [128, 2 * GRP], F32R)
        s_k1b = [A("s_k1b%d" % b, [128, HID // 4], F32R) for b in range(3)]
        s_w2b = [A("s_w2b%d" % b, [128, D], F32R) for b in range(2)]
        s_ln = A("s_ln", [128, 16], F32)

        ps = [nc.alloc_psum_tensor("ps%d" % i, [128, 512], F32) for i in range(6)]
        psb = [nc.alloc_psum_tensor("pb%d" % i, [128, 1024], BF16) for i in range(2)]

        pr = Prog(nc)
        pr.make_sems(estack)

        def ts(eng, out_ap, in_ap, s1, s2, op0, op1, reads, writes):
            def emit(e, o=out_ap, i=in_ap, s1=s1, s2=s2, op0=op0, op1=op1):
                return e.tensor_scalar(out=o, in0=i, scalar1=s1, scalar2=s2,
                                       op0=op0, op1=op1)
            pr.op(eng, emit, reads=reads, writes=writes)

        def stt(eng, out_ap, in0_ap, sc, in1_ap, op0, op1, reads, writes):
            def emit(e, o=out_ap, i0=in0_ap, sc=sc, i1=in1_ap, op0=op0, op1=op1):
                return e.scalar_tensor_tensor(out=o, in0=i0, scalar=sc, in1=i1,
                                              op0=op0, op1=op1)
            pr.op(eng, emit, reads=reads, writes=writes)

        def tt(eng, out_ap, a_ap, b_ap, alu, reads, writes):
            def emit(e, o=out_ap, a=a_ap, b=b_ap, alu=alu):
                return e.tensor_tensor(out=o, in0=a, in1=b, op=alu)
            pr.op(eng, emit, reads=reads, writes=writes)

        def cp(eng, out_ap, in_ap, reads, writes):
            if eng == "S":
                def emit(e, o=out_ap, i=in_ap):
                    return e.copy(out=o, in_=i)
            else:
                def emit(e, o=out_ap, i=in_ap):
                    return e.tensor_copy(out=o, in_=i)
            pr.op(eng, emit, reads=reads, writes=writes)

        def act(out_ap, in_ap, func, bias, scale, accum, reads, writes):
            def emit(e, o=out_ap, i=in_ap, f=func, b=bias, s=scale, a=accum):
                return e.activation(out=o, in_=i, func=f, bias=b, scale=s,
                                    accum_out=a)
            pr.op("S", emit, reads=reads, writes=writes)

        def mm(gkey, out_ap, lhsT, rhs, reads, start, stop, close, final, tp=None):
            def emit(e, o=out_ap, l=lhsT, r=rhs, st=start, sp=stop, tp=tp):
                return e.matmul(o, l, r, start=st, stop=sp, tile_position=tp)
            pr.gop("T", emit, reads=reads, gkey=gkey, first=start, close=close,
                   final=final)

        def tr(gkey, out_ap, in_ap, reads, first, close, final):
            def emit(e, o=out_ap, i=in_ap):
                return e.transpose(o, i, s_cid[:])
            pr.gop("T", emit, reads=tuple(reads) + (("cid",),), gkey=gkey,
                   first=first, close=close, final=final)

        def wbuf_slice(b, p, c0, c1):
            base = 0 if p == "hi" else D
            return s_wbuf[b][:, base + c0:base + c1]

        def wbuf_key(b, p):
            return ("wbuf", b)

        def wbuf_full(b):
            return s_wbuf[b][:, 0:2 * D]

        def pre_consts():
            pr.dma("Q0", s_cid[:], cid_d, writes=[("cid",)])
            pr.dma("Q0", s_msk[:], cmsk_d, writes=[("msk",)])
            pr.dma("Q0", s_msk0[:], cmsk0_d, writes=[("msk0",)])
            pr.dma("Q0", s_ones[:], cones_d, writes=[("ones",)])
            pr.dma("Q0", s_onesr[:], conesr_d, writes=[("onesr",)])
            pr.dma("Q0", s_scal[:], scal_d, writes=[("scal",)])
            pr.dma("Q0", s_n12[:], cn12_d, writes=[("n12",)])
            ts("V", s_S0[:], s_scal[:, 0:1], 4.0, None, ALU.mult, ALU.bypass,
               reads=[("scal",)], writes=[("S0",)])
            ts("V", s_S1[:], s_scal[:, 1:2], 2.0, None, ALU.mult, ALU.bypass,
               reads=[("scal",)], writes=[("S1",)])
            ts("V", s_S2[:], s_scal[:, 2:3], 1.0, None, ALU.mult, ALU.bypass,
               reads=[("scal",)], writes=[("S2",)])

        def pre_wsplit(nm, xbuf=0):
            for ic in range(8):
                sl = ic % 2
                wf32 = s_xacc[xbuf][:, sl * D:(sl + 1) * D]
                fk = ("xacc", xbuf, sl)
                whl_hi = s_wbuf[sl][:, 0:D]
                whl_lo = s_wbuf[sl][:, D:2 * D]
                wk_ = ("wbuf", sl)
                r0 = ic * 128
                pr.dma("Q0", wf32, wt_d[nm][r0:r0 + 128, :],
                       reads=[fk], writes=[fk])
                cp("V", whl_hi, wf32, reads=[fk], writes=[wk_])
                stt("V", whl_lo, wf32, 1.0, whl_hi, ALU.mult, ALU.subtract,
                    reads=[fk, wk_], writes=[wk_])
                pr.dma("Q0", wsp_d[nm][r0:r0 + 128, 0:D], whl_hi,
                       reads=[wk_], writes=[("wspd", nm, "hi", ic)])
                pr.dma("Q0", wsp_d[nm][r0:r0 + 128, D:2 * D], whl_lo,
                       reads=[wk_], writes=[("wspd", nm, "lo", ic)])

        def okeys_span(p):
            return [("O", p, ob, cc) for ob in range(2) for cc in range(2)]

        def pre_d0():
            H2 = HID // 2
            for half in range(16):
                ic, hf = divmod(half, 2)
                bt = s_O["hi"][:, 0:H2]
                bb = s_O["lo"][:, 0:H2]
                kt = okeys_span("hi")
                kb_ = okeys_span("lo")
                pr.dma("Q1", bt, k0_d[ic * 128:(ic + 1) * 128, hf * H2:(hf + 1) * H2],
                       reads=kt, writes=kt)
                pr.dma("Q1", bb, k0_d[D + ic * 128:D + (ic + 1) * 128,
                                      hf * H2:(hf + 1) * H2],
                       reads=kb_, writes=kb_)
                tt("V", s_d0[:, ic * HID + hf * H2:ic * HID + (hf + 1) * H2],
                   bt, bb, ALU.subtract, reads=kt + kb_, writes=[("d0", ic)])

        def pre_w2():
            for oc in range(32):
                sl = oc % 2
                bin_ = s_O["hi"][:, sl * D:(sl + 1) * D]
                bout = s_O["lo"][:, sl * D:(sl + 1) * D]
                ki = okeys_span("hi")
                ko = okeys_span("lo")
                r0 = oc * 128
                pr.dma("Q1", bin_, k2_d[r0:r0 + 128, :], reads=ki, writes=ki)
                def em_ts2(e, o=bout, i=bin_):
                    return e.tensor_scalar(out=o, in0=i, scalar1=2.0,
                                           scalar2=-1.0, op0=ALU.mult,
                                           op1=ALU.add)
                pr.op("G", em_ts2, reads=ki + ko, writes=ko)
                pr.dma("Q1", w2pm_d[r0:r0 + 128, :], bout,
                       reads=ko, writes=[("w2pmd", oc)])

        def pre_w1(lo, hi):
            H2 = HID // 2
            for half in range(2 * lo, 2 * hi):
                m2, hf = divmod(half, 2)
                bin_ = s_O["hi"][:, 0:H2]
                bout = s_O["lo"][:, 0:H2]
                ki = okeys_span("hi")
                ko = okeys_span("lo")
                r0 = m2 * 128
                pr.dma("Q1", bin_, k1s_d[r0:r0 + 128, hf * H2:(hf + 1) * H2],
                       reads=ki, writes=ki)
                def em_ts(e, o=bout, i=bin_):
                    return e.tensor_scalar(out=o, in0=i, scalar1=2.0,
                                           scalar2=-1.0, op0=ALU.mult,
                                           op1=ALU.add)
                pr.op("G", em_ts, reads=ki + ko, writes=ko)
                pr.dma("Q1", w1pm_d[r0:r0 + 128, hf * H2:(hf + 1) * H2], bout,
                       reads=ko, writes=[("w1pmd", m2, hf)])

        xn_keys = [("xn", p, cc) for p in ("hi", "lo") for cc in range(2)]

        r1024 = 1.0 / 1024.0

        groups = [(sq, g) for sq in range(n_seq) for g in range(n_grp)]

        def phase_X_ln(gi, cc):
            sq, g = groups[gi]
            gb = gi % 2
            tok0 = sq * SEQ + g * GRP
            xa = s_xacc[gb]
            if True:
                r0 = tok0 + cc * 128
                xt = xa[:, cc * D:(cc + 1) * D]
                kx = ("xacc", gb, cc)
                L = lambda i, cc=cc: ("ln", cc, i)
                lc = s_ln[:, cc * 8:(cc + 1) * 8]
                pr.dma("Q0", xt, x_d[r0:r0 + 128, :], reads=[kx], writes=[kx])
                def em_rs(e, o=lc[:, 0:1], i=xt):
                    return e.reduce_sum(o, i, AX)
                pr.op("V", em_rs, reads=[kx], writes=[L(0)])
                ts("V", lc[:, 1:2], lc[:, 0:1], r1024, None, ALU.mult,
                   ALU.bypass, reads=[L(0)], writes=[L(1)])
                act(s_xnt["hi"][:, 0:D], xt, AF.Square,
                    0.0, 1.0, lc[:, 2:3],
                    reads=[kx], writes=[("xnt", "hi"), L(2)])
                ts("V", lc[:, 3:4], lc[:, 2:3], r1024, None, ALU.mult,
                   ALU.bypass, reads=[L(2)], writes=[L(3)])
                tt("V", lc[:, 4:5], lc[:, 1:2], lc[:, 1:2], ALU.mult,
                   reads=[L(1)], writes=[L(4)])
                tt("V", lc[:, 5:6], lc[:, 3:4], lc[:, 4:5], ALU.subtract,
                   reads=[L(3), L(4)], writes=[L(5)])
                ts("V", lc[:, 5:6], lc[:, 5:6], 1.0, 1e-5, ALU.mult, ALU.add,
                   reads=[L(5)], writes=[L(5)])
                def em_sq(e, o=lc[:, 6:7], i=lc[:, 5:6]):
                    return e.sqrt(o, i)
                pr.op("S", em_sq, reads=[L(5)], writes=[L(6)])
                def em_rc(e, o=lc[:, 7:8], i=lc[:, 6:7]):
                    return e.reciprocal(o, i)
                pr.op("V", em_rc, reads=[L(6)], writes=[L(7)])
                xnf = s_xnf[:, 0:D]
                ts("V", xnf, xt, lc[:, 1:2], None, ALU.subtract, ALU.bypass,
                   reads=[kx, L(1)], writes=[("xnf",)])
                ts("V", s_xnt["hi"][:, 0:D], xnf,
                   lc[:, 7:8], None, ALU.mult, ALU.bypass,
                   reads=[("xnf",), L(7)], writes=[("xnt", "hi")])
                stt("V", s_xnt["lo"][:, 0:D], xnf,
                    lc[:, 7:8], s_xnt["hi"][:, 0:D],
                    ALU.mult, ALU.subtract,
                    reads=[("xnf",), L(7), ("xnt", "hi")],
                    writes=[("xnt", "lo")])

        def phase_X_tr(gi, cc):
            for ip, p in enumerate(("hi", "lo")):
                bk = ("pb", ip)
                for ic in range(8):
                    src_ = s_xnt[p][:, ic * 128:(ic + 1) * 128]
                    dst = psb[ip][:, ic * 128:(ic + 1) * 128]
                    tr(bk, dst, src_, [("xnt", p)],
                       first=(ic == 0), close=(ic == 7), final=(ic == 7))
                dsts = bass.AP(s_xn[p], cc * 128,
                               [[8 * GRP, 128], [GRP, 8], [1, 128]])
                srcs = bass.AP(psb[ip], 0,
                               [[1024, 128], [128, 8], [1, 128]])
                cp("V", dsts, srcs, reads=[bk], writes=[("xn", p, cc)])

        def phase_P_qk(gi):
            for nm, dsthl, okey in (("wq", s_q, "q"), ("wk", s_k, "k")):
                for ic in range(8):
                    bsel = ic % 2
                    wkeys = [wbuf_key(bsel, "hi")]
                    pr.dma("Q0", wbuf_full(bsel),
                           wsp_d[nm][ic * 128:(ic + 1) * 128, :],
                           reads=wkeys + [("wspd", nm, "hi", ic),
                                          ("wspd", nm, "lo", ic)],
                           writes=wkeys)
                    for m in range(8):
                        bank = m // 2
                        bk = ("ps", bank)
                        lh = wbuf_slice(bsel, "hi", m * 128, (m + 1) * 128)
                        ll = wbuf_slice(bsel, "lo", m * 128, (m + 1) * 128)
                        rh = s_xn["hi"][:, ic * GRP:(ic + 1) * GRP]
                        rl = s_xn["lo"][:, ic * GRP:(ic + 1) * GRP]
                        out = ps[bank][:, (m % 2) * GRP:(m % 2 + 1) * GRP]
                        rds = wkeys + xn_keys
                        mm(bk, out, lh, rh, rds,
                           start=(ic == 0 and m % 2 == 0), stop=False,
                           close=False, final=False)
                        mm(bk, out, lh, rl, [], start=False, stop=False,
                           close=False, final=False)
                        mm(bk, out, ll, rh, [], start=False,
                           stop=(ic == 7 and m % 2 == 1),
                           close=True, final=(ic == 7 and m % 2 == 1))
                for pair in range(4):
                    bk = ("ps", pair)
                    hi = dsthl["hi"][:, pair * 512:(pair + 1) * 512]
                    lo = dsthl["lo"][:, pair * 512:(pair + 1) * 512]
                    cp("S", hi, ps[pair][:, 0:512], reads=[bk],
                       writes=[(okey, "hi", pair)])
                    stt("V", lo, ps[pair][:, 0:512], 1.0, hi, ALU.mult,
                        ALU.subtract, reads=[bk, (okey, "hi", pair)],
                        writes=[(okey, "lo", pair)])

        def phase_P_v(gi):
            for ic in range(8):
                bsel = ic % 2
                wkeys = [wbuf_key(bsel, "hi")]
                pr.dma("Q0", wbuf_full(bsel),
                       wsp_d["wv"][ic * 128:(ic + 1) * 128, :],
                       reads=wkeys + [("wspd", "wv", "hi", ic),
                                      ("wspd", "wv", "lo", ic)],
                       writes=wkeys)
                for tc in range(2):
                    xh = s_xn["hi"][:, ic * GRP + tc * 128:
                                    ic * GRP + (tc + 1) * 128]
                    xl = s_xn["lo"][:, ic * GRP + tc * 128:
                                    ic * GRP + (tc + 1) * 128]
                    rds = wkeys + xn_keys
                    for oh in range(2):
                        bank = tc * 2 + oh
                        wh = wbuf_slice(bsel, "hi", oh * 512, (oh + 1) * 512)
                        wl = wbuf_slice(bsel, "lo", oh * 512, (oh + 1) * 512)
                        out = ps[bank][:, 0:512]
                        mm(("ps", bank), out, xh, wh, rds, start=(ic == 0),
                           stop=False, close=False, final=False)
                        mm(("ps", bank), out, xh, wl, [], start=False,
                           stop=False, close=False, final=False)
                    for oh in range(2):
                        bank = tc * 2 + oh
                        wh = wbuf_slice(bsel, "hi", oh * 512, (oh + 1) * 512)
                        out = ps[bank][:, 0:512]
                        mm(("ps", bank), out, xl, wh, [], start=False,
                           stop=(ic == 7), close=True, final=(ic == 7))
            for tc in range(2):
                for oh in range(2):
                    bank = tc * 2 + oh
                    bk = ("ps", bank)
                    hi = s_v["hi"][:, tc * D + oh * 512:tc * D + (oh + 1) * 512]
                    lo = s_v["lo"][:, tc * D + oh * 512:tc * D + (oh + 1) * 512]
                    cp("S", hi, ps[bank][:, 0:512], reads=[bk],
                       writes=[("v", "hi", tc, oh)])
                    stt("V", lo, ps[bank][:, 0:512], 1.0, hi, ALU.mult,
                        ALU.subtract, reads=[bk, ("v", "hi", tc, oh)],
                        writes=[("v", "lo", tc, oh)])

        def phase_A(gi):
            sq, g = groups[gi]
            for cc in range(2):
                seq_start = (g == 0 and cc == 0)
                msk_s = s_msk0 if seq_start else s_msk
                mkey = ("msk0",) if seq_start else ("msk",)

                def kap(p, h, blk):
                    pbase = 64 * (h % 2)
                    m = h // 2
                    if blk == 1 or seq_start:
                        off = m * GRP + cc * 128
                        return (s_k[p][pbase:pbase + 64, off:off + 128],
                                ("k", p, m // 2))
                    if cc == 1:
                        return (s_k[p][pbase:pbase + 64, m * GRP:m * GRP + 128],
                                ("k", p, m // 2))
                    return (s_kcar[p][pbase:pbase + 64, m * 128:(m + 1) * 128],
                            ("kcar", p))

                def vap(p, h, blk):
                    c0 = h * HD
                    if blk == 1 or seq_start:
                        return (s_v[p][:, cc * D + c0:cc * D + c0 + HD],
                                ("v", p, cc, c0 // 512))
                    if cc == 1:
                        return (s_v[p][:, c0:c0 + HD], ("v", p, 0, c0 // 512))
                    return (s_vcar[p][:, c0:c0 + HD], ("vcar", p))

                def s_wave(w):
                    sbank = 2 + (w % 4)
                    sbk = ("ps", sbank)
                    for hh in range(2):
                        h = 2 * w + hh
                        pb = 64 * (h % 2)
                        m = h // 2
                        tp = (pb, 0) if pb else None
                        c0 = m * GRP + cc * 128
                        qh = s_q["hi"][pb:pb + 64, c0:c0 + 128]
                        ql = s_q["lo"][pb:pb + 64, c0:c0 + 128]
                        qkeys = [("q", "hi", m // 2), ("q", "lo", m // 2)]
                        omsk = ps[sbank][:, hh * 256:hh * 256 + 256]
                        if cc == 1 and not seq_start:
                            kh = s_k["hi"][pb:pb + 64, m * GRP:m * GRP + 256]
                            kl = s_k["lo"][pb:pb + 64, m * GRP:m * GRP + 256]
                            rds = qkeys + [("k", "hi", m // 2),
                                           ("k", "lo", m // 2)]
                            mm(sbk, omsk, qh, kh, rds, start=(hh == 0),
                               stop=False, close=False, final=False, tp=tp)
                            mm(sbk, omsk, qh, kl, [], start=False, stop=False,
                               close=False, final=False, tp=tp)
                            mm(sbk, omsk, ql, kh, [], start=False, stop=False,
                               close=False, final=False, tp=tp)
                        else:
                            k0h, kk0h = kap("hi", h, 0)
                            k0l, kk0l = kap("lo", h, 0)
                            k1h, kk1h = kap("hi", h, 1)
                            k1l, kk1l = kap("lo", h, 1)
                            o0 = ps[sbank][:, hh * 256:hh * 256 + 128]
                            o1 = ps[sbank][:, hh * 256 + 128:hh * 256 + 256]
                            rds = qkeys + [kk0h, kk0l, kk1h, kk1l]
                            mm(sbk, o0, qh, k0h, rds, start=(hh == 0),
                               stop=False, close=False, final=False, tp=tp)
                            mm(sbk, o0, qh, k0l, [], start=False, stop=False,
                               close=False, final=False, tp=tp)
                            mm(sbk, o1, qh, k1h, [], start=False, stop=False,
                               close=False, final=False, tp=tp)
                            mm(sbk, o1, qh, k1l, [], start=False, stop=False,
                               close=False, final=False, tp=tp)
                            mm(sbk, o0, ql, k0h, [], start=False, stop=False,
                               close=False, final=False, tp=tp)
                            mm(sbk, o1, ql, k1h, [], start=False, stop=False,
                               close=False, final=False, tp=tp)
                        mm(sbk, omsk, s_cid[:], msk_s[:],
                           [mkey, ("cid",)], start=False, stop=(hh == 1),
                           close=True, final=(hh == 1))

                def head_work(w):
                    sbank = 2 + (w % 4)
                    sbk = ("ps", sbank)
                    for hh in range(2):
                        h = 2 * w + hh
                        slot = h % 2
                        act(s_P0[:, slot * 256:(slot + 1) * 256],
                            ps[sbank][:, hh * 256:hh * 256 + 256],
                            AF.Exp, s_n12[:, 0:1], 0.125,
                            s_sums[:, h:h + 1],
                            reads=[sbk, ("n12",)],
                            writes=[("P0", slot), ("sums", h)])
                    def em_rcp(e, o=s_rcp[:, 2 * w:2 * w + 2],
                               i=s_sums[:, 2 * w:2 * w + 2]):
                        return e.reciprocal(o, i)
                    pr.op("V", em_rcp,
                          reads=[("sums", 2 * w), ("sums", 2 * w + 1)],
                          writes=[("rcp", w)])
                    for hh in range(2):
                        h = 2 * w + hh
                        slot = h % 2
                        pslot = h % 2
                        p0 = s_P0[:, slot * 256:(slot + 1) * 256]
                        ph = s_Ph[:, pslot * 256:(pslot + 1) * 256]
                        pl = s_Pl[:, pslot * 256:(pslot + 1) * 256]
                        rc = s_rcp[:, h:h + 1]
                        ts("V", ph, p0, rc, None, ALU.mult, ALU.bypass,
                           reads=[("P0", slot), ("rcp", w)],
                           writes=[("Ph", pslot)])
                        stt("V", pl, p0, rc, ph, ALU.mult, ALU.subtract,
                            reads=[("P0", slot), ("rcp", w), ("Ph", pslot)],
                            writes=[("Pl", pslot)])
                    for hh in range(2):
                        h = 2 * w + hh
                        pslot = h % 2
                        ptbank = h % 2
                        ptk = ("pb", ptbank)
                        for idx in range(4):
                            ti, blk = divmod(idx, 2)
                            sap = (s_Ph if ti == 0 else s_Pl)[
                                :, pslot * 256 + blk * 128:
                                pslot * 256 + (blk + 1) * 128]
                            dst = psb[ptbank][:, idx * 128:(idx + 1) * 128]
                            tr(ptk, dst, sap,
                               [("Ph", pslot), ("Pl", pslot)],
                               first=(idx == 0), close=(idx == 3),
                               final=(idx == 3))
                    for hh in range(2):
                        h = 2 * w + hh
                        pslot = h % 2
                        ptk = ("pb", h % 2)
                        dpt = s_PT[:, pslot * 512:(pslot + 1) * 512]
                        cp("V", dpt, psb[h % 2][:, 0:512], reads=[ptk],
                           writes=[("PT", pslot)])
                    for hh in range(2):
                        h = 2 * w + hh
                        pslot = h % 2
                        obank = w // 4
                        obk = ("ps", obank)
                        pb = 64 * (h % 2)
                        qq = w % 4
                        oout = ps[obank][pb:pb + 64, qq * 128:(qq + 1) * 128]
                        otp = (0, pb) if pb else None
                        seqm = []
                        for blk in range(2):
                            vh, vkh = vap("hi", h, blk)
                            vl, vkl = vap("lo", h, blk)
                            pthi = s_PT[:, pslot * 512 + blk * 128:
                                        pslot * 512 + (blk + 1) * 128]
                            ptlo = s_PT[:, pslot * 512 + 256 + blk * 128:
                                        pslot * 512 + 256 + (blk + 1) * 128]
                            seqm.append((vh, pthi, [vkh, ("PT", pslot)]))
                            seqm.append((vh, ptlo, []))
                            seqm.append((vl, pthi, [vkl]))
                        for si, (lh, rr, rds) in enumerate(seqm):
                            mm(obk, oout, lh, rr, rds,
                               start=(si == 0), stop=(si == 5),
                               close=(si == 5),
                               final=(w % 4 == 3 and hh == 1 and si == 5),
                               tp=otp)

                for w0 in range(3):
                    s_wave(w0)
                for w in range(NH // 2):
                    if w + 3 < NH // 2:
                        s_wave(w + 3)
                    head_work(w)
                for ob in range(2):
                    obk = ("ps", ob)
                    dsts_h = bass.AP(s_O["hi"], ob * 4 * GRP + cc * 128,
                                     [[8 * GRP, 128], [GRP, 4], [1, 128]])
                    dsts_l = bass.AP(s_O["lo"], ob * 4 * GRP + cc * 128,
                                     [[8 * GRP, 128], [GRP, 4], [1, 128]])
                    srcs = bass.AP(ps[ob], 0,
                                   [[512, 128], [128, 4], [1, 128]])
                    cp("S", dsts_h, srcs, reads=[obk],
                       writes=[("O", "hi", ob, cc)])
                    stt("V", dsts_l, srcs, 1.0, dsts_h, ALU.mult,
                        ALU.subtract, reads=[obk, ("O", "hi", ob, cc)],
                        writes=[("O", "lo", ob, cc)])
            for p in ("hi", "lo"):
                dsts = bass.AP(s_kcar[p], 0, [[8 * 128, 128], [128, 8], [1, 128]])
                srcs = bass.AP(s_k[p], 128, [[8 * GRP, 128], [GRP, 8], [1, 128]])
                cp("V", dsts, srcs,
                   reads=[("k", p, m) for m in range(4)], writes=[("kcar", p)])
                cp("V", s_vcar[p][:], s_v[p][:, D:2 * D],
                   reads=[("v", p, 1, 0), ("v", p, 1, 1)], writes=[("vcar", p)])

        def phase_Wo(gi):
            gb = gi % 2
            xa = s_xacc[gb]
            for dc in range(8):
                bsel = dc % 2
                wkeys = [wbuf_key(bsel, "hi")]
                pr.dma("Q0", wbuf_full(bsel),
                       wsp_d["wo"][dc * 128:(dc + 1) * 128, :],
                       reads=wkeys + [("wspd", "wo", "hi", dc),
                                      ("wspd", "wo", "lo", dc)],
                       writes=wkeys)
                for tc in range(2):
                    ohi = s_O["hi"][:, dc * GRP + tc * 128:
                                    dc * GRP + (tc + 1) * 128]
                    olo = s_O["lo"][:, dc * GRP + tc * 128:
                                    dc * GRP + (tc + 1) * 128]
                    okeys = wkeys + [("O", "hi", dc // 4, tc),
                                     ("O", "lo", dc // 4, tc)]
                    for oh in range(2):
                        bank = tc * 2 + oh
                        wh = wbuf_slice(bsel, "hi", oh * 512, (oh + 1) * 512)
                        wl = wbuf_slice(bsel, "lo", oh * 512, (oh + 1) * 512)
                        out = ps[bank][:, 0:512]
                        mm(("ps", bank), out, ohi, wh, okeys, start=(dc == 0),
                           stop=False, close=False, final=False)
                        mm(("ps", bank), out, ohi, wl, [], start=False,
                           stop=False, close=False, final=False)
                    for oh in range(2):
                        bank = tc * 2 + oh
                        wh = wbuf_slice(bsel, "hi", oh * 512, (oh + 1) * 512)
                        out = ps[bank][:, 0:512]
                        mm(("ps", bank), out, olo, wh, [], start=False,
                           stop=(dc == 7), close=True, final=(dc == 7))
            for tc in range(2):
                kx = ("xacc", gb, tc)
                for oh in range(2):
                    bank = tc * 2 + oh
                    bk = ("ps", bank)
                    xap = xa[:, tc * D + oh * 512:tc * D + (oh + 1) * 512]
                    tt("V", xap, ps[bank][:, 0:512], xap, ALU.add,
                       reads=[bk, kx], writes=[kx])

        a0keys = [("a0", m) for m in range(32)]

        def phase_F0(gi):
            for m in range(32):
                bank = 2 + (m % 2)
                bk = ("ps", bank)
                out = ps[bank][:, 0:GRP]
                d0keys = [("d0", ic) for ic in range(8)]
                for ic in range(8):
                    lh = s_d0[:, ic * HID + m * 128:ic * HID + (m + 1) * 128]
                    rh = s_xn["hi"][:, ic * GRP:(ic + 1) * GRP]
                    rl = s_xn["lo"][:, ic * GRP:(ic + 1) * GRP]
                    rds = (d0keys + xn_keys) if ic == 0 else []
                    mm(bk, out, lh, rh, rds, start=(ic == 0), stop=False,
                       close=False, final=False)
                    mm(bk, out, lh, rl, [], start=False, stop=(ic == 7),
                       close=True, final=(ic == 7))
                act(s_a0[:, m * GRP:(m + 1) * GRP], out, AF.Relu, 0.0,
                    s_S0[:, 0:1], None, reads=[bk, ("S0",)],
                    writes=[("a0", m)])

        def g1_mms(m2):
            H4 = HID // 4
            def strip_dma(qf):
                b = (m2 * 4 + qf) % 3
                pr.dma("Q1", s_k1b[b][:],
                       w1pm_d[m2 * 128:(m2 + 1) * 128, qf * H4:(qf + 1) * H4],
                       reads=[("k1b", b), ("w1pmd", m2, qf // 2)],
                       writes=[("k1b", b)])
            for qf in range(3):
                strip_dma(qf)
            bank = m2 % 2
            bk = ("ps", bank)
            out = ps[bank][:, 0:GRP]
            for kc in range(32):
                qf, kk = divmod(kc, 8)
                b = (m2 * 4 + qf) % 3
                lh = s_k1b[b][:, kk * 128:(kk + 1) * 128]
                rh = s_a0[:, kc * GRP:(kc + 1) * GRP]
                rds = [("k1b", b)] + (a0keys if kc == 0 else [])
                mm(bk, out, lh, rh, rds, start=(kc == 0), stop=(kc == 31),
                   close=(kc % 8 == 7), final=(kc == 31))
                if kc == 7:
                    strip_dma(3)
            sl = m2 % 2
            act(s_a1[:, sl * GRP:(sl + 1) * GRP], out, AF.Relu, 0.0,
                s_S1[:, 0:1], None, reads=[bk, ("S1",)], writes=[("a1", sl)])
            w2b = s_w2b[m2 % 2]
            pr.dma("Q0", w2b[:], w2pm_d[m2 * 128:(m2 + 1) * 128, :],
                   reads=[("w2b", m2 % 2), ("w2pmd", m2)],
                   writes=[("w2b", m2 % 2)])

        def g2_mms(m2):
            sl = m2 % 2
            w2b = s_w2b[sl]
            g2rds = [("a1", sl), ("w2b", sl)]
            for tc in range(2):
                lh = s_a1[:, sl * GRP + tc * 128:sl * GRP + (tc + 1) * 128]
                for oh in range(2):
                    gbank = 2 + tc * 2 + oh
                    wslc = w2b[:, oh * 512:(oh + 1) * 512]
                    mm(("ps", gbank), ps[gbank][:, 0:512], lh, wslc, g2rds,
                       start=(m2 == 0), stop=(m2 == 31), close=True,
                       final=(m2 == 31))

        def phase_F12(gi, xnext):
            sq, g = groups[gi]
            gb = gi % 2
            tok0 = sq * SEQ + g * GRP
            xa = s_xacc[gb]
            g1_mms(0)
            if xnext is not None:
                phase_X_tr(xnext, 0)
                phase_X_ln(xnext, 1)
            g1_mms(1)
            g2_mms(0)
            if xnext is not None:
                phase_X_tr(xnext, 1)
            for m2 in range(2, 32):
                g1_mms(m2)
                g2_mms(m2 - 1)
            g2_mms(31)
            for tc in range(2):
                kx = ("xacc", gb, tc)
                for oh in range(2):
                    gbank = 2 + tc * 2 + oh
                    gbk = ("ps", gbank)
                    xap = xa[:, tc * D + oh * 512:tc * D + (oh + 1) * 512]
                    stt("V", xap, ps[gbank][:, 0:512], s_S2[:, 0:1], xap,
                        ALU.mult, ALU.add, reads=[gbk, ("S2",), kx],
                        writes=[kx])
                r0 = tok0 + tc * 128
                pr.dma("Q1", y_d[r0:r0 + 128, :], xa[:, tc * D:(tc + 1) * D],
                       reads=[kx], writes=[("ydone", gi, tc)])

        pre_consts()
        for cc in range(2):
            phase_X_ln(0, cc)
            phase_X_tr(0, cc)
        pre_wsplit("wq", xbuf=1)
        pre_wsplit("wk", xbuf=1)
        pre_wsplit("wv", xbuf=1)
        pre_wsplit("wo", xbuf=1)
        pre_d0()
        pre_w2()
        pre_w1(0, 32)
        for gi in range(len(groups)):
            nxt = gi + 1 if gi + 1 < len(groups) else None
            phase_P_qk(gi)
            phase_P_v(gi)
            phase_A(gi)
            if nxt is not None:
                phase_X_ln(nxt, 0)
            phase_Wo(gi)
            phase_F0(gi)
            phase_F12(gi, nxt)

        pr.finalize_queues()
        pr.render()
    return nc



_CACHE = {}


def _consts():
    i = np.arange(128)[:, None]
    j = np.arange(256)[None, :]
    band = (j >= i + 1) & (j <= i + 128)
    valid0 = band & (j >= 128)
    cmsk = np.where(band, 0.0, MASK_RAW).astype(ml_dtypes.bfloat16)
    cmsk0 = np.where(valid0, 0.0, MASK_RAW).astype(ml_dtypes.bfloat16)
    cid = np.eye(128, dtype=ml_dtypes.bfloat16)
    cones = np.ones((128, 1), dtype=ml_dtypes.bfloat16)
    conesr = np.ones((1, 128), dtype=ml_dtypes.bfloat16)
    cn12 = np.full((128, 1), EXP_SHIFT, dtype=np.float32)
    return cmsk, cmsk0, cid, cones, conesr, cn12


def make_inputs(L0_attn_x, L0_attn_Wq, L0_attn_Wk, L0_attn_Wv, L0_attn_Wo,
                L0_ffn_k0, L0_ffn_k1, L0_ffn_k2, L0_ffn_s0, L0_ffn_s1,
                L0_ffn_s2):
    x = np.ascontiguousarray(np.asarray(L0_attn_x, dtype=np.float32))
    wqt = np.ascontiguousarray(np.asarray(L0_attn_Wq, np.float32).T)
    wkt = np.ascontiguousarray(np.asarray(L0_attn_Wk, np.float32).T)
    wvt = np.ascontiguousarray(np.asarray(L0_attn_Wv, np.float32).T)
    wot = np.ascontiguousarray(np.asarray(L0_attn_Wo, np.float32).T)
    k0 = np.ascontiguousarray(np.asarray(L0_ffn_k0).astype(np.uint8))
    k1 = np.asarray(L0_ffn_k1).astype(np.uint8)
    k1s = np.ascontiguousarray(
        k1.reshape(32, 128, 32, 128).transpose(2, 1, 0, 3)).reshape(HID, HID)
    k2 = np.ascontiguousarray(np.asarray(L0_ffn_k2).astype(np.uint8))
    scal = np.zeros((128, 4), np.float32)
    scal[:, 0] = np.float32(L0_ffn_s0)
    scal[:, 1] = np.float32(L0_ffn_s1)
    scal[:, 2] = np.float32(L0_ffn_s2)
    cmsk, cmsk0, cid, cones, conesr, cn12 = _consts()
    shared = dict(wqt=wqt, wkt=wkt, wvt=wvt, wot=wot, k0=k0, k1s=k1s, k2=k2,
                  scal=scal, cmsk=cmsk, cmsk0=cmsk0, cid=cid, cones=cones,
                  conesr=conesr, cn12=cn12)
    in_maps = []
    for c in range(8):
        m = dict(shared)
        m["x"] = np.ascontiguousarray(x[2 * c:2 * c + 2].reshape(2 * SEQ, D))
        in_maps.append(m)
    return in_maps


def kernel(L0_attn_x, L0_attn_Wq, L0_attn_Wk, L0_attn_Wv, L0_attn_Wo,
           L0_ffn_k0, L0_ffn_k1, L0_ffn_k2, L0_ffn_s0, L0_ffn_s1, L0_ffn_s2):
    in_maps = make_inputs(L0_attn_x, L0_attn_Wq, L0_attn_Wk, L0_attn_Wv,
                          L0_attn_Wo, L0_ffn_k0, L0_ffn_k1, L0_ffn_k2,
                          L0_ffn_s0, L0_ffn_s1, L0_ffn_s2)
    if "nc" not in _CACHE:
        _CACHE["nc"] = build_program()
    nc = _CACHE["nc"]
    res = run_bass_kernel_spmd(nc, in_maps, core_ids=list(range(8)))
    out = np.empty((16, SEQ, D), np.float32)
    for c in range(8):
        out[2 * c:2 * c + 2] = np.asarray(res.results[c]["y"]).reshape(2, SEQ, D)
    return out


# version 16
# speedup vs baseline: 1.1438x; 1.1438x over previous
import numpy as np
import ml_dtypes
import concourse.bass as bass
import concourse.mybir as mybir
from concourse.bass_utils import run_bass_kernel_spmd

F32 = mybir.dt.float32
F32R = mybir.dt.float32r
BF16 = mybir.dt.bfloat16
U8 = mybir.dt.uint8
AF = mybir.ActivationFunctionType
ALU = mybir.AluOpType
AX = mybir.AxisListType.X

D = 1024
NH = 16
HD = 64
CH = 128
GRP = 256
HID = 4096
SEQ = 2048
NSEQ = 2
NGRP = SEQ // GRP
MASK_RAW = -262144.0
EXP_SHIFT = -12.0

ENG = ("T", "V", "S", "Q0", "Q1")
NPOOL = {"Q0": 12, "Q1": 8, "Q2": 8}
DIMS = (["T", "V", "S", "G"]
        + [(q, k) for q in ("Q0", "Q1", "Q2") for k in range(NPOOL[q])])


class Prog:

    def __init__(self, nc):
        self.nc = nc
        self.thunks = {e: [] for e in ENG}
        self.cnt = {e: 0 for e in ("T", "V", "S", "G")}
        self.dcnt = {q: 0 for q in ("Q0", "Q1", "Q2")}
        self.vc = {e: {d: 0 for d in DIMS} for e in ENG}
        self.snap = {d: [None] for d in DIMS}
        self.bufs = {}
        self.sems = {}
        self.pool_uses = {q: [0] * NPOOL[q] for q in ("Q0", "Q1", "Q2")}
        self.pend = {}

    def make_sems(self, estack):
        for e in ("T", "V", "S", "G"):
            self.sems[e] = estack.enter_context(self.nc.semaphore("sem" + e))
        for q in ("Q0", "Q1", "Q2"):
            for k in range(NPOOL[q]):
                self.sems[(q, k)] = estack.enter_context(
                    self.nc.semaphore("sem%s_%d" % (q, k)))

    def _need_for(self, reads, writes):
        need = {}
        for key in reads:
            b = self.bufs.get(key)
            if b and b[0] is not None:
                e, i = b[0]
                if i > need.get(e, 0):
                    need[e] = i
        for key in writes:
            b = self.bufs.get(key)
            if b:
                if b[0] is not None:
                    e, i = b[0]
                    if i > need.get(e, 0):
                        need[e] = i
                for e, i in b[1].items():
                    if i > need.get(e, 0):
                        need[e] = i
        return need

    def _emit_waits(self, eng, need):
        myvc = self.vc[eng]
        for d in sorted(need, key=str):
            i = need[d]
            if myvc[d] >= i:
                continue
            if isinstance(d, tuple):
                self.thunks[eng].append(("w", self.sems[d], 16 * i))
            else:
                self.thunks[eng].append(("w", self.sems[d], i))
            sn = self.snap[d][i]
            for dd, v in sn.items():
                if v > myvc[dd]:
                    myvc[dd] = v
            if i > myvc[d]:
                myvc[d] = i

    def _marker(self, eng):
        self.cnt[eng] += 1
        i = self.cnt[eng]
        sn = dict(self.vc[eng])
        sn[eng] = i
        self.snap[eng].append(sn)
        return (eng, i)

    def _register(self, reads, writes, marker):
        for key in writes:
            self.bufs[key] = [marker, {}]
        e, i = marker
        for key in reads:
            b = self.bufs.get(key)
            if b is None:
                b = [None, {}]
                self.bufs[key] = b
            if i > b[1].get(e, 0):
                b[1][e] = i

    def op(self, eng, emit, reads=(), writes=()):
        stream = "Q1" if eng == "G" else eng
        self._emit_waits(stream, self._need_for(reads, writes))
        self.cnt[eng] += 1
        i = self.cnt[eng]
        sn = dict(self.vc[stream])
        sn[eng] = i
        self.snap[eng].append(sn)
        self.thunks[stream].append(("i", emit, self.sems[eng], 1))
        self._register(reads, writes, (eng, i))

    def gop(self, eng, emit, reads=(), gkey=None, first=False, close=False,
            final=False):
        writes = (gkey,) if first else ()
        self._emit_waits(eng, self._need_for(reads, writes))
        pend = self.pend.setdefault((eng, gkey), set())
        pend.update(reads)
        if close:
            m = self._marker(eng)
            self.thunks[eng].append(("i", emit, self.sems[eng], 1))
            self._register(tuple(pend), (gkey,) if final else (), m)
            pend.clear()
            if final:
                self.pend.pop((eng, gkey), None)
        else:
            self.thunks[eng].append(("x", emit))

    def dma(self, q, out_ap, in_ap, reads=(), writes=()):
        stream = "S" if q == "Q2" else q
        R = NPOOL[q]
        idx = self.dcnt[q]
        self.dcnt[q] = idx + 1
        k = idx % R
        d = (q, k)
        prev_uses = self.pool_uses[q][k]
        if prev_uses > 0 and self.vc[stream][d] < prev_uses:
            self.thunks[stream].append(("w", self.sems[d], 16 * prev_uses))
            sn = self.snap[d][prev_uses]
            myvc = self.vc[stream]
            for dd, v in sn.items():
                if v > myvc[dd]:
                    myvc[dd] = v
        self._emit_waits(stream, self._need_for(reads, writes))
        self.thunks[stream].append(("d", out_ap, in_ap, self.sems[d]))
        u = prev_uses + 1
        self.pool_uses[q][k] = u
        sn = dict(self.vc[stream])
        sn[d] = u
        self.snap[d].append(sn)
        self._register(reads, writes, (d, u))

    def finalize_queues(self):
        for q in ("Q0", "Q1", "Q2"):
            stream = "S" if q == "Q2" else q
            for k in range(NPOOL[q]):
                u = self.pool_uses[q][k]
                if u > 0:
                    self.thunks[stream].append(("w", self.sems[(q, k)], 16 * u))

    def render(self):
        nc = self.nc

        def runner(e):
            lst = self.thunks[e]
            def run(engine_obj):
                for t in lst:
                    tag = t[0]
                    if tag == "w":
                        engine_obj.wait_ge(t[1], t[2])
                    elif tag == "i":
                        t[1](engine_obj).then_inc(t[2], t[3])
                    elif tag == "x":
                        t[1](engine_obj)
                    else:
                        engine_obj.dma_start(out=t[1], in_=t[2]).then_inc(t[3], 16)
            return run

        with nc.Block() as block:
            block.tensor(runner("T"))
            block.vector(runner("V"))
            block.scalar(runner("S"))
            block.sync(runner("Q0"))
            block.gpsimd(runner("Q1"))



def build_program(n_seq=NSEQ, n_grp=NGRP):
    import contextlib
    nc = bass.Bass("TRN2", target_bir_lowering=False, debug=False)

    x_d = nc.dram_tensor("x", [NSEQ * SEQ, D], F32, kind="ExternalInput").ap()
    y_d = nc.dram_tensor("y", [NSEQ * SEQ, D], F32, kind="ExternalOutput").ap()
    wt_d = {}
    for nm, t in (("wq", "wqt"), ("wk", "wkt"), ("wv", "wvt"), ("wo", "wot")):
        wt_d[nm] = nc.dram_tensor(t, [D, D], F32R, kind="ExternalInput").ap()
    k0s_d = nc.dram_tensor("k0s", [HID, 2 * D], U8, kind="ExternalInput").ap()
    k1s_d = nc.dram_tensor("k1s", [HID, HID], U8, kind="ExternalInput").ap()
    k2_d = nc.dram_tensor("k2", [HID, D], U8, kind="ExternalInput").ap()
    scal_d = nc.dram_tensor("scal", [128, 4], F32, kind="ExternalInput").ap()
    cmsk_d = nc.dram_tensor("cmsk", [128, 256], F32R, kind="ExternalInput").ap()
    cmsk0_d = nc.dram_tensor("cmsk0", [128, 256], F32R,
                             kind="ExternalInput").ap()
    cid_d = nc.dram_tensor("cid", [128, 128], BF16, kind="ExternalInput").ap()
    cidr_d = nc.dram_tensor("cidr", [128, 128], F32R, kind="ExternalInput").ap()
    cn12_d = nc.dram_tensor("cn12", [128, 1], F32, kind="ExternalInput").ap()
    czero_d = nc.dram_tensor("czero", [128, 8 * 384], F32R,
                             kind="ExternalInput").ap()
    d0s_d = nc.dram_tensor("d0s", [HID, D], F32R, kind="Internal").ap()
    w1pm_d = nc.dram_tensor("w1pm", [HID, HID], F32R, kind="Internal").ap()
    w2pm_d = nc.dram_tensor("w2pm", [HID, D], F32R, kind="Internal").ap()

    estack = contextlib.ExitStack()
    with estack:
        A = nc.alloc_sbuf_tensor
        s_cid = A("s_cid", [128, 128], BF16)
        s_cidr = A("s_cidr", [128, 128], F32R)
        s_msk = A("s_msk", [128, 256], F32R)
        s_msk0 = A("s_msk0", [128, 256], F32R)
        s_scal = A("s_scal", [128, 4], F32)
        s_n12 = A("s_n12", [128, 1], F32)
        s_S0 = A("s_S0", [128, 1], F32)
        s_S1 = A("s_S1", [128, 1], F32)
        s_S2 = A("s_S2", [128, 1], F32)
        s_ln = A("s_ln", [128, 16], F32)
        s_sums = A("s_sums", [128, NH], F32)
        s_rcp = A("s_rcp", [128, NH], F32)
        s_pre = A("s_pre", [128, 1024], BF16)
        s_xc = A("s_xc", [128, 6 * D], F32)
        s_xnf = A("s_xnf", [128, D], F32)
        s_xnt = {p: A("s_xnt_" + p, [128, D], BF16) for p in ("hi", "lo")}
        s_xn = A("s_xn", [128, 8 * 512], F32R)
        s_q = A("s_q", [128, 8 * GRP], F32R)
        s_k = A("s_k", [128, 8 * 384], F32R)
        s_v = {p: A("s_v_" + p, [128, 3 * D], BF16) for p in ("hi", "lo")}
        s_O = A("s_O", [128, 8 * GRP], F32R)
        s_P0 = A("s_P0", [128, 2 * 256], F32)
        s_Ph = A("s_Ph", [128, 2 * 256], BF16)
        s_Pl = A("s_Pl", [128, 2 * 256], BF16)
        s_PT = A("s_PT", [128, 2 * 512], BF16)
        s_wbuf = [A("s_wbuf%d" % b, [128, 512], F32R) for b in range(4)]
        s_d0b = [A("s_d0b%d" % b, [128, 512], F32R) for b in range(3)]
        s_k1b = [A("s_k1b%d" % b, [128, 512], F32R) for b in range(3)]
        s_w2b = [A("s_w2b%d" % b, [128, D], F32R) for b in range(4)]
        s_a0 = A("s_a0", [128, 32 * 512], F32R)
        s_a1 = A("s_a1", [128, 4 * 512], F32R)

        ps = [nc.alloc_psum_tensor("ps%d" % i, [128, 512], F32) for i in range(6)]
        psb = [nc.alloc_psum_tensor("pb%d" % i, [128, 1024], BF16) for i in range(2)]

        pr = Prog(nc)
        pr.make_sems(estack)

        def ts(eng, out_ap, in_ap, s1, s2, op0, op1, reads, writes):
            def emit(e, o=out_ap, i=in_ap, s1=s1, s2=s2, op0=op0, op1=op1):
                return e.tensor_scalar(out=o, in0=i, scalar1=s1, scalar2=s2,
                                       op0=op0, op1=op1)
            pr.op(eng, emit, reads=reads, writes=writes)

        def stt(eng, out_ap, in0_ap, sc, in1_ap, op0, op1, reads, writes):
            def emit(e, o=out_ap, i0=in0_ap, sc=sc, i1=in1_ap, op0=op0, op1=op1):
                return e.scalar_tensor_tensor(out=o, in0=i0, scalar=sc, in1=i1,
                                              op0=op0, op1=op1)
            pr.op(eng, emit, reads=reads, writes=writes)

        def tt(eng, out_ap, a_ap, b_ap, alu, reads, writes):
            def emit(e, o=out_ap, a=a_ap, b=b_ap, alu=alu):
                return e.tensor_tensor(out=o, in0=a, in1=b, op=alu)
            pr.op(eng, emit, reads=reads, writes=writes)

        def cp(eng, out_ap, in_ap, reads, writes):
            if eng == "S":
                def emit(e, o=out_ap, i=in_ap):
                    return e.copy(out=o, in_=i)
            else:
                def emit(e, o=out_ap, i=in_ap):
                    return e.tensor_copy(out=o, in_=i)
            pr.op(eng, emit, reads=reads, writes=writes)

        def memset(eng, ap, val, reads, writes):
            def emit(e, a=ap, v=val):
                return e.memset(a, v)
            pr.op(eng, emit, reads=reads, writes=writes)

        def act(out_ap, in_ap, func, bias, scale, accum, reads, writes):
            def emit(e, o=out_ap, i=in_ap, f=func, b=bias, s=scale, a=accum):
                return e.activation(out=o, in_=i, func=f, bias=b, scale=s,
                                    accum_out=a)
            pr.op("S", emit, reads=reads, writes=writes)

        def mm(gkey, out_ap, lhsT, rhs, reads, start, stop, close, final, tp=None):
            def emit(e, o=out_ap, l=lhsT, r=rhs, st=start, sp=stop, tp=tp):
                return e.matmul(o, l, r, start=st, stop=sp, tile_position=tp)
            pr.gop("T", emit, reads=reads, gkey=gkey, first=start, close=close,
                   final=final)

        def tr(gkey, out_ap, in_ap, reads, first, close, final):
            def emit(e, o=out_ap, i=in_ap):
                return e.transpose(o, i, s_cid[:])
            pr.gop("T", emit, reads=tuple(reads) + (("cid",),), gkey=gkey,
                   first=first, close=close, final=final)

        groups = [(sq, g) for sq in range(n_seq) for g in range(n_grp)]
        assert len(groups) % 2 == 0
        npairs = len(groups) // 2

        def xnk(pos):
            return [("xn", pos, 0), ("xn", pos, 1)]
        xn_all = xnk(0) + xnk(1)
        a0keys = [("a0", m) for m in range(32)]
        r1024 = 1.0 / 1024.0

        pcnt = [0]

        def pre_consts():
            pr.dma("Q0", s_cid[:], cid_d, writes=[("cid",)])
            pr.dma("Q0", s_cidr[:], cidr_d, writes=[("cidr",)])
            pr.dma("Q0", s_msk[:], cmsk_d, writes=[("msk",)])
            pr.dma("Q0", s_msk0[:], cmsk0_d, writes=[("msk0",)])
            pr.dma("Q0", s_scal[:], scal_d, writes=[("scal",)])
            pr.dma("Q0", s_n12[:], cn12_d, writes=[("n12",)])
            ts("V", s_S0[:], s_scal[:, 0:1], 4.0, None, ALU.mult, ALU.bypass,
               reads=[("scal",)], writes=[("S0",)])
            ts("V", s_S1[:], s_scal[:, 1:2], 2.0, None, ALU.mult, ALU.bypass,
               reads=[("scal",)], writes=[("S1",)])
            ts("V", s_S2[:], s_scal[:, 2:3], 1.0, None, ALU.mult, ALU.bypass,
               reads=[("scal",)], writes=[("S2",)])
            pr.dma("Q0", s_k[:], czero_d, reads=[],
                   writes=[("kcar",)] + [("k", b) for b in range(4)])
            for p in ("hi", "lo"):
                memset("V", s_v[p][:, 2 * D:3 * D], 0.0, reads=[],
                       writes=[("v", p, 2, 0), ("v", p, 2, 1)])

        def pre_d0():
            for m in range(32):
                for qh in range(2):
                    kt = ("pre", 0)
                    kb = ("pre", 1)
                    bt = s_pre[:, 0:512]
                    bb = s_pre[:, 512:1024]
                    r0 = m * 128
                    pr.dma("Q1", bt, k0s_d[r0:r0 + 128, qh * 512:qh * 512 + 512],
                           reads=[kt], writes=[kt])
                    pr.dma("Q1", bb, k0s_d[r0:r0 + 128,
                                           1024 + qh * 512:1024 + qh * 512 + 512],
                           reads=[kb], writes=[kb])
                    tt("G", bt, bt, bb, ALU.subtract, reads=[kt, kb],
                       writes=[kt])
                    pr.dma("Q1", d0s_d[r0:r0 + 128, qh * 512:(qh + 1) * 512], bt,
                           reads=[kt], writes=[("d0sd", m)])

        def pre_w2():
            for oc in range(32):
                kk = ("pre", 0)
                buf = s_pre[:, 0:1024]
                r0 = oc * 128
                pr.dma("Q1", buf, k2_d[r0:r0 + 128, :], reads=[kk, ("pre", 1)],
                       writes=[kk, ("pre", 1)])
                ts("G", buf, buf, 2.0, -1.0, ALU.mult, ALU.add,
                   reads=[kk, ("pre", 1)], writes=[kk, ("pre", 1)])
                pr.dma("Q1", w2pm_d[r0:r0 + 128, :], buf,
                       reads=[kk, ("pre", 1)], writes=[("w2pmd", oc)])

        def pre_w1():
            for m2 in range(32):
                for qf in range(4):
                    kk = ("pre", 0)
                    buf = s_pre[:, 0:1024]
                    r0 = m2 * 128
                    pr.dma("Q1", buf,
                           k1s_d[r0:r0 + 128, qf * 1024:(qf + 1) * 1024],
                           reads=[kk, ("pre", 1)], writes=[kk, ("pre", 1)])
                    ts("G", buf, buf, 2.0, -1.0, ALU.mult, ALU.add,
                       reads=[kk, ("pre", 1)], writes=[kk, ("pre", 1)])
                    pr.dma("Q1", w1pm_d[r0:r0 + 128, qf * 1024:(qf + 1) * 1024],
                           buf, reads=[kk, ("pre", 1)],
                           writes=[("w1pmd", m2, qf // 2)])

        def phase_X_ln(gi, cc):
            sq, g = groups[gi]
            gch = 2 * gi + cc
            slot = gch % 6
            xt = s_xc[:, slot * D:(slot + 1) * D]
            kx = ("xc", slot)
            r0 = sq * SEQ + g * GRP + cc * 128
            L = lambda i, cc=cc: ("ln", cc, i)
            lc = s_ln[:, cc * 8:(cc + 1) * 8]
            pr.dma("Q0", xt, x_d[r0:r0 + 128, :], reads=[kx], writes=[kx])
            def em_rs(e, o=lc[:, 0:1], i=xt):
                return e.reduce_sum(o, i, AX)
            pr.op("V", em_rs, reads=[kx], writes=[L(0)])
            ts("V", lc[:, 1:2], lc[:, 0:1], r1024, None, ALU.mult,
               ALU.bypass, reads=[L(0)], writes=[L(1)])
            act(s_xnt["hi"][:, 0:D], xt, AF.Square, 0.0, 1.0, lc[:, 2:3],
                reads=[kx], writes=[("xnt", "hi"), L(2)])
            ts("V", lc[:, 3:4], lc[:, 2:3], r1024, None, ALU.mult,
               ALU.bypass, reads=[L(2)], writes=[L(3)])
            tt("V", lc[:, 4:5], lc[:, 1:2], lc[:, 1:2], ALU.mult,
               reads=[L(1)], writes=[L(4)])
            tt("V", lc[:, 5:6], lc[:, 3:4], lc[:, 4:5], ALU.subtract,
               reads=[L(3), L(4)], writes=[L(5)])
            ts("V", lc[:, 5:6], lc[:, 5:6], 1.0, 1e-5, ALU.mult, ALU.add,
               reads=[L(5)], writes=[L(5)])
            def em_sq(e, o=lc[:, 6:7], i=lc[:, 5:6]):
                return e.sqrt(o, i)
            pr.op("S", em_sq, reads=[L(5)], writes=[L(6)])
            def em_rc(e, o=lc[:, 7:8], i=lc[:, 6:7]):
                return e.reciprocal(o, i)
            pr.op("V", em_rc, reads=[L(6)], writes=[L(7)])
            xnf = s_xnf[:, 0:D]
            ts("V", xnf, xt, lc[:, 1:2], None, ALU.subtract, ALU.bypass,
               reads=[kx, L(1)], writes=[("xnf",)])
            ts("V", s_xnt["hi"][:, 0:D], xnf, lc[:, 7:8], None, ALU.mult,
               ALU.bypass, reads=[("xnf",), L(7)], writes=[("xnt", "hi")])
            stt("V", s_xnt["lo"][:, 0:D], xnf, lc[:, 7:8],
                s_xnt["hi"][:, 0:D], ALU.mult, ALU.subtract,
                reads=[("xnf",), L(7), ("xnt", "hi")], writes=[("xnt", "lo")])

        def phase_X_tr(gi, cc):
            pos = gi % 2
            for ip, p in enumerate(("hi", "lo")):
                bk = ("pb", ip)
                for ic in range(8):
                    src_ = s_xnt[p][:, ic * 128:(ic + 1) * 128]
                    dst = psb[ip][:, ic * 128:(ic + 1) * 128]
                    tr(bk, dst, src_, [("xnt", p)],
                       first=(ic == 0), close=(ic == 7), final=(ic == 7))
            cp("V", s_pre[:], psb[1][:, 0:1024],
               reads=[("pb", 1)], writes=[("pre", 0), ("pre", 1)])
            dsts = bass.AP(s_xn, pos * 256 + cc * 128,
                           [[8 * 512, 128], [512, 8], [1, 128]])
            src0 = bass.AP(psb[0], 0, [[1024, 128], [128, 8], [1, 128]])
            src1 = bass.AP(s_pre, 0, [[1024, 128], [128, 8], [1, 128]])
            tt("V", dsts, src0, src1, ALU.add,
               reads=[("pb", 0), ("pb", 1), ("pre", 0), ("pre", 1)],
               writes=[("xn", pos, cc)])

        def xblock(gi, cc):
            phase_X_ln(gi, cc)
            phase_X_tr(gi, cc)

        def panel_load(nm, ic, hf):
            b = pcnt[0] % 4
            pcnt[0] += 1
            pr.dma("Q0", s_wbuf[b][:],
                   wt_d[nm][ic * 128:(ic + 1) * 128, hf * 512:(hf + 1) * 512],
                   reads=[("wbuf", b)], writes=[("wbuf", b)])
            return b

        def phase_P_qk(gi):
            pos = gi % 2
            for nm, okey in (("wq", "q"), ("wk", "k")):
                for ic in range(8):
                    for hf in range(2):
                        b = panel_load(nm, ic, hf)
                        for mi in range(4):
                            m = hf * 4 + mi
                            bank = m // 2
                            bk = ("ps", bank)
                            lh = s_wbuf[b][:, mi * 128:(mi + 1) * 128]
                            rh = s_xn[:, ic * 512 + pos * 256:
                                      ic * 512 + pos * 256 + 256]
                            out = ps[bank][:, (m % 2) * 256:(m % 2 + 1) * 256]
                            rds = ([("wbuf", b)] + xnk(pos)) if mi % 2 == 0 else []
                            mm(bk, out, lh, rh, rds,
                               start=(ic == 0 and m % 2 == 0),
                               stop=(ic == 7 and m % 2 == 1),
                               close=(m % 2 == 1),
                               final=(ic == 7 and m % 2 == 1))
                for bank in range(4):
                    bk = ("ps", bank)
                    if okey == "q":
                        dst = s_q[:, bank * 512:(bank + 1) * 512]
                        src = ps[bank][:, 0:512]
                    else:
                        dst = bass.AP(s_k, 2 * bank * 384 + 128,
                                      [[8 * 384, 128], [384, 2], [1, 256]])
                        src = bass.AP(ps[bank], 0,
                                      [[512, 128], [256, 2], [1, 256]])
                    cp("S", dst, src, reads=[bk], writes=[(okey, bank)])

        def phase_P_v(gi):
            pos = gi % 2
            vch0 = 2 * gi
            for ic in range(8):
                for hf in range(2):
                    b = panel_load("wv", ic, hf)
                    for tc in range(2):
                        bank = tc * 2 + hf
                        lh = s_xn[:, ic * 512 + pos * 256 + tc * 128:
                                  ic * 512 + pos * 256 + (tc + 1) * 128]
                        out = ps[bank][:, 0:512]
                        mm(("ps", bank), out, lh, s_wbuf[b][:],
                           [("wbuf", b)] + xnk(pos),
                           start=(ic == 0), stop=(ic == 7), close=True,
                           final=(ic == 7))
            for tc in range(2):
                sl = (vch0 + tc) % 3
                for hf in range(2):
                    bank = tc * 2 + hf
                    bk = ("ps", bank)
                    hi = s_v["hi"][:, sl * D + hf * 512:sl * D + (hf + 1) * 512]
                    lo = s_v["lo"][:, sl * D + hf * 512:sl * D + (hf + 1) * 512]
                    cp("S", hi, ps[bank][:, 0:512], reads=[bk],
                       writes=[("v", "hi", sl, hf)])
                    stt("V", lo, ps[bank][:, 0:512], 1.0, hi, ALU.mult,
                        ALU.subtract, reads=[bk, ("v", "hi", sl, hf)],
                        writes=[("v", "lo", sl, hf)])

        def phase_A(gi):
            sq, g = groups[gi]
            for cc in range(2):
                vch = 2 * gi + cc
                seq_start = (g == 0 and cc == 0)
                msk_s = s_msk0 if seq_start else s_msk
                mkey = ("msk0",) if seq_start else ("msk",)

                def vap(p, h, blk, vch=vch):
                    sl = (vch - 1 + blk) % 3
                    c0 = h * HD
                    return (s_v[p][:, sl * D + c0:sl * D + c0 + HD],
                            ("v", p, sl, c0 // 512))

                def s_wave(w, cc=cc, msk_s=msk_s, mkey=mkey):
                    sbank = 2 + (w % 4)
                    sbk = ("ps", sbank)
                    for hh in range(2):
                        h = 2 * w + hh
                        pb = 64 * (h % 2)
                        m = h // 2
                        tp = (pb, 0) if pb else None
                        qh = s_q[pb:pb + 64, m * 256 + cc * 128:
                                 m * 256 + (cc + 1) * 128]
                        kh = s_k[pb:pb + 64, m * 384 + cc * 128:
                                 m * 384 + cc * 128 + 256]
                        omsk = ps[sbank][:, hh * 256:hh * 256 + 256]
                        rds = [("q", m // 2), ("k", m // 2)]
                        if cc == 0:
                            rds.append(("kcar",))
                        mm(sbk, omsk, qh, kh, rds, start=(hh == 0),
                           stop=False, close=False, final=False, tp=tp)
                        mm(sbk, omsk, s_cidr[:], msk_s[:],
                           [mkey, ("cidr",)], start=False, stop=(hh == 1),
                           close=True, final=(hh == 1))

                def head_work(w, cc=cc, vap=vap):
                    sbank = 2 + (w % 4)
                    sbk = ("ps", sbank)
                    for hh in range(2):
                        h = 2 * w + hh
                        slot = h % 2
                        act(s_P0[:, slot * 256:(slot + 1) * 256],
                            ps[sbank][:, hh * 256:hh * 256 + 256],
                            AF.Exp, s_n12[:, 0:1], 0.125,
                            s_sums[:, h:h + 1],
                            reads=[sbk, ("n12",)],
                            writes=[("P0", slot), ("sums", h)])
                    def em_rcp(e, o=s_rcp[:, 2 * w:2 * w + 2],
                               i=s_sums[:, 2 * w:2 * w + 2]):
                        return e.reciprocal(o, i)
                    pr.op("V", em_rcp,
                          reads=[("sums", 2 * w), ("sums", 2 * w + 1)],
                          writes=[("rcp", w)])
                    for hh in range(2):
                        h = 2 * w + hh
                        slot = h % 2
                        p0 = s_P0[:, slot * 256:(slot + 1) * 256]
                        ph = s_Ph[:, slot * 256:(slot + 1) * 256]
                        pl = s_Pl[:, slot * 256:(slot + 1) * 256]
                        rc = s_rcp[:, h:h + 1]
                        ts("V", ph, p0, rc, None, ALU.mult, ALU.bypass,
                           reads=[("P0", slot), ("rcp", w)],
                           writes=[("Ph", slot)])
                        stt("V", pl, p0, rc, ph, ALU.mult, ALU.subtract,
                            reads=[("P0", slot), ("rcp", w), ("Ph", slot)],
                            writes=[("Pl", slot)])
                    for hh in range(2):
                        h = 2 * w + hh
                        slot = h % 2
                        ptk = ("pb", slot)
                        for idx in range(4):
                            ti, blk = divmod(idx, 2)
                            sap = (s_Ph if ti == 0 else s_Pl)[
                                :, slot * 256 + blk * 128:
                                slot * 256 + (blk + 1) * 128]
                            dst = psb[slot][:, idx * 128:(idx + 1) * 128]
                            tr(ptk, dst, sap,
                               [("Ph", slot), ("Pl", slot)],
                               first=(idx == 0), close=(idx == 3),
                               final=(idx == 3))
                    for hh in range(2):
                        h = 2 * w + hh
                        slot = h % 2
                        ptk = ("pb", slot)
                        dpt = s_PT[:, slot * 512:(slot + 1) * 512]
                        cp("V", dpt, psb[slot][:, 0:512], reads=[ptk],
                           writes=[("PT", slot)])
                    for hh in range(2):
                        h = 2 * w + hh
                        slot = h % 2
                        obank = w // 4
                        obk = ("ps", obank)
                        pb = 64 * (h % 2)
                        qq = w % 4
                        oout = ps[obank][pb:pb + 64, qq * 128:(qq + 1) * 128]
                        otp = (0, pb) if pb else None
                        seqm = []
                        for blk in range(2):
                            vh, vkh = vap("hi", h, blk)
                            vl, vkl = vap("lo", h, blk)
                            pthi = s_PT[:, slot * 512 + blk * 128:
                                        slot * 512 + (blk + 1) * 128]
                            ptlo = s_PT[:, slot * 512 + 256 + blk * 128:
                                        slot * 512 + 256 + (blk + 1) * 128]
                            seqm.append((vh, pthi, [vkh, ("PT", slot)]))
                            seqm.append((vh, ptlo, []))
                            seqm.append((vl, pthi, [vkl]))
                        for si, (lh, rr, rds) in enumerate(seqm):
                            mm(obk, oout, lh, rr, rds,
                               start=(si == 0), stop=(si == 5),
                               close=(si == 5),
                               final=(w % 4 == 3 and hh == 1 and si == 5),
                               tp=otp)

                for w0 in range(3):
                    s_wave(w0)
                for w in range(NH // 2):
                    if w + 3 < NH // 2:
                        s_wave(w + 3)
                    head_work(w)
                for ob in range(2):
                    obk = ("ps", ob)
                    dsts = bass.AP(s_O, ob * 4 * GRP + cc * 128,
                                   [[8 * GRP, 128], [GRP, 4], [1, 128]])
                    srcs = bass.AP(ps[ob], 0, [[512, 128], [128, 4], [1, 128]])
                    cp("S", dsts, srcs, reads=[obk], writes=[("O", ob, cc)])
            dstc = bass.AP(s_k, 0, [[8 * 384, 128], [384, 8], [1, 128]])
            srcc = bass.AP(s_k, 256, [[8 * 384, 128], [384, 8], [1, 128]])
            pr.dma("Q1", dstc, srcc,
                   reads=[("k", b) for b in range(4)] + [("kcar",)],
                   writes=[("kcar",)])

        def phase_Wo(gi):
            gi2 = 2 * gi
            for ic in range(8):
                for hf in range(2):
                    b = panel_load("wo", ic, hf)
                    for tc in range(2):
                        bank = tc * 2 + hf
                        lh = s_O[:, ic * 256 + tc * 128:ic * 256 + (tc + 1) * 128]
                        mm(("ps", bank), ps[bank][:, 0:512], lh, s_wbuf[b][:],
                           [("wbuf", b), ("O", ic // 4, tc)],
                           start=(ic == 0), stop=(ic == 7), close=True,
                           final=(ic == 7))
            for tc in range(2):
                slot = (gi2 + tc) % 6
                kx = ("xc", slot)
                for hf in range(2):
                    bank = tc * 2 + hf
                    bk = ("ps", bank)
                    xap = s_xc[:, slot * D + hf * 512:slot * D + (hf + 1) * 512]
                    tt("V", xap, ps[bank][:, 0:512], xap, ALU.add,
                       reads=[bk, kx], writes=[kx])

        def phase_F0(pj):
            def strip_dma(idx):
                if idx >= 64:
                    return
                m, h = divmod(idx, 2)
                b = idx % 3
                pr.dma("Q0", s_d0b[b][:],
                       d0s_d[m * 128:(m + 1) * 128, h * 512:(h + 1) * 512],
                       reads=[("d0b", b), ("d0sd", m)], writes=[("d0b", b)])
            for idx in range(3):
                strip_dma(idx)
            for m in range(32):
                bank = 4 + (m % 2)
                bk = ("ps", bank)
                out = ps[bank][:, 0:512]
                for ic in range(8):
                    h = ic // 4
                    b = (m * 2 + h) % 3
                    lh = s_d0b[b][:, (ic % 4) * 128:(ic % 4 + 1) * 128]
                    rh = s_xn[:, ic * 512:(ic + 1) * 512]
                    rds = [("d0b", b)] + (xn_all if ic == 0 else [])
                    mm(bk, out, lh, rh, rds, start=(ic == 0), stop=(ic == 7),
                       close=(ic % 4 == 3), final=(ic == 7))
                    if ic % 4 == 3:
                        strip_dma(m * 2 + h + 3)
                act(s_a0[:, m * 512:(m + 1) * 512], out, AF.Relu, 0.0,
                    s_S0[:, 0:1], None, reads=[bk, ("S0",)],
                    writes=[("a0", m)])

        def g1_mms(m2):
            def strip_dma(idx):
                if idx >= 256:
                    return
                m2_, e_ = divmod(idx, 8)
                b = idx % 3
                pr.dma("Q1", s_k1b[b][:],
                       w1pm_d[m2_ * 128:(m2_ + 1) * 128,
                              e_ * 512:(e_ + 1) * 512],
                       reads=[("k1b", b), ("w1pmd", m2_, e_ // 4)],
                       writes=[("k1b", b)])
            if m2 == 0:
                for idx in range(3):
                    strip_dma(idx)
            sl = m2 % 4
            pr.dma("Q0", s_w2b[sl][:], w2pm_d[m2 * 128:(m2 + 1) * 128, :],
                   reads=[("w2b", sl), ("w2pmd", m2)], writes=[("w2b", sl)])
            bank = m2 % 2
            bk = ("ps", bank)
            out = ps[bank][:, 0:512]
            for kc in range(32):
                e = kc // 4
                b = (m2 * 8 + e) % 3
                lh = s_k1b[b][:, (kc % 4) * 128:(kc % 4 + 1) * 128]
                rh = s_a0[:, kc * 512:(kc + 1) * 512]
                rds = [("k1b", b)] + (a0keys if kc == 0 else [])
                mm(bk, out, lh, rh, rds, start=(kc == 0), stop=(kc == 31),
                   close=(kc % 4 == 3), final=(kc == 31))
                if kc % 4 == 3:
                    strip_dma(m2 * 8 + e + 3)
            act(s_a1[:, sl * 512:(sl + 1) * 512], out, AF.Relu, 0.0,
                s_S1[:, 0:1], None, reads=[bk, ("S1",)], writes=[("a1", sl)])

        def g2_batch(pj, bt):
            batch = range(4 * bt, 4 * bt + 4)
            akeys = [("a1", m2 % 4) for m2 in batch]
            wkeys = [("w2b", m2 % 4) for m2 in batch]
            for ti in range(8):
                tc, oh = divmod(ti, 2)
                bank = 2 + (ti % 4)
                gk = ("ps", bank)
                for j, m2 in enumerate(batch):
                    sl = m2 % 4
                    mm(gk, ps[bank][:, 0:512],
                       s_a1[:, sl * 512 + tc * 128:sl * 512 + (tc + 1) * 128],
                       s_w2b[sl][:, oh * 512:(oh + 1) * 512],
                       (akeys + wkeys) if j == 0 else [],
                       start=(j == 0), stop=(j == 3), close=(j == 3),
                       final=True)
                slot = (4 * pj + tc) % 6
                xap = s_xc[:, slot * D + oh * 512:slot * D + (oh + 1) * 512]
                stt("V", xap, ps[bank][:, 0:512], s_S2[:, 0:1], xap,
                    ALU.mult, ALU.add, reads=[gk, ("S2",), ("xc", slot)],
                    writes=[("xc", slot)])

        def phase_F12(pj):
            gi0 = 2 * pj
            nxtA = gi0 + 2 if gi0 + 2 < len(groups) else None
            for bt in range(8):
                for m2 in range(4 * bt, 4 * bt + 4):
                    g1_mms(m2)
                g2_batch(pj, bt)
                if nxtA is not None:
                    if bt == 1:
                        phase_X_ln(nxtA, 0)
                    elif bt == 2:
                        phase_X_tr(nxtA, 0)
                    elif bt == 3:
                        phase_X_ln(nxtA, 1)
                    elif bt == 4:
                        phase_X_tr(nxtA, 1)
            for pos in range(2):
                gi = gi0 + pos
                sq, g = groups[gi]
                for cc in range(2):
                    slot = (2 * gi + cc) % 6
                    r0 = sq * SEQ + g * GRP + cc * 128
                    pr.dma("Q1", y_d[r0:r0 + 128, :],
                           s_xc[:, slot * D:(slot + 1) * D],
                           reads=[("xc", slot)], writes=[("ydone", gi, cc)])

        pre_consts()
        for cc in range(2):
            xblock(0, cc)
        pre_d0()
        pre_w2()
        pre_w1()
        for pj in range(npairs):
            gia, gib = 2 * pj, 2 * pj + 1
            phase_P_qk(gia)
            phase_P_v(gia)
            phase_A(gia)
            phase_X_ln(gib, 0)
            phase_X_tr(gib, 0)
            phase_Wo(gia)
            phase_X_ln(gib, 1)
            phase_X_tr(gib, 1)
            phase_P_qk(gib)
            phase_P_v(gib)
            phase_A(gib)
            phase_Wo(gib)
            phase_F0(pj)
            phase_F12(pj)

        pr.finalize_queues()
        pr.render()
    return nc




_CACHE = {}


def _consts():
    i = np.arange(128)[:, None]
    j = np.arange(256)[None, :]
    band = (j >= i + 1) & (j <= i + 128)
    valid0 = band & (j >= 128)
    cmsk = np.where(band, 0.0, MASK_RAW).astype(np.float32)
    cmsk0 = np.where(valid0, 0.0, MASK_RAW).astype(np.float32)
    cid = np.eye(128, dtype=ml_dtypes.bfloat16)
    cidr = np.eye(128, dtype=np.float32)
    cn12 = np.full((128, 1), EXP_SHIFT, dtype=np.float32)
    return cmsk, cmsk0, cid, cidr, cn12


def make_inputs(L0_attn_x, L0_attn_Wq, L0_attn_Wk, L0_attn_Wv, L0_attn_Wo,
                L0_ffn_k0, L0_ffn_k1, L0_ffn_k2, L0_ffn_s0, L0_ffn_s1,
                L0_ffn_s2):
    x = np.ascontiguousarray(np.asarray(L0_attn_x, dtype=np.float32))
    wqt = np.ascontiguousarray(np.asarray(L0_attn_Wq, np.float32).T)
    wkt = np.ascontiguousarray(np.asarray(L0_attn_Wk, np.float32).T)
    wvt = np.ascontiguousarray(np.asarray(L0_attn_Wv, np.float32).T)
    wot = np.ascontiguousarray(np.asarray(L0_attn_Wo, np.float32).T)
    k0 = np.asarray(L0_ffn_k0).astype(np.uint8)
    k0s = np.ascontiguousarray(
        k0.reshape(16, 128, 32, 128).transpose(2, 1, 0, 3)).reshape(HID, 2 * D)
    k1 = np.asarray(L0_ffn_k1).astype(np.uint8)
    k1s = np.ascontiguousarray(
        k1.reshape(32, 128, 32, 128).transpose(2, 1, 0, 3)).reshape(HID, HID)
    k2 = np.ascontiguousarray(np.asarray(L0_ffn_k2).astype(np.uint8))
    scal = np.zeros((128, 4), np.float32)
    scal[:, 0] = np.float32(L0_ffn_s0)
    scal[:, 1] = np.float32(L0_ffn_s1)
    scal[:, 2] = np.float32(L0_ffn_s2)
    cmsk, cmsk0, cid, cidr, cn12 = _consts()
    czero = np.zeros((128, 8 * 384), np.float32)
    shared = dict(wqt=wqt, wkt=wkt, wvt=wvt, wot=wot, k0s=k0s, k1s=k1s, k2=k2,
                  scal=scal, cmsk=cmsk, cmsk0=cmsk0, cid=cid, cidr=cidr,
                  cn12=cn12, czero=czero)
    in_maps = []
    for c in range(8):
        m = dict(shared)
        m["x"] = np.ascontiguousarray(x[2 * c:2 * c + 2].reshape(2 * SEQ, D))
        in_maps.append(m)
    return in_maps


def kernel(L0_attn_x, L0_attn_Wq, L0_attn_Wk, L0_attn_Wv, L0_attn_Wo,
           L0_ffn_k0, L0_ffn_k1, L0_ffn_k2, L0_ffn_s0, L0_ffn_s1, L0_ffn_s2):
    in_maps = make_inputs(L0_attn_x, L0_attn_Wq, L0_attn_Wk, L0_attn_Wv,
                          L0_attn_Wo, L0_ffn_k0, L0_ffn_k1, L0_ffn_k2,
                          L0_ffn_s0, L0_ffn_s1, L0_ffn_s2)
    if "nc" not in _CACHE:
        _CACHE["nc"] = build_program()
    nc = _CACHE["nc"]
    res = run_bass_kernel_spmd(nc, in_maps, core_ids=list(range(8)))
    out = np.empty((16, SEQ, D), np.float32)
    for c in range(8):
        out[2 * c:2 * c + 2] = np.asarray(res.results[c]["y"]).reshape(2, SEQ, D)
    return out


# version 17
# speedup vs baseline: 1.1466x; 1.0025x over previous
import numpy as np
import ml_dtypes
import concourse.bass as bass
import concourse.mybir as mybir
from concourse.bass_utils import run_bass_kernel_spmd

F32 = mybir.dt.float32
F32R = mybir.dt.float32r
BF16 = mybir.dt.bfloat16
U8 = mybir.dt.uint8
AF = mybir.ActivationFunctionType
ALU = mybir.AluOpType
AX = mybir.AxisListType.X

D = 1024
NH = 16
HD = 64
CH = 128
GRP = 256
HID = 4096
SEQ = 2048
NSEQ = 2
NGRP = SEQ // GRP
MASK_RAW = -262144.0
EXP_SHIFT = -12.0

ENG = ("T", "V", "S", "Q0", "Q1")
NPOOL = {"Q0": 12, "Q1": 8, "Q2": 8}
DIMS = (["T", "V", "S", "G"]
        + [(q, k) for q in ("Q0", "Q1", "Q2") for k in range(NPOOL[q])])


class Prog:

    def __init__(self, nc):
        self.nc = nc
        self.thunks = {e: [] for e in ENG}
        self.cnt = {e: 0 for e in ("T", "V", "S", "G")}
        self.dcnt = {q: 0 for q in ("Q0", "Q1", "Q2")}
        self.vc = {e: {d: 0 for d in DIMS} for e in ENG}
        self.snap = {d: [None] for d in DIMS}
        self.bufs = {}
        self.sems = {}
        self.pool_uses = {q: [0] * NPOOL[q] for q in ("Q0", "Q1", "Q2")}
        self.pend = {}

    def make_sems(self, estack):
        for e in ("T", "V", "S", "G"):
            self.sems[e] = estack.enter_context(self.nc.semaphore("sem" + e))
        for q in ("Q0", "Q1", "Q2"):
            for k in range(NPOOL[q]):
                self.sems[(q, k)] = estack.enter_context(
                    self.nc.semaphore("sem%s_%d" % (q, k)))

    def _need_for(self, reads, writes):
        need = {}
        for key in reads:
            b = self.bufs.get(key)
            if b and b[0] is not None:
                e, i = b[0]
                if i > need.get(e, 0):
                    need[e] = i
        for key in writes:
            b = self.bufs.get(key)
            if b:
                if b[0] is not None:
                    e, i = b[0]
                    if i > need.get(e, 0):
                        need[e] = i
                for e, i in b[1].items():
                    if i > need.get(e, 0):
                        need[e] = i
        return need

    def _emit_waits(self, eng, need):
        myvc = self.vc[eng]
        for d in sorted(need, key=str):
            i = need[d]
            if myvc[d] >= i:
                continue
            if isinstance(d, tuple):
                self.thunks[eng].append(("w", self.sems[d], 16 * i))
            else:
                self.thunks[eng].append(("w", self.sems[d], i))
            sn = self.snap[d][i]
            for dd, v in sn.items():
                if v > myvc[dd]:
                    myvc[dd] = v
            if i > myvc[d]:
                myvc[d] = i

    def _marker(self, eng):
        self.cnt[eng] += 1
        i = self.cnt[eng]
        sn = dict(self.vc[eng])
        sn[eng] = i
        self.snap[eng].append(sn)
        return (eng, i)

    def _register(self, reads, writes, marker):
        for key in writes:
            self.bufs[key] = [marker, {}]
        e, i = marker
        for key in reads:
            b = self.bufs.get(key)
            if b is None:
                b = [None, {}]
                self.bufs[key] = b
            if i > b[1].get(e, 0):
                b[1][e] = i

    def op(self, eng, emit, reads=(), writes=()):
        stream = "Q1" if eng == "G" else eng
        self._emit_waits(stream, self._need_for(reads, writes))
        self.cnt[eng] += 1
        i = self.cnt[eng]
        sn = dict(self.vc[stream])
        sn[eng] = i
        self.snap[eng].append(sn)
        self.thunks[stream].append(("i", emit, self.sems[eng], 1))
        self._register(reads, writes, (eng, i))

    def gop(self, eng, emit, reads=(), gkey=None, first=False, close=False,
            final=False):
        writes = (gkey,) if first else ()
        self._emit_waits(eng, self._need_for(reads, writes))
        pend = self.pend.setdefault((eng, gkey), set())
        pend.update(reads)
        if close:
            m = self._marker(eng)
            self.thunks[eng].append(("i", emit, self.sems[eng], 1))
            self._register(tuple(pend), (gkey,) if final else (), m)
            pend.clear()
            if final:
                self.pend.pop((eng, gkey), None)
        else:
            self.thunks[eng].append(("x", emit))

    def dma(self, q, out_ap, in_ap, reads=(), writes=()):
        stream = "S" if q == "Q2" else q
        R = NPOOL[q]
        idx = self.dcnt[q]
        self.dcnt[q] = idx + 1
        k = idx % R
        d = (q, k)
        prev_uses = self.pool_uses[q][k]
        if prev_uses > 0 and self.vc[stream][d] < prev_uses:
            self.thunks[stream].append(("w", self.sems[d], 16 * prev_uses))
            sn = self.snap[d][prev_uses]
            myvc = self.vc[stream]
            for dd, v in sn.items():
                if v > myvc[dd]:
                    myvc[dd] = v
        self._emit_waits(stream, self._need_for(reads, writes))
        self.thunks[stream].append(("d", out_ap, in_ap, self.sems[d]))
        u = prev_uses + 1
        self.pool_uses[q][k] = u
        sn = dict(self.vc[stream])
        sn[d] = u
        self.snap[d].append(sn)
        self._register(reads, writes, (d, u))

    def finalize_queues(self):
        for q in ("Q0", "Q1", "Q2"):
            stream = "S" if q == "Q2" else q
            for k in range(NPOOL[q]):
                u = self.pool_uses[q][k]
                if u > 0:
                    self.thunks[stream].append(("w", self.sems[(q, k)], 16 * u))

    def render(self):
        nc = self.nc

        def runner(e):
            lst = self.thunks[e]
            def run(engine_obj):
                for t in lst:
                    tag = t[0]
                    if tag == "w":
                        engine_obj.wait_ge(t[1], t[2])
                    elif tag == "i":
                        t[1](engine_obj).then_inc(t[2], t[3])
                    elif tag == "x":
                        t[1](engine_obj)
                    else:
                        engine_obj.dma_start(out=t[1], in_=t[2]).then_inc(t[3], 16)
            return run

        with nc.Block() as block:
            block.tensor(runner("T"))
            block.vector(runner("V"))
            block.scalar(runner("S"))
            block.sync(runner("Q0"))
            block.gpsimd(runner("Q1"))



def build_program(n_seq=NSEQ, n_grp=NGRP):
    import contextlib
    nc = bass.Bass("TRN2", target_bir_lowering=False, debug=False)

    x_d = nc.dram_tensor("x", [NSEQ * SEQ, D], F32, kind="ExternalInput").ap()
    y_d = nc.dram_tensor("y", [NSEQ * SEQ, D], F32, kind="ExternalOutput").ap()
    wt_d = {}
    for nm, t in (("wq", "wqt"), ("wk", "wkt"), ("wv", "wvt"), ("wo", "wot")):
        wt_d[nm] = nc.dram_tensor(t, [D, D], F32R, kind="ExternalInput").ap()
    k0s_d = nc.dram_tensor("k0s", [HID, 2 * D], U8, kind="ExternalInput").ap()
    k1s_d = nc.dram_tensor("k1s", [HID, HID], U8, kind="ExternalInput").ap()
    k2_d = nc.dram_tensor("k2", [HID, D], U8, kind="ExternalInput").ap()
    scal_d = nc.dram_tensor("scal", [128, 4], F32, kind="ExternalInput").ap()
    cmsk_d = nc.dram_tensor("cmsk", [128, 256], F32R, kind="ExternalInput").ap()
    cmsk0_d = nc.dram_tensor("cmsk0", [128, 256], F32R,
                             kind="ExternalInput").ap()
    cid_d = nc.dram_tensor("cid", [128, 128], BF16, kind="ExternalInput").ap()
    cidr_d = nc.dram_tensor("cidr", [128, 128], F32R, kind="ExternalInput").ap()
    cn12_d = nc.dram_tensor("cn12", [128, 1], F32, kind="ExternalInput").ap()
    czero_d = nc.dram_tensor("czero", [128, 8 * 384], F32R,
                             kind="ExternalInput").ap()
    d0s_d = nc.dram_tensor("d0s", [HID, D], F32R, kind="Internal").ap()
    w1pm_d = nc.dram_tensor("w1pm", [HID, HID], F32R, kind="Internal").ap()
    w2pm_d = nc.dram_tensor("w2pm", [HID, D], F32R, kind="Internal").ap()

    estack = contextlib.ExitStack()
    with estack:
        A = nc.alloc_sbuf_tensor
        s_cid = A("s_cid", [128, 128], BF16)
        s_cidr = A("s_cidr", [128, 128], F32R)
        s_msk = A("s_msk", [128, 256], F32R)
        s_msk0 = A("s_msk0", [128, 256], F32R)
        s_scal = A("s_scal", [128, 4], F32)
        s_n12 = A("s_n12", [128, 1], F32)
        s_S0 = A("s_S0", [128, 1], F32)
        s_S1 = A("s_S1", [128, 1], F32)
        s_S2 = A("s_S2", [128, 1], F32)
        s_ln = A("s_ln", [128, 16], F32)
        s_sums = A("s_sums", [128, NH], F32)
        s_rcp = A("s_rcp", [128, NH], F32)
        s_pre = A("s_pre", [128, 1024], BF16)
        s_xc = A("s_xc", [128, 6 * D], F32)
        s_xnf = A("s_xnf", [128, D], F32)
        s_xnt = {p: A("s_xnt_" + p, [128, D], BF16) for p in ("hi", "lo")}
        s_xn = A("s_xn", [128, 8 * 512], F32R)
        s_q = A("s_q", [128, 8 * GRP], F32R)
        s_k = A("s_k", [128, 8 * 384], F32R)
        s_v = {p: A("s_v_" + p, [128, 3 * D], BF16) for p in ("hi", "lo")}
        s_O = A("s_O", [128, 8 * GRP], F32R)
        s_P0 = A("s_P0", [128, 2 * 256], F32)
        s_Ph = A("s_Ph", [128, 2 * 256], BF16)
        s_Pl = A("s_Pl", [128, 2 * 256], BF16)
        s_PT = A("s_PT", [128, 2 * 512], BF16)
        s_wbuf = [A("s_wbuf%d" % b, [128, 512], F32R) for b in range(4)]
        s_d0b = [A("s_d0b%d" % b, [128, 512], F32R) for b in range(3)]
        s_k1b = [A("s_k1b%d" % b, [128, 512], F32R) for b in range(3)]
        s_w2b = [A("s_w2b%d" % b, [128, D], F32R) for b in range(4)]
        s_a0 = A("s_a0", [128, 32 * 512], F32R)
        s_a1 = A("s_a1", [128, 4 * 512], F32R)

        ps = [nc.alloc_psum_tensor("ps%d" % i, [128, 512], F32) for i in range(6)]
        psb = [nc.alloc_psum_tensor("pb%d" % i, [128, 1024], BF16) for i in range(2)]

        pr = Prog(nc)
        pr.make_sems(estack)

        def ts(eng, out_ap, in_ap, s1, s2, op0, op1, reads, writes):
            def emit(e, o=out_ap, i=in_ap, s1=s1, s2=s2, op0=op0, op1=op1):
                return e.tensor_scalar(out=o, in0=i, scalar1=s1, scalar2=s2,
                                       op0=op0, op1=op1)
            pr.op(eng, emit, reads=reads, writes=writes)

        def stt(eng, out_ap, in0_ap, sc, in1_ap, op0, op1, reads, writes):
            def emit(e, o=out_ap, i0=in0_ap, sc=sc, i1=in1_ap, op0=op0, op1=op1):
                return e.scalar_tensor_tensor(out=o, in0=i0, scalar=sc, in1=i1,
                                              op0=op0, op1=op1)
            pr.op(eng, emit, reads=reads, writes=writes)

        def tt(eng, out_ap, a_ap, b_ap, alu, reads, writes):
            def emit(e, o=out_ap, a=a_ap, b=b_ap, alu=alu):
                return e.tensor_tensor(out=o, in0=a, in1=b, op=alu)
            pr.op(eng, emit, reads=reads, writes=writes)

        def cp(eng, out_ap, in_ap, reads, writes):
            if eng == "S":
                def emit(e, o=out_ap, i=in_ap):
                    return e.copy(out=o, in_=i)
            else:
                def emit(e, o=out_ap, i=in_ap):
                    return e.tensor_copy(out=o, in_=i)
            pr.op(eng, emit, reads=reads, writes=writes)

        def memset(eng, ap, val, reads, writes):
            def emit(e, a=ap, v=val):
                return e.memset(a, v)
            pr.op(eng, emit, reads=reads, writes=writes)

        def act(out_ap, in_ap, func, bias, scale, accum, reads, writes):
            def emit(e, o=out_ap, i=in_ap, f=func, b=bias, s=scale, a=accum):
                return e.activation(out=o, in_=i, func=f, bias=b, scale=s,
                                    accum_out=a)
            pr.op("S", emit, reads=reads, writes=writes)

        def mm(gkey, out_ap, lhsT, rhs, reads, start, stop, close, final, tp=None):
            def emit(e, o=out_ap, l=lhsT, r=rhs, st=start, sp=stop, tp=tp):
                return e.matmul(o, l, r, start=st, stop=sp, tile_position=tp)
            pr.gop("T", emit, reads=reads, gkey=gkey, first=start, close=close,
                   final=final)

        def tr(gkey, out_ap, in_ap, reads, first, close, final):
            def emit(e, o=out_ap, i=in_ap):
                return e.transpose(o, i, s_cid[:])
            pr.gop("T", emit, reads=tuple(reads) + (("cid",),), gkey=gkey,
                   first=first, close=close, final=final)

        groups = [(sq, g) for sq in range(n_seq) for g in range(n_grp)]
        assert len(groups) % 2 == 0
        npairs = len(groups) // 2

        def xnk(pos):
            return [("xn", pos, 0), ("xn", pos, 1)]
        xn_all = xnk(0) + xnk(1)
        a0keys = [("a0", m) for m in range(32)]
        r1024 = 1.0 / 1024.0

        pcnt = [0]

        def pre_consts():
            pr.dma("Q0", s_cid[:], cid_d, writes=[("cid",)])
            pr.dma("Q0", s_cidr[:], cidr_d, writes=[("cidr",)])
            pr.dma("Q0", s_msk[:], cmsk_d, writes=[("msk",)])
            pr.dma("Q0", s_msk0[:], cmsk0_d, writes=[("msk0",)])
            pr.dma("Q0", s_scal[:], scal_d, writes=[("scal",)])
            pr.dma("Q0", s_n12[:], cn12_d, writes=[("n12",)])
            ts("V", s_S0[:], s_scal[:, 0:1], 4.0, None, ALU.mult, ALU.bypass,
               reads=[("scal",)], writes=[("S0",)])
            ts("V", s_S1[:], s_scal[:, 1:2], 2.0, None, ALU.mult, ALU.bypass,
               reads=[("scal",)], writes=[("S1",)])
            ts("V", s_S2[:], s_scal[:, 2:3], 1.0, None, ALU.mult, ALU.bypass,
               reads=[("scal",)], writes=[("S2",)])
            pr.dma("Q0", s_k[:], czero_d, reads=[],
                   writes=[("kcar",)] + [("k", b) for b in range(4)])
            for p in ("hi", "lo"):
                memset("V", s_v[p][:, 2 * D:3 * D], 0.0, reads=[],
                       writes=[("v", p, 2, 0), ("v", p, 2, 1)])

        pre_tasks = []

        def task_d0(m, qh):
            def run():
                kt = ("pre", 0)
                kb = ("pre", 1)
                bt = s_pre[:, 0:512]
                bb = s_pre[:, 512:1024]
                r0 = m * 128
                pr.dma("Q1", bt, k0s_d[r0:r0 + 128, qh * 512:qh * 512 + 512],
                       reads=[kt], writes=[kt])
                pr.dma("Q1", bb, k0s_d[r0:r0 + 128,
                                       1024 + qh * 512:1024 + qh * 512 + 512],
                       reads=[kb], writes=[kb])
                tt("V", bt, bt, bb, ALU.subtract, reads=[kt, kb], writes=[kt])
                pr.dma("Q1", d0s_d[r0:r0 + 128, qh * 512:(qh + 1) * 512], bt,
                       reads=[kt], writes=[("d0sd", m)])
            return run

        def task_pm(src_d, dst_d, rc, c0, c1, wkey, eng):
            def run():
                kk = [("pre", 0), ("pre", 1)]
                buf = s_pre[:, 0:1024]
                pr.dma("Q1", buf, src_d[rc:rc + 128, c0:c1], reads=kk,
                       writes=kk)
                if eng == "S":
                    act(buf, buf, AF.Copy, -1.0, 2.0, None, reads=kk,
                        writes=kk)
                else:
                    ts("V", buf, buf, 2.0, -1.0, ALU.mult, ALU.add,
                       reads=kk, writes=kk)
                pr.dma("Q1", dst_d[rc:rc + 128, c0:c1], buf, reads=kk,
                       writes=[wkey])
            return run

        def build_pre_tasks():
            for m in range(32):
                for qh in range(2):
                    pre_tasks.append(task_d0(m, qh))
            for oc in range(32):
                pre_tasks.append(task_pm(k2_d, w2pm_d, oc * 128, 0, D,
                                         ("w2pmd", oc), ("V", "S")[oc % 2]))
            for m2 in range(32):
                for qf in range(4):
                    pre_tasks.append(task_pm(
                        k1s_d, w1pm_d, m2 * 128, qf * 1024, (qf + 1) * 1024,
                        ("w1pmd", m2, qf // 2), ("V", "S")[qf % 2]))

        def drain(n):
            for _ in range(n):
                if pre_tasks:
                    pre_tasks.pop(0)()

        def phase_X_ln(gi, cc):
            sq, g = groups[gi]
            gch = 2 * gi + cc
            slot = gch % 6
            xt = s_xc[:, slot * D:(slot + 1) * D]
            kx = ("xc", slot)
            r0 = sq * SEQ + g * GRP + cc * 128
            L = lambda i, cc=cc: ("ln", cc, i)
            lc = s_ln[:, cc * 8:(cc + 1) * 8]
            pr.dma("Q0", xt, x_d[r0:r0 + 128, :], reads=[kx], writes=[kx])
            def em_rs(e, o=lc[:, 0:1], i=xt):
                return e.reduce_sum(o, i, AX)
            pr.op("V", em_rs, reads=[kx], writes=[L(0)])
            ts("V", lc[:, 1:2], lc[:, 0:1], r1024, None, ALU.mult,
               ALU.bypass, reads=[L(0)], writes=[L(1)])
            act(s_xnt["hi"][:, 0:D], xt, AF.Square, 0.0, 1.0, lc[:, 2:3],
                reads=[kx], writes=[("xnt", "hi"), L(2)])
            ts("V", lc[:, 3:4], lc[:, 2:3], r1024, None, ALU.mult,
               ALU.bypass, reads=[L(2)], writes=[L(3)])
            tt("V", lc[:, 4:5], lc[:, 1:2], lc[:, 1:2], ALU.mult,
               reads=[L(1)], writes=[L(4)])
            tt("V", lc[:, 5:6], lc[:, 3:4], lc[:, 4:5], ALU.subtract,
               reads=[L(3), L(4)], writes=[L(5)])
            ts("V", lc[:, 5:6], lc[:, 5:6], 1.0, 1e-5, ALU.mult, ALU.add,
               reads=[L(5)], writes=[L(5)])
            def em_sq(e, o=lc[:, 6:7], i=lc[:, 5:6]):
                return e.sqrt(o, i)
            pr.op("S", em_sq, reads=[L(5)], writes=[L(6)])
            def em_rc(e, o=lc[:, 7:8], i=lc[:, 6:7]):
                return e.reciprocal(o, i)
            pr.op("V", em_rc, reads=[L(6)], writes=[L(7)])
            xnf = s_xnf[:, 0:D]
            ts("V", xnf, xt, lc[:, 1:2], None, ALU.subtract, ALU.bypass,
               reads=[kx, L(1)], writes=[("xnf",)])
            ts("V", s_xnt["hi"][:, 0:D], xnf, lc[:, 7:8], None, ALU.mult,
               ALU.bypass, reads=[("xnf",), L(7)], writes=[("xnt", "hi")])
            stt("V", s_xnt["lo"][:, 0:D], xnf, lc[:, 7:8],
                s_xnt["hi"][:, 0:D], ALU.mult, ALU.subtract,
                reads=[("xnf",), L(7), ("xnt", "hi")], writes=[("xnt", "lo")])

        def phase_X_tr(gi, cc):
            pos = gi % 2
            for ip, p in enumerate(("hi", "lo")):
                bk = ("pb", ip)
                for ic in range(8):
                    src_ = s_xnt[p][:, ic * 128:(ic + 1) * 128]
                    dst = psb[ip][:, ic * 128:(ic + 1) * 128]
                    tr(bk, dst, src_, [("xnt", p)],
                       first=(ic == 0), close=(ic == 7), final=(ic == 7))
            cp("V", s_pre[:], psb[1][:, 0:1024],
               reads=[("pb", 1)], writes=[("pre", 0), ("pre", 1)])
            dsts = bass.AP(s_xn, pos * 256 + cc * 128,
                           [[8 * 512, 128], [512, 8], [1, 128]])
            src0 = bass.AP(psb[0], 0, [[1024, 128], [128, 8], [1, 128]])
            src1 = bass.AP(s_pre, 0, [[1024, 128], [128, 8], [1, 128]])
            tt("V", dsts, src0, src1, ALU.add,
               reads=[("pb", 0), ("pb", 1), ("pre", 0), ("pre", 1)],
               writes=[("xn", pos, cc)])

        def xblock(gi, cc):
            phase_X_ln(gi, cc)
            phase_X_tr(gi, cc)

        def panel_load(nm, ic, hf):
            b = pcnt[0] % 4
            pcnt[0] += 1
            pr.dma("Q0", s_wbuf[b][:],
                   wt_d[nm][ic * 128:(ic + 1) * 128, hf * 512:(hf + 1) * 512],
                   reads=[("wbuf", b)], writes=[("wbuf", b)])
            return b

        def phase_P_qk(gi):
            pos = gi % 2
            for nm, okey in (("wq", "q"), ("wk", "k")):
                for ic in range(8):
                    for hf in range(2):
                        b = panel_load(nm, ic, hf)
                        for mi in range(4):
                            m = hf * 4 + mi
                            bank = m // 2
                            bk = ("ps", bank)
                            lh = s_wbuf[b][:, mi * 128:(mi + 1) * 128]
                            rh = s_xn[:, ic * 512 + pos * 256:
                                      ic * 512 + pos * 256 + 256]
                            out = ps[bank][:, (m % 2) * 256:(m % 2 + 1) * 256]
                            rds = ([("wbuf", b)] + xnk(pos)) if mi % 2 == 0 else []
                            mm(bk, out, lh, rh, rds,
                               start=(ic == 0 and m % 2 == 0),
                               stop=(ic == 7 and m % 2 == 1),
                               close=(m % 2 == 1),
                               final=(ic == 7 and m % 2 == 1))
                for bank in range(4):
                    bk = ("ps", bank)
                    if okey == "q":
                        dst = s_q[:, bank * 512:(bank + 1) * 512]
                        src = ps[bank][:, 0:512]
                    else:
                        dst = bass.AP(s_k, 2 * bank * 384 + 128,
                                      [[8 * 384, 128], [384, 2], [1, 256]])
                        src = bass.AP(ps[bank], 0,
                                      [[512, 128], [256, 2], [1, 256]])
                    cp("S", dst, src, reads=[bk], writes=[(okey, bank)])

        def phase_P_v(gi):
            pos = gi % 2
            vch0 = 2 * gi
            for ic in range(8):
                for hf in range(2):
                    b = panel_load("wv", ic, hf)
                    for tc in range(2):
                        bank = tc * 2 + hf
                        lh = s_xn[:, ic * 512 + pos * 256 + tc * 128:
                                  ic * 512 + pos * 256 + (tc + 1) * 128]
                        out = ps[bank][:, 0:512]
                        mm(("ps", bank), out, lh, s_wbuf[b][:],
                           [("wbuf", b)] + xnk(pos),
                           start=(ic == 0), stop=(ic == 7), close=True,
                           final=(ic == 7))
            for tc in range(2):
                sl = (vch0 + tc) % 3
                for hf in range(2):
                    bank = tc * 2 + hf
                    bk = ("ps", bank)
                    hi = s_v["hi"][:, sl * D + hf * 512:sl * D + (hf + 1) * 512]
                    lo = s_v["lo"][:, sl * D + hf * 512:sl * D + (hf + 1) * 512]
                    cp("S", hi, ps[bank][:, 0:512], reads=[bk],
                       writes=[("v", "hi", sl, hf)])
                    stt("V", lo, ps[bank][:, 0:512], 1.0, hi, ALU.mult,
                        ALU.subtract, reads=[bk, ("v", "hi", sl, hf)],
                        writes=[("v", "lo", sl, hf)])

        def phase_A(gi):
            sq, g = groups[gi]
            for cc in range(2):
                vch = 2 * gi + cc
                seq_start = (g == 0 and cc == 0)
                msk_s = s_msk0 if seq_start else s_msk
                mkey = ("msk0",) if seq_start else ("msk",)

                def vap(p, h, blk, vch=vch):
                    sl = (vch - 1 + blk) % 3
                    c0 = h * HD
                    return (s_v[p][:, sl * D + c0:sl * D + c0 + HD],
                            ("v", p, sl, c0 // 512))

                def s_wave(w, cc=cc, msk_s=msk_s, mkey=mkey):
                    sbank = 2 + (w % 4)
                    sbk = ("ps", sbank)
                    for hh in range(2):
                        h = 2 * w + hh
                        pb = 64 * (h % 2)
                        m = h // 2
                        tp = (pb, 0) if pb else None
                        qh = s_q[pb:pb + 64, m * 256 + cc * 128:
                                 m * 256 + (cc + 1) * 128]
                        kh = s_k[pb:pb + 64, m * 384 + cc * 128:
                                 m * 384 + cc * 128 + 256]
                        omsk = ps[sbank][:, hh * 256:hh * 256 + 256]
                        rds = [("q", m // 2), ("k", m // 2)]
                        if cc == 0:
                            rds.append(("kcar",))
                        mm(sbk, omsk, qh, kh, rds, start=(hh == 0),
                           stop=False, close=False, final=False, tp=tp)
                        mm(sbk, omsk, s_cidr[:], msk_s[:],
                           [mkey, ("cidr",)], start=False, stop=(hh == 1),
                           close=True, final=(hh == 1))

                def head_work(w, cc=cc, vap=vap):
                    sbank = 2 + (w % 4)
                    sbk = ("ps", sbank)
                    for hh in range(2):
                        h = 2 * w + hh
                        slot = h % 2
                        act(s_P0[:, slot * 256:(slot + 1) * 256],
                            ps[sbank][:, hh * 256:hh * 256 + 256],
                            AF.Exp, s_n12[:, 0:1], 0.125,
                            s_sums[:, h:h + 1],
                            reads=[sbk, ("n12",)],
                            writes=[("P0", slot), ("sums", h)])
                    def em_rcp(e, o=s_rcp[:, 2 * w:2 * w + 2],
                               i=s_sums[:, 2 * w:2 * w + 2]):
                        return e.reciprocal(o, i)
                    pr.op("V", em_rcp,
                          reads=[("sums", 2 * w), ("sums", 2 * w + 1)],
                          writes=[("rcp", w)])
                    for hh in range(2):
                        h = 2 * w + hh
                        slot = h % 2
                        p0 = s_P0[:, slot * 256:(slot + 1) * 256]
                        ph = s_Ph[:, slot * 256:(slot + 1) * 256]
                        pl = s_Pl[:, slot * 256:(slot + 1) * 256]
                        rc = s_rcp[:, h:h + 1]
                        ts("V", ph, p0, rc, None, ALU.mult, ALU.bypass,
                           reads=[("P0", slot), ("rcp", w)],
                           writes=[("Ph", slot)])
                        stt("V", pl, p0, rc, ph, ALU.mult, ALU.subtract,
                            reads=[("P0", slot), ("rcp", w), ("Ph", slot)],
                            writes=[("Pl", slot)])
                    for hh in range(2):
                        h = 2 * w + hh
                        slot = h % 2
                        ptk = ("pb", slot)
                        for idx in range(4):
                            ti, blk = divmod(idx, 2)
                            sap = (s_Ph if ti == 0 else s_Pl)[
                                :, slot * 256 + blk * 128:
                                slot * 256 + (blk + 1) * 128]
                            dst = psb[slot][:, idx * 128:(idx + 1) * 128]
                            tr(ptk, dst, sap,
                               [("Ph", slot), ("Pl", slot)],
                               first=(idx == 0), close=(idx == 3),
                               final=(idx == 3))
                    for hh in range(2):
                        h = 2 * w + hh
                        slot = h % 2
                        ptk = ("pb", slot)
                        dpt = s_PT[:, slot * 512:(slot + 1) * 512]
                        cp("V", dpt, psb[slot][:, 0:512], reads=[ptk],
                           writes=[("PT", slot)])
                    for hh in range(2):
                        h = 2 * w + hh
                        slot = h % 2
                        obank = w // 4
                        obk = ("ps", obank)
                        pb = 64 * (h % 2)
                        qq = w % 4
                        oout = ps[obank][pb:pb + 64, qq * 128:(qq + 1) * 128]
                        otp = (0, pb) if pb else None
                        seqm = []
                        for blk in range(2):
                            vh, vkh = vap("hi", h, blk)
                            vl, vkl = vap("lo", h, blk)
                            pthi = s_PT[:, slot * 512 + blk * 128:
                                        slot * 512 + (blk + 1) * 128]
                            ptlo = s_PT[:, slot * 512 + 256 + blk * 128:
                                        slot * 512 + 256 + (blk + 1) * 128]
                            seqm.append((vh, pthi, [vkh, ("PT", slot)]))
                            seqm.append((vh, ptlo, []))
                            seqm.append((vl, pthi, [vkl]))
                        for si, (lh, rr, rds) in enumerate(seqm):
                            mm(obk, oout, lh, rr, rds,
                               start=(si == 0), stop=(si == 5),
                               close=(si == 5),
                               final=(w % 4 == 3 and hh == 1 and si == 5),
                               tp=otp)

                for w0 in range(3):
                    s_wave(w0)
                for w in range(NH // 2):
                    if w + 3 < NH // 2:
                        s_wave(w + 3)
                    head_work(w)
                for ob in range(2):
                    obk = ("ps", ob)
                    dsts = bass.AP(s_O, ob * 4 * GRP + cc * 128,
                                   [[8 * GRP, 128], [GRP, 4], [1, 128]])
                    srcs = bass.AP(ps[ob], 0, [[512, 128], [128, 4], [1, 128]])
                    cp("S", dsts, srcs, reads=[obk], writes=[("O", ob, cc)])
            dstc = bass.AP(s_k, 0, [[8 * 384, 128], [384, 8], [1, 128]])
            srcc = bass.AP(s_k, 256, [[8 * 384, 128], [384, 8], [1, 128]])
            pr.dma("Q1", dstc, srcc,
                   reads=[("k", b) for b in range(4)] + [("kcar",)],
                   writes=[("kcar",)])

        def phase_Wo(gi):
            gi2 = 2 * gi
            for ic in range(8):
                for hf in range(2):
                    b = panel_load("wo", ic, hf)
                    for tc in range(2):
                        bank = tc * 2 + hf
                        lh = s_O[:, ic * 256 + tc * 128:ic * 256 + (tc + 1) * 128]
                        mm(("ps", bank), ps[bank][:, 0:512], lh, s_wbuf[b][:],
                           [("wbuf", b), ("O", ic // 4, tc)],
                           start=(ic == 0), stop=(ic == 7), close=True,
                           final=(ic == 7))
            for tc in range(2):
                slot = (gi2 + tc) % 6
                kx = ("xc", slot)
                for hf in range(2):
                    bank = tc * 2 + hf
                    bk = ("ps", bank)
                    xap = s_xc[:, slot * D + hf * 512:slot * D + (hf + 1) * 512]
                    tt("V", xap, ps[bank][:, 0:512], xap, ALU.add,
                       reads=[bk, kx], writes=[kx])

        def phase_F0(pj):
            def strip_dma(idx):
                if idx >= 64:
                    return
                m, h = divmod(idx, 2)
                b = idx % 3
                assert ("d0sd", m) in pr.bufs
                pr.dma("Q0" if idx % 2 == 0 else "Q1", s_d0b[b][:],
                       d0s_d[m * 128:(m + 1) * 128, h * 512:(h + 1) * 512],
                       reads=[("d0b", b), ("d0sd", m)], writes=[("d0b", b)])
            for idx in range(3):
                strip_dma(idx)
            for m in range(32):
                drain(2)
                bank = 4 + (m % 2)
                bk = ("ps", bank)
                out = ps[bank][:, 0:512]
                for ic in range(8):
                    h = ic // 4
                    b = (m * 2 + h) % 3
                    lh = s_d0b[b][:, (ic % 4) * 128:(ic % 4 + 1) * 128]
                    rh = s_xn[:, ic * 512:(ic + 1) * 512]
                    rds = [("d0b", b)] + (xn_all if ic == 0 else [])
                    mm(bk, out, lh, rh, rds, start=(ic == 0), stop=(ic == 7),
                       close=(ic % 4 == 3), final=(ic == 7))
                    if ic % 4 == 3:
                        strip_dma(m * 2 + h + 3)
                act(s_a0[:, m * 512:(m + 1) * 512], out, AF.Relu, 0.0,
                    s_S0[:, 0:1], None, reads=[bk, ("S0",)],
                    writes=[("a0", m)])

        def g1_mms(m2):
            def strip_dma(idx):
                if idx >= 256:
                    return
                m2_, e_ = divmod(idx, 8)
                b = idx % 3
                assert ("w1pmd", m2_, e_ // 4) in pr.bufs
                pr.dma("Q0" if idx % 2 == 0 else "Q1", s_k1b[b][:],
                       w1pm_d[m2_ * 128:(m2_ + 1) * 128,
                              e_ * 512:(e_ + 1) * 512],
                       reads=[("k1b", b), ("w1pmd", m2_, e_ // 4)],
                       writes=[("k1b", b)])
            drain(3)
            if m2 == 0:
                for idx in range(3):
                    strip_dma(idx)
            sl = m2 % 4
            assert ("w2pmd", m2) in pr.bufs
            pr.dma("Q0", s_w2b[sl][:], w2pm_d[m2 * 128:(m2 + 1) * 128, :],
                   reads=[("w2b", sl), ("w2pmd", m2)], writes=[("w2b", sl)])
            bank = m2 % 2
            bk = ("ps", bank)
            out = ps[bank][:, 0:512]
            for kc in range(32):
                e = kc // 4
                b = (m2 * 8 + e) % 3
                lh = s_k1b[b][:, (kc % 4) * 128:(kc % 4 + 1) * 128]
                rh = s_a0[:, kc * 512:(kc + 1) * 512]
                rds = [("k1b", b)] + (a0keys if kc == 0 else [])
                mm(bk, out, lh, rh, rds, start=(kc == 0), stop=(kc == 31),
                   close=(kc % 4 == 3), final=(kc == 31))
                if kc % 4 == 3:
                    strip_dma(m2 * 8 + e + 3)
            act(s_a1[:, sl * 512:(sl + 1) * 512], out, AF.Relu, 0.0,
                s_S1[:, 0:1], None, reads=[bk, ("S1",)], writes=[("a1", sl)])

        def g2_batch(pj, bt):
            batch = range(4 * bt, 4 * bt + 4)
            akeys = [("a1", m2 % 4) for m2 in batch]
            wkeys = [("w2b", m2 % 4) for m2 in batch]
            for ti in range(8):
                tc, oh = divmod(ti, 2)
                bank = 2 + (ti % 4)
                gk = ("ps", bank)
                for j, m2 in enumerate(batch):
                    sl = m2 % 4
                    mm(gk, ps[bank][:, 0:512],
                       s_a1[:, sl * 512 + tc * 128:sl * 512 + (tc + 1) * 128],
                       s_w2b[sl][:, oh * 512:(oh + 1) * 512],
                       (akeys + wkeys) if j == 0 else [],
                       start=(j == 0), stop=(j == 3), close=(j == 3),
                       final=True)
                slot = (4 * pj + tc) % 6
                xap = s_xc[:, slot * D + oh * 512:slot * D + (oh + 1) * 512]
                stt("V", xap, ps[bank][:, 0:512], s_S2[:, 0:1], xap,
                    ALU.mult, ALU.add, reads=[gk, ("S2",), ("xc", slot)],
                    writes=[("xc", slot)])

        def phase_F12(pj):
            gi0 = 2 * pj
            nxtA = gi0 + 2 if gi0 + 2 < len(groups) else None
            for bt in range(8):
                for m2 in range(4 * bt, 4 * bt + 4):
                    g1_mms(m2)
                g2_batch(pj, bt)
                if nxtA is not None:
                    if bt == 1:
                        phase_X_ln(nxtA, 0)
                    elif bt == 2:
                        phase_X_tr(nxtA, 0)
                    elif bt == 3:
                        phase_X_ln(nxtA, 1)
                    elif bt == 4:
                        phase_X_tr(nxtA, 1)
            for pos in range(2):
                gi = gi0 + pos
                sq, g = groups[gi]
                for cc in range(2):
                    slot = (2 * gi + cc) % 6
                    r0 = sq * SEQ + g * GRP + cc * 128
                    pr.dma("Q1", y_d[r0:r0 + 128, :],
                           s_xc[:, slot * D:(slot + 1) * D],
                           reads=[("xc", slot)], writes=[("ydone", gi, cc)])

        pre_consts()
        build_pre_tasks()
        for cc in range(2):
            xblock(0, cc)
            drain(8)
        for pj in range(npairs):
            gia, gib = 2 * pj, 2 * pj + 1
            phase_P_qk(gia)
            drain(12)
            phase_P_v(gia)
            drain(8)
            phase_A(gia)
            drain(12)
            phase_X_ln(gib, 0)
            phase_X_tr(gib, 0)
            drain(8)
            phase_Wo(gia)
            drain(8)
            phase_X_ln(gib, 1)
            phase_X_tr(gib, 1)
            drain(8)
            phase_P_qk(gib)
            drain(12)
            phase_P_v(gib)
            drain(8)
            phase_A(gib)
            drain(8)
            phase_Wo(gib)
            drain(8)
            phase_F0(pj)
            phase_F12(pj)
        assert not pre_tasks, len(pre_tasks)

        pr.finalize_queues()
        pr.render()
    return nc




_CACHE = {}


def _consts():
    i = np.arange(128)[:, None]
    j = np.arange(256)[None, :]
    band = (j >= i + 1) & (j <= i + 128)
    valid0 = band & (j >= 128)
    cmsk = np.where(band, 0.0, MASK_RAW).astype(np.float32)
    cmsk0 = np.where(valid0, 0.0, MASK_RAW).astype(np.float32)
    cid = np.eye(128, dtype=ml_dtypes.bfloat16)
    cidr = np.eye(128, dtype=np.float32)
    cn12 = np.full((128, 1), EXP_SHIFT, dtype=np.float32)
    return cmsk, cmsk0, cid, cidr, cn12


def make_inputs(L0_attn_x, L0_attn_Wq, L0_attn_Wk, L0_attn_Wv, L0_attn_Wo,
                L0_ffn_k0, L0_ffn_k1, L0_ffn_k2, L0_ffn_s0, L0_ffn_s1,
                L0_ffn_s2):
    x = np.ascontiguousarray(np.asarray(L0_attn_x, dtype=np.float32))
    wqt = np.ascontiguousarray(np.asarray(L0_attn_Wq, np.float32).T)
    wkt = np.ascontiguousarray(np.asarray(L0_attn_Wk, np.float32).T)
    wvt = np.ascontiguousarray(np.asarray(L0_attn_Wv, np.float32).T)
    wot = np.ascontiguousarray(np.asarray(L0_attn_Wo, np.float32).T)
    k0 = np.asarray(L0_ffn_k0).astype(np.uint8)
    k0s = np.ascontiguousarray(
        k0.reshape(16, 128, 32, 128).transpose(2, 1, 0, 3)).reshape(HID, 2 * D)
    k1 = np.asarray(L0_ffn_k1).astype(np.uint8)
    k1s = np.ascontiguousarray(
        k1.reshape(32, 128, 32, 128).transpose(2, 1, 0, 3)).reshape(HID, HID)
    k2 = np.ascontiguousarray(np.asarray(L0_ffn_k2).astype(np.uint8))
    scal = np.zeros((128, 4), np.float32)
    scal[:, 0] = np.float32(L0_ffn_s0)
    scal[:, 1] = np.float32(L0_ffn_s1)
    scal[:, 2] = np.float32(L0_ffn_s2)
    cmsk, cmsk0, cid, cidr, cn12 = _consts()
    czero = np.zeros((128, 8 * 384), np.float32)
    shared = dict(wqt=wqt, wkt=wkt, wvt=wvt, wot=wot, k0s=k0s, k1s=k1s, k2=k2,
                  scal=scal, cmsk=cmsk, cmsk0=cmsk0, cid=cid, cidr=cidr,
                  cn12=cn12, czero=czero)
    in_maps = []
    for c in range(8):
        m = dict(shared)
        m["x"] = np.ascontiguousarray(x[2 * c:2 * c + 2].reshape(2 * SEQ, D))
        in_maps.append(m)
    return in_maps


def kernel(L0_attn_x, L0_attn_Wq, L0_attn_Wk, L0_attn_Wv, L0_attn_Wo,
           L0_ffn_k0, L0_ffn_k1, L0_ffn_k2, L0_ffn_s0, L0_ffn_s1, L0_ffn_s2):
    in_maps = make_inputs(L0_attn_x, L0_attn_Wq, L0_attn_Wk, L0_attn_Wv,
                          L0_attn_Wo, L0_ffn_k0, L0_ffn_k1, L0_ffn_k2,
                          L0_ffn_s0, L0_ffn_s1, L0_ffn_s2)
    if "nc" not in _CACHE:
        _CACHE["nc"] = build_program()
    nc = _CACHE["nc"]
    res = run_bass_kernel_spmd(nc, in_maps, core_ids=list(range(8)))
    out = np.empty((16, SEQ, D), np.float32)
    for c in range(8):
        out[2 * c:2 * c + 2] = np.asarray(res.results[c]["y"]).reshape(2, SEQ, D)
    return out


# version 18
# speedup vs baseline: 1.3727x; 1.1972x over previous
import numpy as np
import ml_dtypes
import concourse.bass as bass
import concourse.mybir as mybir
from concourse.bass_utils import run_bass_kernel_spmd

F32 = mybir.dt.float32
F32R = mybir.dt.float32r
BF16 = mybir.dt.bfloat16
U8 = mybir.dt.uint8
AF = mybir.ActivationFunctionType
ALU = mybir.AluOpType
AX = mybir.AxisListType.X

D = 1024
NH = 16
HD = 64
CH = 128
GRP = 256
HID = 4096
SEQ = 2048
NSEQ = 2
NGRP = SEQ // GRP
MASK_RAW = -262144.0
EXP_SHIFT = -12.0

ENG = ("T", "V", "S", "Q0", "Q1")
NPOOL = {"Q0": 12, "Q1": 8, "Q2": 8}
DIMS = (["T", "V", "S", "G"]
        + [(q, k) for q in ("Q0", "Q1", "Q2") for k in range(NPOOL[q])])


class Prog:

    def __init__(self, nc):
        self.nc = nc
        self.thunks = {e: [] for e in ENG}
        self.cnt = {e: 0 for e in ("T", "V", "S", "G")}
        self.dcnt = {q: 0 for q in ("Q0", "Q1", "Q2")}
        self.vc = {e: {d: 0 for d in DIMS} for e in ENG}
        self.snap = {d: [None] for d in DIMS}
        self.bufs = {}
        self.sems = {}
        self.pool_uses = {q: [0] * NPOOL[q] for q in ("Q0", "Q1", "Q2")}
        self.pend = {}

    def make_sems(self, estack):
        for e in ("T", "V", "S", "G"):
            self.sems[e] = estack.enter_context(self.nc.semaphore("sem" + e))
        for q in ("Q0", "Q1", "Q2"):
            for k in range(NPOOL[q]):
                self.sems[(q, k)] = estack.enter_context(
                    self.nc.semaphore("sem%s_%d" % (q, k)))

    def _need_for(self, reads, writes):
        need = {}
        for key in reads:
            b = self.bufs.get(key)
            if b and b[0] is not None:
                e, i = b[0]
                if i > need.get(e, 0):
                    need[e] = i
        for key in writes:
            b = self.bufs.get(key)
            if b:
                if b[0] is not None:
                    e, i = b[0]
                    if i > need.get(e, 0):
                        need[e] = i
                for e, i in b[1].items():
                    if i > need.get(e, 0):
                        need[e] = i
        return need

    def _emit_waits(self, eng, need):
        myvc = self.vc[eng]
        for d in sorted(need, key=str):
            i = need[d]
            if myvc[d] >= i:
                continue
            if isinstance(d, tuple):
                self.thunks[eng].append(("w", self.sems[d], 16 * i))
            else:
                self.thunks[eng].append(("w", self.sems[d], i))
            sn = self.snap[d][i]
            for dd, v in sn.items():
                if v > myvc[dd]:
                    myvc[dd] = v
            if i > myvc[d]:
                myvc[d] = i

    def _marker(self, eng):
        self.cnt[eng] += 1
        i = self.cnt[eng]
        sn = dict(self.vc[eng])
        sn[eng] = i
        self.snap[eng].append(sn)
        return (eng, i)

    def _register(self, reads, writes, marker):
        for key in writes:
            self.bufs[key] = [marker, {}]
        e, i = marker
        for key in reads:
            b = self.bufs.get(key)
            if b is None:
                b = [None, {}]
                self.bufs[key] = b
            if i > b[1].get(e, 0):
                b[1][e] = i

    def op(self, eng, emit, reads=(), writes=()):
        stream = "Q1" if eng == "G" else eng
        self._emit_waits(stream, self._need_for(reads, writes))
        self.cnt[eng] += 1
        i = self.cnt[eng]
        sn = dict(self.vc[stream])
        sn[eng] = i
        self.snap[eng].append(sn)
        self.thunks[stream].append(("i", emit, self.sems[eng], 1))
        self._register(reads, writes, (eng, i))

    def gop(self, eng, emit, reads=(), gkey=None, first=False, close=False,
            final=False):
        writes = (gkey,) if first else ()
        self._emit_waits(eng, self._need_for(reads, writes))
        pend = self.pend.setdefault((eng, gkey), set())
        pend.update(reads)
        if close:
            m = self._marker(eng)
            self.thunks[eng].append(("i", emit, self.sems[eng], 1))
            self._register(tuple(pend), (gkey,) if final else (), m)
            pend.clear()
            if final:
                self.pend.pop((eng, gkey), None)
        else:
            self.thunks[eng].append(("x", emit))

    def dma(self, q, out_ap, in_ap, reads=(), writes=()):
        stream = "S" if q == "Q2" else q
        R = NPOOL[q]
        idx = self.dcnt[q]
        self.dcnt[q] = idx + 1
        k = idx % R
        d = (q, k)
        prev_uses = self.pool_uses[q][k]
        if prev_uses > 0 and self.vc[stream][d] < prev_uses:
            self.thunks[stream].append(("w", self.sems[d], 16 * prev_uses))
            sn = self.snap[d][prev_uses]
            myvc = self.vc[stream]
            for dd, v in sn.items():
                if v > myvc[dd]:
                    myvc[dd] = v
        self._emit_waits(stream, self._need_for(reads, writes))
        self.thunks[stream].append(("d", out_ap, in_ap, self.sems[d]))
        u = prev_uses + 1
        self.pool_uses[q][k] = u
        sn = dict(self.vc[stream])
        sn[d] = u
        self.snap[d].append(sn)
        self._register(reads, writes, (d, u))

    def finalize_queues(self):
        for q in ("Q0", "Q1", "Q2"):
            stream = "S" if q == "Q2" else q
            for k in range(NPOOL[q]):
                u = self.pool_uses[q][k]
                if u > 0:
                    self.thunks[stream].append(("w", self.sems[(q, k)], 16 * u))

    def render(self):
        nc = self.nc

        def runner(e):
            lst = self.thunks[e]
            def run(engine_obj):
                for t in lst:
                    tag = t[0]
                    if tag == "w":
                        engine_obj.wait_ge(t[1], t[2])
                    elif tag == "i":
                        t[1](engine_obj).then_inc(t[2], t[3])
                    elif tag == "x":
                        t[1](engine_obj)
                    else:
                        engine_obj.dma_start(out=t[1], in_=t[2]).then_inc(t[3], 16)
            return run

        with nc.Block() as block:
            block.tensor(runner("T"))
            block.vector(runner("V"))
            block.scalar(runner("S"))
            block.sync(runner("Q0"))
            block.gpsimd(runner("Q1"))



def build_program(n_seq=NSEQ, n_grp=NGRP):
    import contextlib
    nc = bass.Bass("TRN2", target_bir_lowering=False, debug=False)

    x_d = nc.dram_tensor("x", [NSEQ * SEQ, D], F32, kind="ExternalInput").ap()
    y_d = nc.dram_tensor("y", [NSEQ * SEQ, D], F32, kind="ExternalOutput").ap()
    wt_d = {}
    for nm, t in (("wq", "wqt"), ("wk", "wkt"), ("wv", "wvt"), ("wo", "wot")):
        wt_d[nm] = nc.dram_tensor(t, [D, D], F32R, kind="ExternalInput").ap()
    k0s_d = nc.dram_tensor("k0s", [HID, 2 * D], U8, kind="ExternalInput").ap()
    k1s_d = nc.dram_tensor("k1s", [HID, HID], U8, kind="ExternalInput").ap()
    k2_d = nc.dram_tensor("k2", [HID, D], U8, kind="ExternalInput").ap()
    scal_d = nc.dram_tensor("scal", [128, 4], F32, kind="ExternalInput").ap()
    cmsk_d = nc.dram_tensor("cmsk", [128, 256], F32R, kind="ExternalInput").ap()
    cmsk0_d = nc.dram_tensor("cmsk0", [128, 256], F32R,
                             kind="ExternalInput").ap()
    cid_d = nc.dram_tensor("cid", [128, 128], BF16, kind="ExternalInput").ap()
    cidr_d = nc.dram_tensor("cidr", [128, 128], F32R, kind="ExternalInput").ap()
    cn12_d = nc.dram_tensor("cn12", [128, 1], F32, kind="ExternalInput").ap()
    czero_d = nc.dram_tensor("czero", [128, 8 * 384], F32R,
                             kind="ExternalInput").ap()
    d0s_d = nc.dram_tensor("d0s", [HID, D], F32R, kind="Internal").ap()
    w1pm_d = nc.dram_tensor("w1pm", [HID, HID], F32R, kind="Internal").ap()
    w2pm_d = nc.dram_tensor("w2pm", [HID, D], F32R, kind="Internal").ap()

    estack = contextlib.ExitStack()
    with estack:
        A = nc.alloc_sbuf_tensor
        s_cid = A("s_cid", [128, 128], BF16)
        s_cidr = A("s_cidr", [128, 128], F32R)
        s_msk = A("s_msk", [128, 256], F32R)
        s_msk0 = A("s_msk0", [128, 256], F32R)
        s_scal = A("s_scal", [128, 4], F32)
        s_n12 = A("s_n12", [128, 1], F32)
        s_S0 = A("s_S0", [128, 1], F32)
        s_S1 = A("s_S1", [128, 1], F32)
        s_S2 = A("s_S2", [128, 1], F32)
        s_ln = A("s_ln", [128, 16], F32)
        s_sums = A("s_sums", [128, NH], F32)
        s_rcp = A("s_rcp", [128, NH], F32)
        s_pre = A("s_pre", [128, 1024], BF16)
        s_xc = A("s_xc", [128, 5 * D], F32)
        s_xnf = A("s_xnf", [128, D], F32)
        s_xnt = {p: A("s_xnt_" + p, [128, D], BF16) for p in ("hi", "lo")}
        s_xn = A("s_xn", [128, 8 * 512], F32R)
        s_q = A("s_q", [128, 8 * GRP], F32R)
        s_k = A("s_k", [128, 8 * 384], F32R)
        s_v = {p: A("s_v_" + p, [128, 3 * D], BF16) for p in ("hi", "lo")}
        s_O = A("s_O", [128, 8 * GRP], F32R)
        s_P0 = A("s_P0", [128, 2 * 256], F32)
        s_Ph = A("s_Ph", [128, 2 * 256], BF16)
        s_Pl = A("s_Pl", [128, 2 * 256], BF16)
        s_PT = A("s_PT", [128, 2 * 512], BF16)
        s_wbuf = [A("s_wbuf%d" % b, [128, 512], F32R) for b in range(4)]
        s_d0b = [A("s_d0b%d" % b, [128, 512], F32R) for b in range(6)]
        s_k1b = [A("s_k1b%d" % b, [128, 512], F32R) for b in range(6)]
        s_w2b = [A("s_w2b%d" % b, [128, D], F32R) for b in range(2)]
        s_a0 = A("s_a0", [128, 32 * 512], F32R)
        s_a1 = A("s_a1", [128, 2 * 512], F32R)

        ps = [nc.alloc_psum_tensor("ps%d" % i, [128, 512], F32) for i in range(6)]
        psb = [nc.alloc_psum_tensor("pb%d" % i, [128, 1024], BF16) for i in range(2)]

        pr = Prog(nc)
        pr.make_sems(estack)

        def ts(eng, out_ap, in_ap, s1, s2, op0, op1, reads, writes):
            def emit(e, o=out_ap, i=in_ap, s1=s1, s2=s2, op0=op0, op1=op1):
                return e.tensor_scalar(out=o, in0=i, scalar1=s1, scalar2=s2,
                                       op0=op0, op1=op1)
            pr.op(eng, emit, reads=reads, writes=writes)

        def stt(eng, out_ap, in0_ap, sc, in1_ap, op0, op1, reads, writes):
            def emit(e, o=out_ap, i0=in0_ap, sc=sc, i1=in1_ap, op0=op0, op1=op1):
                return e.scalar_tensor_tensor(out=o, in0=i0, scalar=sc, in1=i1,
                                              op0=op0, op1=op1)
            pr.op(eng, emit, reads=reads, writes=writes)

        def tt(eng, out_ap, a_ap, b_ap, alu, reads, writes):
            def emit(e, o=out_ap, a=a_ap, b=b_ap, alu=alu):
                return e.tensor_tensor(out=o, in0=a, in1=b, op=alu)
            pr.op(eng, emit, reads=reads, writes=writes)

        def cp(eng, out_ap, in_ap, reads, writes):
            if eng == "S":
                def emit(e, o=out_ap, i=in_ap):
                    return e.copy(out=o, in_=i)
            else:
                def emit(e, o=out_ap, i=in_ap):
                    return e.tensor_copy(out=o, in_=i)
            pr.op(eng, emit, reads=reads, writes=writes)

        def memset(eng, ap, val, reads, writes):
            def emit(e, a=ap, v=val):
                return e.memset(a, v)
            pr.op(eng, emit, reads=reads, writes=writes)

        def act(out_ap, in_ap, func, bias, scale, accum, reads, writes):
            def emit(e, o=out_ap, i=in_ap, f=func, b=bias, s=scale, a=accum):
                return e.activation(out=o, in_=i, func=f, bias=b, scale=s,
                                    accum_out=a)
            pr.op("S", emit, reads=reads, writes=writes)

        def mm(gkey, out_ap, lhsT, rhs, reads, start, stop, close, final, tp=None):
            def emit(e, o=out_ap, l=lhsT, r=rhs, st=start, sp=stop, tp=tp):
                return e.matmul(o, l, r, start=st, stop=sp, tile_position=tp)
            pr.gop("T", emit, reads=reads, gkey=gkey, first=start, close=close,
                   final=final)

        def tr(gkey, out_ap, in_ap, reads, first, close, final):
            def emit(e, o=out_ap, i=in_ap):
                return e.transpose(o, i, s_cid[:])
            pr.gop("T", emit, reads=tuple(reads) + (("cid",),), gkey=gkey,
                   first=first, close=close, final=final)

        groups = [(sq, g) for sq in range(n_seq) for g in range(n_grp)]
        assert len(groups) % 2 == 0
        npairs = len(groups) // 2

        def xnk(pos):
            return [("xn", pos, 0), ("xn", pos, 1)]
        xn_all = xnk(0) + xnk(1)
        a0keys = [("a0", m) for m in range(32)]
        r1024 = 1.0 / 1024.0

        pcnt = [0]

        def pre_consts():
            pr.dma("Q0", s_cid[:], cid_d, writes=[("cid",)])
            pr.dma("Q0", s_cidr[:], cidr_d, writes=[("cidr",)])
            pr.dma("Q0", s_msk[:], cmsk_d, writes=[("msk",)])
            pr.dma("Q0", s_msk0[:], cmsk0_d, writes=[("msk0",)])
            pr.dma("Q0", s_scal[:], scal_d, writes=[("scal",)])
            pr.dma("Q0", s_n12[:], cn12_d, writes=[("n12",)])
            ts("V", s_S0[:], s_scal[:, 0:1], 4.0, None, ALU.mult, ALU.bypass,
               reads=[("scal",)], writes=[("S0",)])
            ts("V", s_S1[:], s_scal[:, 1:2], 2.0, None, ALU.mult, ALU.bypass,
               reads=[("scal",)], writes=[("S1",)])
            ts("V", s_S2[:], s_scal[:, 2:3], 1.0, None, ALU.mult, ALU.bypass,
               reads=[("scal",)], writes=[("S2",)])
            pr.dma("Q0", s_k[:], czero_d, reads=[],
                   writes=[("kcar",)] + [("k", b) for b in range(4)])
            for p in ("hi", "lo"):
                memset("V", s_v[p][:, 2 * D:3 * D], 0.0, reads=[],
                       writes=[("v", p, 2, 0), ("v", p, 2, 1)])

        pre_tasks = []

        def task_d0(m, qh):
            def run():
                kt = ("pre", 0)
                kb = ("pre", 1)
                bt = s_pre[:, 0:512]
                bb = s_pre[:, 512:1024]
                r0 = m * 128
                pr.dma("Q1", bt, k0s_d[r0:r0 + 128, qh * 512:qh * 512 + 512],
                       reads=[kt], writes=[kt])
                pr.dma("Q1", bb, k0s_d[r0:r0 + 128,
                                       1024 + qh * 512:1024 + qh * 512 + 512],
                       reads=[kb], writes=[kb])
                tt("V", bt, bt, bb, ALU.subtract, reads=[kt, kb], writes=[kt])
                pr.dma("Q1", d0s_d[r0:r0 + 128, qh * 512:(qh + 1) * 512], bt,
                       reads=[kt], writes=[("d0sd", m)])
            return run

        def task_pm(src_d, dst_d, rc, c0, c1, wkey, eng):
            def run():
                kk = [("pre", 0), ("pre", 1)]
                buf = s_pre[:, 0:1024]
                pr.dma("Q1", buf, src_d[rc:rc + 128, c0:c1], reads=kk,
                       writes=kk)
                if eng == "S":
                    act(buf, buf, AF.Copy, -1.0, 2.0, None, reads=kk,
                        writes=kk)
                else:
                    ts("V", buf, buf, 2.0, -1.0, ALU.mult, ALU.add,
                       reads=kk, writes=kk)
                pr.dma("Q1", dst_d[rc:rc + 128, c0:c1], buf, reads=kk,
                       writes=[wkey])
            return run

        def build_pre_tasks():
            for m in range(32):
                for qh in range(2):
                    pre_tasks.append(task_d0(m, qh))
            for oc in range(32):
                pre_tasks.append(task_pm(k2_d, w2pm_d, oc * 128, 0, D,
                                         ("w2pmd", oc), ("V", "S")[oc % 2]))
            for m2 in range(32):
                for qf in range(4):
                    pre_tasks.append(task_pm(
                        k1s_d, w1pm_d, m2 * 128, qf * 1024, (qf + 1) * 1024,
                        ("w1pmd", m2, qf // 2), ("V", "S")[qf % 2]))

        def drain(n):
            for _ in range(n):
                if pre_tasks:
                    pre_tasks.pop(0)()

        def phase_X_ln(gi, cc):
            sq, g = groups[gi]
            gch = 2 * gi + cc
            slot = gch % 5
            xt = s_xc[:, slot * D:(slot + 1) * D]
            kx = ("xc", slot)
            r0 = sq * SEQ + g * GRP + cc * 128
            L = lambda i, cc=cc: ("ln", cc, i)
            lc = s_ln[:, cc * 8:(cc + 1) * 8]
            pr.dma("Q0", xt, x_d[r0:r0 + 128, :], reads=[kx], writes=[kx])
            def em_rs(e, o=lc[:, 0:1], i=xt):
                return e.reduce_sum(o, i, AX)
            pr.op("V", em_rs, reads=[kx], writes=[L(0)])
            ts("V", lc[:, 1:2], lc[:, 0:1], r1024, None, ALU.mult,
               ALU.bypass, reads=[L(0)], writes=[L(1)])
            act(s_xnt["hi"][:, 0:D], xt, AF.Square, 0.0, 1.0, lc[:, 2:3],
                reads=[kx], writes=[("xnt", "hi"), L(2)])
            ts("V", lc[:, 3:4], lc[:, 2:3], r1024, None, ALU.mult,
               ALU.bypass, reads=[L(2)], writes=[L(3)])
            tt("V", lc[:, 4:5], lc[:, 1:2], lc[:, 1:2], ALU.mult,
               reads=[L(1)], writes=[L(4)])
            tt("V", lc[:, 5:6], lc[:, 3:4], lc[:, 4:5], ALU.subtract,
               reads=[L(3), L(4)], writes=[L(5)])
            ts("V", lc[:, 5:6], lc[:, 5:6], 1.0, 1e-5, ALU.mult, ALU.add,
               reads=[L(5)], writes=[L(5)])
            def em_sq(e, o=lc[:, 6:7], i=lc[:, 5:6]):
                return e.sqrt(o, i)
            pr.op("S", em_sq, reads=[L(5)], writes=[L(6)])
            def em_rc(e, o=lc[:, 7:8], i=lc[:, 6:7]):
                return e.reciprocal(o, i)
            pr.op("V", em_rc, reads=[L(6)], writes=[L(7)])
            xnf = s_xnf[:, 0:D]
            ts("V", xnf, xt, lc[:, 1:2], None, ALU.subtract, ALU.bypass,
               reads=[kx, L(1)], writes=[("xnf",)])
            ts("V", s_xnt["hi"][:, 0:D], xnf, lc[:, 7:8], None, ALU.mult,
               ALU.bypass, reads=[("xnf",), L(7)], writes=[("xnt", "hi")])
            stt("V", s_xnt["lo"][:, 0:D], xnf, lc[:, 7:8],
                s_xnt["hi"][:, 0:D], ALU.mult, ALU.subtract,
                reads=[("xnf",), L(7), ("xnt", "hi")], writes=[("xnt", "lo")])

        def phase_X_tr(gi, cc):
            pos = gi % 2
            for ip, p in enumerate(("hi", "lo")):
                bk = ("pb", ip)
                for ic in range(8):
                    src_ = s_xnt[p][:, ic * 128:(ic + 1) * 128]
                    dst = psb[ip][:, ic * 128:(ic + 1) * 128]
                    tr(bk, dst, src_, [("xnt", p)],
                       first=(ic == 0), close=(ic == 7), final=(ic == 7))
            cp("V", s_pre[:], psb[1][:, 0:1024],
               reads=[("pb", 1)], writes=[("pre", 0), ("pre", 1)])
            dsts = bass.AP(s_xn, pos * 256 + cc * 128,
                           [[8 * 512, 128], [512, 8], [1, 128]])
            src0 = bass.AP(psb[0], 0, [[1024, 128], [128, 8], [1, 128]])
            src1 = bass.AP(s_pre, 0, [[1024, 128], [128, 8], [1, 128]])
            tt("V", dsts, src0, src1, ALU.add,
               reads=[("pb", 0), ("pb", 1), ("pre", 0), ("pre", 1)],
               writes=[("xn", pos, cc)])

        def xblock(gi, cc):
            phase_X_ln(gi, cc)
            phase_X_tr(gi, cc)

        def panel_load(nm, ic, hf):
            b = pcnt[0] % 4
            pcnt[0] += 1
            pr.dma("Q0", s_wbuf[b][:],
                   wt_d[nm][ic * 128:(ic + 1) * 128, hf * 512:(hf + 1) * 512],
                   reads=[("wbuf", b)], writes=[("wbuf", b)])
            return b

        def phase_P_qk(gi):
            pos = gi % 2
            for nm, okey in (("wq", "q"), ("wk", "k")):
                for ic in range(8):
                    for hf in range(2):
                        b = panel_load(nm, ic, hf)
                        for mi in range(4):
                            m = hf * 4 + mi
                            bank = m // 2
                            bk = ("ps", bank)
                            lh = s_wbuf[b][:, mi * 128:(mi + 1) * 128]
                            rh = s_xn[:, ic * 512 + pos * 256:
                                      ic * 512 + pos * 256 + 256]
                            out = ps[bank][:, (m % 2) * 256:(m % 2 + 1) * 256]
                            rds = ([("wbuf", b)] + xnk(pos)) if mi % 2 == 0 else []
                            mm(bk, out, lh, rh, rds,
                               start=(ic == 0 and m % 2 == 0),
                               stop=(ic == 7 and m % 2 == 1),
                               close=(m % 2 == 1),
                               final=(ic == 7 and m % 2 == 1))
                for bank in range(4):
                    bk = ("ps", bank)
                    if okey == "q":
                        dst = s_q[:, bank * 512:(bank + 1) * 512]
                        src = ps[bank][:, 0:512]
                    else:
                        dst = bass.AP(s_k, 2 * bank * 384 + 128,
                                      [[8 * 384, 128], [384, 2], [1, 256]])
                        src = bass.AP(ps[bank], 0,
                                      [[512, 128], [256, 2], [1, 256]])
                    cp("S", dst, src, reads=[bk], writes=[(okey, bank)])

        def phase_P_v(gi):
            pos = gi % 2
            vch0 = 2 * gi
            for ic in range(8):
                for hf in range(2):
                    b = panel_load("wv", ic, hf)
                    for tc in range(2):
                        bank = tc * 2 + hf
                        lh = s_xn[:, ic * 512 + pos * 256 + tc * 128:
                                  ic * 512 + pos * 256 + (tc + 1) * 128]
                        out = ps[bank][:, 0:512]
                        mm(("ps", bank), out, lh, s_wbuf[b][:],
                           [("wbuf", b)] + xnk(pos),
                           start=(ic == 0), stop=(ic == 7), close=True,
                           final=(ic == 7))
            for tc in range(2):
                sl = (vch0 + tc) % 3
                for hf in range(2):
                    bank = tc * 2 + hf
                    bk = ("ps", bank)
                    hi = s_v["hi"][:, sl * D + hf * 512:sl * D + (hf + 1) * 512]
                    lo = s_v["lo"][:, sl * D + hf * 512:sl * D + (hf + 1) * 512]
                    cp("S", hi, ps[bank][:, 0:512], reads=[bk],
                       writes=[("v", "hi", sl, hf)])
                    stt("V", lo, ps[bank][:, 0:512], 1.0, hi, ALU.mult,
                        ALU.subtract, reads=[bk, ("v", "hi", sl, hf)],
                        writes=[("v", "lo", sl, hf)])

        def phase_A(gi):
            sq, g = groups[gi]
            for cc in range(2):
                vch = 2 * gi + cc
                seq_start = (g == 0 and cc == 0)
                msk_s = s_msk0 if seq_start else s_msk
                mkey = ("msk0",) if seq_start else ("msk",)

                def vap(p, h, blk, vch=vch):
                    sl = (vch - 1 + blk) % 3
                    c0 = h * HD
                    return (s_v[p][:, sl * D + c0:sl * D + c0 + HD],
                            ("v", p, sl, c0 // 512))

                def s_wave(w, cc=cc, msk_s=msk_s, mkey=mkey):
                    sbank = 2 + (w % 4)
                    sbk = ("ps", sbank)
                    for hh in range(2):
                        h = 2 * w + hh
                        pb = 64 * (h % 2)
                        m = h // 2
                        tp = (pb, 0) if pb else None
                        qh = s_q[pb:pb + 64, m * 256 + cc * 128:
                                 m * 256 + (cc + 1) * 128]
                        kh = s_k[pb:pb + 64, m * 384 + cc * 128:
                                 m * 384 + cc * 128 + 256]
                        omsk = ps[sbank][:, hh * 256:hh * 256 + 256]
                        rds = [("q", m // 2), ("k", m // 2)]
                        if cc == 0:
                            rds.append(("kcar",))
                        mm(sbk, omsk, qh, kh, rds, start=(hh == 0),
                           stop=False, close=False, final=False, tp=tp)
                        mm(sbk, omsk, s_cidr[:], msk_s[:],
                           [mkey, ("cidr",)], start=False, stop=(hh == 1),
                           close=True, final=(hh == 1))

                def head_work(w, cc=cc, vap=vap):
                    sbank = 2 + (w % 4)
                    sbk = ("ps", sbank)
                    for hh in range(2):
                        h = 2 * w + hh
                        slot = h % 2
                        act(s_P0[:, slot * 256:(slot + 1) * 256],
                            ps[sbank][:, hh * 256:hh * 256 + 256],
                            AF.Exp, s_n12[:, 0:1], 0.125,
                            s_sums[:, h:h + 1],
                            reads=[sbk, ("n12",)],
                            writes=[("P0", slot), ("sums", h)])
                    def em_rcp(e, o=s_rcp[:, 2 * w:2 * w + 2],
                               i=s_sums[:, 2 * w:2 * w + 2]):
                        return e.reciprocal(o, i)
                    pr.op("V", em_rcp,
                          reads=[("sums", 2 * w), ("sums", 2 * w + 1)],
                          writes=[("rcp", w)])
                    for hh in range(2):
                        h = 2 * w + hh
                        slot = h % 2
                        p0 = s_P0[:, slot * 256:(slot + 1) * 256]
                        ph = s_Ph[:, slot * 256:(slot + 1) * 256]
                        pl = s_Pl[:, slot * 256:(slot + 1) * 256]
                        rc = s_rcp[:, h:h + 1]
                        ts("V", ph, p0, rc, None, ALU.mult, ALU.bypass,
                           reads=[("P0", slot), ("rcp", w)],
                           writes=[("Ph", slot)])
                        stt("V", pl, p0, rc, ph, ALU.mult, ALU.subtract,
                            reads=[("P0", slot), ("rcp", w), ("Ph", slot)],
                            writes=[("Pl", slot)])
                    for hh in range(2):
                        h = 2 * w + hh
                        slot = h % 2
                        ptk = ("pb", slot)
                        for idx in range(4):
                            ti, blk = divmod(idx, 2)
                            sap = (s_Ph if ti == 0 else s_Pl)[
                                :, slot * 256 + blk * 128:
                                slot * 256 + (blk + 1) * 128]
                            dst = psb[slot][:, idx * 128:(idx + 1) * 128]
                            tr(ptk, dst, sap,
                               [("Ph", slot), ("Pl", slot)],
                               first=(idx == 0), close=(idx == 3),
                               final=(idx == 3))
                    for hh in range(2):
                        h = 2 * w + hh
                        slot = h % 2
                        ptk = ("pb", slot)
                        dpt = s_PT[:, slot * 512:(slot + 1) * 512]
                        cp("V", dpt, psb[slot][:, 0:512], reads=[ptk],
                           writes=[("PT", slot)])
                    for hh in range(2):
                        h = 2 * w + hh
                        slot = h % 2
                        obank = w // 4
                        obk = ("ps", obank)
                        pb = 64 * (h % 2)
                        qq = w % 4
                        oout = ps[obank][pb:pb + 64, qq * 128:(qq + 1) * 128]
                        otp = (0, pb) if pb else None
                        seqm = []
                        for blk in range(2):
                            vh, vkh = vap("hi", h, blk)
                            vl, vkl = vap("lo", h, blk)
                            pthi = s_PT[:, slot * 512 + blk * 128:
                                        slot * 512 + (blk + 1) * 128]
                            ptlo = s_PT[:, slot * 512 + 256 + blk * 128:
                                        slot * 512 + 256 + (blk + 1) * 128]
                            seqm.append((vh, pthi, [vkh, ("PT", slot)]))
                            seqm.append((vh, ptlo, []))
                            seqm.append((vl, pthi, [vkl]))
                        for si, (lh, rr, rds) in enumerate(seqm):
                            mm(obk, oout, lh, rr, rds,
                               start=(si == 0), stop=(si == 5),
                               close=(si == 5),
                               final=(w % 4 == 3 and hh == 1 and si == 5),
                               tp=otp)

                for w0 in range(3):
                    s_wave(w0)
                for w in range(NH // 2):
                    if w + 3 < NH // 2:
                        s_wave(w + 3)
                    head_work(w)
                for ob in range(2):
                    obk = ("ps", ob)
                    dsts = bass.AP(s_O, ob * 4 * GRP + cc * 128,
                                   [[8 * GRP, 128], [GRP, 4], [1, 128]])
                    srcs = bass.AP(ps[ob], 0, [[512, 128], [128, 4], [1, 128]])
                    cp("S", dsts, srcs, reads=[obk], writes=[("O", ob, cc)])
            dstc = bass.AP(s_k, 0, [[8 * 384, 128], [384, 8], [1, 128]])
            srcc = bass.AP(s_k, 256, [[8 * 384, 128], [384, 8], [1, 128]])
            pr.dma("Q1", dstc, srcc,
                   reads=[("k", b) for b in range(4)] + [("kcar",)],
                   writes=[("kcar",)])

        def phase_Wo(gi):
            gi2 = 2 * gi
            for ic in range(8):
                for hf in range(2):
                    b = panel_load("wo", ic, hf)
                    for tc in range(2):
                        bank = tc * 2 + hf
                        lh = s_O[:, ic * 256 + tc * 128:ic * 256 + (tc + 1) * 128]
                        mm(("ps", bank), ps[bank][:, 0:512], lh, s_wbuf[b][:],
                           [("wbuf", b), ("O", ic // 4, tc)],
                           start=(ic == 0), stop=(ic == 7), close=True,
                           final=(ic == 7))
            for tc in range(2):
                slot = (gi2 + tc) % 5
                kx = ("xc", slot)
                for hf in range(2):
                    bank = tc * 2 + hf
                    bk = ("ps", bank)
                    xap = s_xc[:, slot * D + hf * 512:slot * D + (hf + 1) * 512]
                    tt("V", xap, ps[bank][:, 0:512], xap, ALU.add,
                       reads=[bk, kx], writes=[kx])

        def phase_F0(pj):
            def strip_dma(idx):
                if idx >= 64:
                    return
                m, h = divmod(idx, 2)
                b = idx % 6
                assert ("d0sd", m) in pr.bufs
                pr.dma("Q0" if idx % 2 == 0 else "Q1", s_d0b[b][:],
                       d0s_d[m * 128:(m + 1) * 128, h * 512:(h + 1) * 512],
                       reads=[("d0b", b), ("d0sd", m)], writes=[("d0b", b)])
            for idx in range(5):
                strip_dma(idx)
            for m in range(32):
                drain(2)
                bank = 4 + (m % 2)
                bk = ("ps", bank)
                out = ps[bank][:, 0:512]
                for ic in range(8):
                    h = ic // 4
                    b = (m * 2 + h) % 6
                    lh = s_d0b[b][:, (ic % 4) * 128:(ic % 4 + 1) * 128]
                    rh = s_xn[:, ic * 512:(ic + 1) * 512]
                    rds = [("d0b", b)] + (xn_all if ic == 0 else [])
                    mm(bk, out, lh, rh, rds, start=(ic == 0), stop=(ic == 7),
                       close=(ic % 4 == 3), final=(ic == 7))
                    if ic % 4 == 3:
                        strip_dma(m * 2 + h + 5)
                act(s_a0[:, m * 512:(m + 1) * 512], out, AF.Relu, 0.0,
                    s_S0[:, 0:1], None, reads=[bk, ("S0",)],
                    writes=[("a0", m)])

        def g1_mms(m2):
            def strip_dma(idx):
                if idx >= 256:
                    return
                m2_, e_ = divmod(idx, 8)
                b = idx % 6
                assert ("w1pmd", m2_, e_ // 4) in pr.bufs
                pr.dma("Q0" if idx % 2 == 0 else "Q1", s_k1b[b][:],
                       w1pm_d[m2_ * 128:(m2_ + 1) * 128,
                              e_ * 512:(e_ + 1) * 512],
                       reads=[("k1b", b), ("w1pmd", m2_, e_ // 4)],
                       writes=[("k1b", b)])
            drain(3)
            if m2 == 0:
                for idx in range(5):
                    strip_dma(idx)
            sl = m2 % 2
            assert ("w2pmd", m2) in pr.bufs
            pr.dma("Q0", s_w2b[sl][:], w2pm_d[m2 * 128:(m2 + 1) * 128, :],
                   reads=[("w2b", sl), ("w2pmd", m2)], writes=[("w2b", sl)])
            bank = m2 % 2
            bk = ("ps", bank)
            out = ps[bank][:, 0:512]
            for kc in range(32):
                e = kc // 4
                b = (m2 * 8 + e) % 6
                lh = s_k1b[b][:, (kc % 4) * 128:(kc % 4 + 1) * 128]
                rh = s_a0[:, kc * 512:(kc + 1) * 512]
                rds = [("k1b", b)] + (a0keys if kc == 0 else [])
                mm(bk, out, lh, rh, rds, start=(kc == 0), stop=(kc == 31),
                   close=(kc % 4 == 3), final=(kc == 31))
                if kc % 4 == 3:
                    strip_dma(m2 * 8 + e + 5)
            act(s_a1[:, sl * 512:(sl + 1) * 512], out, AF.Relu, 0.0,
                s_S1[:, 0:1], None, reads=[bk, ("S1",)], writes=[("a1", sl)])

        def g2_batch(pj, bt):
            batch = range(2 * bt, 2 * bt + 2)
            akeys = [("a1", m2 % 2) for m2 in batch]
            wkeys = [("w2b", m2 % 2) for m2 in batch]
            for ti in range(8):
                tc, oh = divmod(ti, 2)
                bank = 2 + (ti % 4)
                gk = ("ps", bank)
                for j, m2 in enumerate(batch):
                    sl = m2 % 2
                    mm(gk, ps[bank][:, 0:512],
                       s_a1[:, sl * 512 + tc * 128:sl * 512 + (tc + 1) * 128],
                       s_w2b[sl][:, oh * 512:(oh + 1) * 512],
                       (akeys + wkeys) if j == 0 else [],
                       start=(j == 0), stop=(j == 1), close=(j == 1),
                       final=True)
                slot = (4 * pj + tc) % 5
                xap = s_xc[:, slot * D + oh * 512:slot * D + (oh + 1) * 512]
                stt("V", xap, ps[bank][:, 0:512], s_S2[:, 0:1], xap,
                    ALU.mult, ALU.add, reads=[gk, ("S2",), ("xc", slot)],
                    writes=[("xc", slot)])

        def phase_F12(pj):
            gi0 = 2 * pj
            nxtA = gi0 + 2 if gi0 + 2 < len(groups) else None

            def ystore(gi, cc):
                sq, g = groups[gi]
                slot = (2 * gi + cc) % 5
                r0 = sq * SEQ + g * GRP + cc * 128
                pr.dma("Q1", y_d[r0:r0 + 128, :],
                       s_xc[:, slot * D:(slot + 1) * D],
                       reads=[("xc", slot)], writes=[("ydone", gi, cc)])

            for bt in range(16):
                for m2 in range(2 * bt, 2 * bt + 2):
                    g1_mms(m2)
                g2_batch(pj, bt)
                if nxtA is not None:
                    if bt == 2:
                        phase_X_ln(nxtA, 0)
                    elif bt == 4:
                        phase_X_tr(nxtA, 0)
            ystore(gi0, 0)
            if nxtA is not None:
                phase_X_ln(nxtA, 1)
                phase_X_tr(nxtA, 1)
            ystore(gi0, 1)
            ystore(gi0 + 1, 0)
            ystore(gi0 + 1, 1)

        pre_consts()
        build_pre_tasks()
        for cc in range(2):
            xblock(0, cc)
            drain(8)
        for pj in range(npairs):
            gia, gib = 2 * pj, 2 * pj + 1
            phase_P_qk(gia)
            drain(12)
            phase_P_v(gia)
            drain(8)
            phase_A(gia)
            drain(12)
            phase_X_ln(gib, 0)
            phase_X_tr(gib, 0)
            drain(8)
            phase_Wo(gia)
            drain(8)
            phase_X_ln(gib, 1)
            phase_X_tr(gib, 1)
            drain(8)
            phase_P_qk(gib)
            drain(12)
            phase_P_v(gib)
            drain(8)
            phase_A(gib)
            drain(8)
            phase_Wo(gib)
            drain(8)
            phase_F0(pj)
            phase_F12(pj)
        assert not pre_tasks, len(pre_tasks)

        pr.finalize_queues()
        pr.render()
    return nc




_CACHE = {}


def _consts():
    i = np.arange(128)[:, None]
    j = np.arange(256)[None, :]
    band = (j >= i + 1) & (j <= i + 128)
    valid0 = band & (j >= 128)
    cmsk = np.where(band, 0.0, MASK_RAW).astype(np.float32)
    cmsk0 = np.where(valid0, 0.0, MASK_RAW).astype(np.float32)
    cid = np.eye(128, dtype=ml_dtypes.bfloat16)
    cidr = np.eye(128, dtype=np.float32)
    cn12 = np.full((128, 1), EXP_SHIFT, dtype=np.float32)
    return cmsk, cmsk0, cid, cidr, cn12


def make_inputs(L0_attn_x, L0_attn_Wq, L0_attn_Wk, L0_attn_Wv, L0_attn_Wo,
                L0_ffn_k0, L0_ffn_k1, L0_ffn_k2, L0_ffn_s0, L0_ffn_s1,
                L0_ffn_s2):
    x = np.ascontiguousarray(np.asarray(L0_attn_x, dtype=np.float32))
    wqt = np.ascontiguousarray(np.asarray(L0_attn_Wq, np.float32).T)
    wkt = np.ascontiguousarray(np.asarray(L0_attn_Wk, np.float32).T)
    wvt = np.ascontiguousarray(np.asarray(L0_attn_Wv, np.float32).T)
    wot = np.ascontiguousarray(np.asarray(L0_attn_Wo, np.float32).T)
    k0 = np.asarray(L0_ffn_k0).astype(np.uint8)
    k0s = np.ascontiguousarray(
        k0.reshape(16, 128, 32, 128).transpose(2, 1, 0, 3)).reshape(HID, 2 * D)
    k1 = np.asarray(L0_ffn_k1).astype(np.uint8)
    k1s = np.ascontiguousarray(
        k1.reshape(32, 128, 32, 128).transpose(2, 1, 0, 3)).reshape(HID, HID)
    k2 = np.ascontiguousarray(np.asarray(L0_ffn_k2).astype(np.uint8))
    scal = np.zeros((128, 4), np.float32)
    scal[:, 0] = np.float32(L0_ffn_s0)
    scal[:, 1] = np.float32(L0_ffn_s1)
    scal[:, 2] = np.float32(L0_ffn_s2)
    cmsk, cmsk0, cid, cidr, cn12 = _consts()
    czero = np.zeros((128, 8 * 384), np.float32)
    shared = dict(wqt=wqt, wkt=wkt, wvt=wvt, wot=wot, k0s=k0s, k1s=k1s, k2=k2,
                  scal=scal, cmsk=cmsk, cmsk0=cmsk0, cid=cid, cidr=cidr,
                  cn12=cn12, czero=czero)
    in_maps = []
    for c in range(8):
        m = dict(shared)
        m["x"] = np.ascontiguousarray(x[2 * c:2 * c + 2].reshape(2 * SEQ, D))
        in_maps.append(m)
    return in_maps


def kernel(L0_attn_x, L0_attn_Wq, L0_attn_Wk, L0_attn_Wv, L0_attn_Wo,
           L0_ffn_k0, L0_ffn_k1, L0_ffn_k2, L0_ffn_s0, L0_ffn_s1, L0_ffn_s2):
    in_maps = make_inputs(L0_attn_x, L0_attn_Wq, L0_attn_Wk, L0_attn_Wv,
                          L0_attn_Wo, L0_ffn_k0, L0_ffn_k1, L0_ffn_k2,
                          L0_ffn_s0, L0_ffn_s1, L0_ffn_s2)
    if "nc" not in _CACHE:
        _CACHE["nc"] = build_program()
    nc = _CACHE["nc"]
    res = run_bass_kernel_spmd(nc, in_maps, core_ids=list(range(8)))
    out = np.empty((16, SEQ, D), np.float32)
    for c in range(8):
        out[2 * c:2 * c + 2] = np.asarray(res.results[c]["y"]).reshape(2, SEQ, D)
    return out


# version 19
# speedup vs baseline: 1.5162x; 1.1045x over previous
import numpy as np
import ml_dtypes
import concourse.bass as bass
import concourse.mybir as mybir
from concourse.bass_utils import run_bass_kernel_spmd

F32 = mybir.dt.float32
F32R = mybir.dt.float32r
BF16 = mybir.dt.bfloat16
U8 = mybir.dt.uint8
AF = mybir.ActivationFunctionType
ALU = mybir.AluOpType
AX = mybir.AxisListType.X

D = 1024
NH = 16
HD = 64
CH = 128
GRP = 256
HID = 4096
SEQ = 2048
NSEQ = 2
NGRP = SEQ // GRP
MASK_RAW = -262144.0
EXP_SHIFT = -12.0

ENG = ("T", "V", "S", "Q0", "Q1")
NPOOL = {"Q0": 12, "Q1": 8, "Q2": 8}
DIMS = (["T", "V", "S", "G"]
        + [(q, k) for q in ("Q0", "Q1", "Q2") for k in range(NPOOL[q])])


class Prog:

    def __init__(self, nc):
        self.nc = nc
        self.thunks = {e: [] for e in ENG}
        self.cnt = {e: 0 for e in ("T", "V", "S", "G")}
        self.dcnt = {q: 0 for q in ("Q0", "Q1", "Q2")}
        self.vc = {e: {d: 0 for d in DIMS} for e in ENG}
        self.snap = {d: [None] for d in DIMS}
        self.bufs = {}
        self.sems = {}
        self.pool_uses = {q: [0] * NPOOL[q] for q in ("Q0", "Q1", "Q2")}
        self.pend = {}

    def make_sems(self, estack):
        for e in ("T", "V", "S", "G"):
            self.sems[e] = estack.enter_context(self.nc.semaphore("sem" + e))
        for q in ("Q0", "Q1", "Q2"):
            for k in range(NPOOL[q]):
                self.sems[(q, k)] = estack.enter_context(
                    self.nc.semaphore("sem%s_%d" % (q, k)))

    def _need_for(self, reads, writes):
        need = {}
        for key in reads:
            b = self.bufs.get(key)
            if b and b[0] is not None:
                e, i = b[0]
                if i > need.get(e, 0):
                    need[e] = i
        for key in writes:
            b = self.bufs.get(key)
            if b:
                if b[0] is not None:
                    e, i = b[0]
                    if i > need.get(e, 0):
                        need[e] = i
                for e, i in b[1].items():
                    if i > need.get(e, 0):
                        need[e] = i
        return need

    def _emit_waits(self, eng, need):
        myvc = self.vc[eng]
        for d in sorted(need, key=str):
            i = need[d]
            if myvc[d] >= i:
                continue
            if isinstance(d, tuple):
                self.thunks[eng].append(("w", self.sems[d], 16 * i))
            else:
                self.thunks[eng].append(("w", self.sems[d], i))
            sn = self.snap[d][i]
            for dd, v in sn.items():
                if v > myvc[dd]:
                    myvc[dd] = v
            if i > myvc[d]:
                myvc[d] = i

    def _marker(self, eng):
        self.cnt[eng] += 1
        i = self.cnt[eng]
        sn = dict(self.vc[eng])
        sn[eng] = i
        self.snap[eng].append(sn)
        return (eng, i)

    def _register(self, reads, writes, marker):
        for key in writes:
            self.bufs[key] = [marker, {}]
        e, i = marker
        for key in reads:
            b = self.bufs.get(key)
            if b is None:
                b = [None, {}]
                self.bufs[key] = b
            if i > b[1].get(e, 0):
                b[1][e] = i

    def op(self, eng, emit, reads=(), writes=()):
        stream = "Q1" if eng == "G" else eng
        self._emit_waits(stream, self._need_for(reads, writes))
        self.cnt[eng] += 1
        i = self.cnt[eng]
        sn = dict(self.vc[stream])
        sn[eng] = i
        self.snap[eng].append(sn)
        self.thunks[stream].append(("i", emit, self.sems[eng], 1))
        self._register(reads, writes, (eng, i))

    def gop(self, eng, emit, reads=(), gkey=None, first=False, close=False,
            final=False):
        writes = (gkey,) if first else ()
        self._emit_waits(eng, self._need_for(reads, writes))
        pend = self.pend.setdefault((eng, gkey), set())
        pend.update(reads)
        if close:
            m = self._marker(eng)
            self.thunks[eng].append(("i", emit, self.sems[eng], 1))
            self._register(tuple(pend), (gkey,) if final else (), m)
            pend.clear()
            if final:
                self.pend.pop((eng, gkey), None)
        else:
            self.thunks[eng].append(("x", emit))

    def dma(self, q, out_ap, in_ap, reads=(), writes=()):
        stream = "S" if q == "Q2" else q
        R = NPOOL[q]
        idx = self.dcnt[q]
        self.dcnt[q] = idx + 1
        k = idx % R
        d = (q, k)
        prev_uses = self.pool_uses[q][k]
        if prev_uses > 0 and self.vc[stream][d] < prev_uses:
            self.thunks[stream].append(("w", self.sems[d], 16 * prev_uses))
            sn = self.snap[d][prev_uses]
            myvc = self.vc[stream]
            for dd, v in sn.items():
                if v > myvc[dd]:
                    myvc[dd] = v
        self._emit_waits(stream, self._need_for(reads, writes))
        self.thunks[stream].append(("d", out_ap, in_ap, self.sems[d]))
        u = prev_uses + 1
        self.pool_uses[q][k] = u
        sn = dict(self.vc[stream])
        sn[d] = u
        self.snap[d].append(sn)
        self._register(reads, writes, (d, u))

    def finalize_queues(self):
        for q in ("Q0", "Q1", "Q2"):
            stream = "S" if q == "Q2" else q
            for k in range(NPOOL[q]):
                u = self.pool_uses[q][k]
                if u > 0:
                    self.thunks[stream].append(("w", self.sems[(q, k)], 16 * u))

    def render(self):
        nc = self.nc

        def runner(e):
            lst = self.thunks[e]
            def run(engine_obj):
                for t in lst:
                    tag = t[0]
                    if tag == "w":
                        engine_obj.wait_ge(t[1], t[2])
                    elif tag == "i":
                        t[1](engine_obj).then_inc(t[2], t[3])
                    elif tag == "x":
                        t[1](engine_obj)
                    else:
                        engine_obj.dma_start(out=t[1], in_=t[2]).then_inc(t[3], 16)
            return run

        with nc.Block() as block:
            block.tensor(runner("T"))
            block.vector(runner("V"))
            block.scalar(runner("S"))
            block.sync(runner("Q0"))
            block.gpsimd(runner("Q1"))



def build_program(n_seq=NSEQ, n_grp=NGRP):
    import contextlib
    nc = bass.Bass("TRN2", target_bir_lowering=False, debug=False)

    x_d = nc.dram_tensor("x", [NSEQ * SEQ, D], F32, kind="ExternalInput").ap()
    y_d = nc.dram_tensor("y", [NSEQ * SEQ, D], F32, kind="ExternalOutput").ap()
    wt_d = {}
    for nm, t in (("wq", "wqt"), ("wk", "wkt"), ("wv", "wvt"), ("wo", "wot")):
        wt_d[nm] = nc.dram_tensor(t, [D, D], F32R, kind="ExternalInput").ap()
    k0s_d = nc.dram_tensor("k0s", [HID, 2 * D], U8, kind="ExternalInput").ap()
    k1s_d = nc.dram_tensor("k1s", [HID, HID], U8, kind="ExternalInput").ap()
    k2_d = nc.dram_tensor("k2", [HID, D], U8, kind="ExternalInput").ap()
    scal_d = nc.dram_tensor("scal", [128, 4], F32, kind="ExternalInput").ap()
    cmsk_d = nc.dram_tensor("cmsk", [128, 256], F32R, kind="ExternalInput").ap()
    cmsk0_d = nc.dram_tensor("cmsk0", [128, 256], F32R,
                             kind="ExternalInput").ap()
    cid_d = nc.dram_tensor("cid", [128, 128], BF16, kind="ExternalInput").ap()
    cidr_d = nc.dram_tensor("cidr", [128, 128], F32R, kind="ExternalInput").ap()
    cn12_d = nc.dram_tensor("cn12", [128, 1], F32, kind="ExternalInput").ap()
    czero_d = nc.dram_tensor("czero", [128, 8 * 384], F32R,
                             kind="ExternalInput").ap()
    d0s_d = nc.dram_tensor("d0s", [HID, D], BF16, kind="Internal").ap()
    w1pm_d = nc.dram_tensor("w1pm", [HID, HID], BF16, kind="Internal").ap()
    w2pm_d = nc.dram_tensor("w2pm", [HID, D], BF16, kind="Internal").ap()

    estack = contextlib.ExitStack()
    with estack:
        A = nc.alloc_sbuf_tensor
        s_cid = A("s_cid", [128, 128], BF16)
        s_cidr = A("s_cidr", [128, 128], F32R)
        s_msk = A("s_msk", [128, 256], F32R)
        s_msk0 = A("s_msk0", [128, 256], F32R)
        s_scal = A("s_scal", [128, 4], F32)
        s_n12 = A("s_n12", [128, 1], F32)
        s_S0 = A("s_S0", [128, 1], F32)
        s_S1 = A("s_S1", [128, 1], F32)
        s_S2 = A("s_S2", [128, 1], F32)
        s_ln = A("s_ln", [128, 16], F32)
        s_sums = A("s_sums", [128, NH], F32)
        s_rcp = A("s_rcp", [128, NH], F32)
        s_pre = A("s_pre", [128, 1024], BF16)
        s_xc = A("s_xc", [128, 5 * D], F32)
        s_xnf = A("s_xnf", [128, D], F32)
        s_xnt = {p: A("s_xnt_" + p, [128, D], BF16) for p in ("hi", "lo")}
        s_xn = A("s_xn", [128, 8 * 512], F32R)
        s_q = A("s_q", [128, 8 * GRP], F32R)
        s_k = A("s_k", [128, 8 * 384], F32R)
        s_v = {p: A("s_v_" + p, [128, 3 * D], BF16) for p in ("hi", "lo")}
        s_O = A("s_O", [128, 8 * GRP], F32R)
        s_P0 = A("s_P0", [128, 2 * 256], F32)
        s_Ph = A("s_Ph", [128, 2 * 256], BF16)
        s_Pl = A("s_Pl", [128, 2 * 256], BF16)
        s_PT = A("s_PT", [128, 2 * 512], BF16)
        s_wbuf = [A("s_wbuf%d" % b, [128, 512], F32R) for b in range(4)]
        s_d0b = [A("s_d0b%d" % b, [128, 512], F32R) for b in range(6)]
        s_k1b = [A("s_k1b%d" % b, [128, 512], F32R) for b in range(6)]
        s_w2b = [A("s_w2b%d" % b, [128, D], F32R) for b in range(2)]
        s_a0 = A("s_a0", [128, 32 * 512], F32R)
        s_a1 = A("s_a1", [128, 2 * 512], F32R)

        ps = [nc.alloc_psum_tensor("ps%d" % i, [128, 512], F32) for i in range(6)]
        psb = [nc.alloc_psum_tensor("pb%d" % i, [128, 1024], BF16) for i in range(2)]

        pr = Prog(nc)
        pr.make_sems(estack)

        def ts(eng, out_ap, in_ap, s1, s2, op0, op1, reads, writes):
            def emit(e, o=out_ap, i=in_ap, s1=s1, s2=s2, op0=op0, op1=op1):
                return e.tensor_scalar(out=o, in0=i, scalar1=s1, scalar2=s2,
                                       op0=op0, op1=op1)
            pr.op(eng, emit, reads=reads, writes=writes)

        def stt(eng, out_ap, in0_ap, sc, in1_ap, op0, op1, reads, writes):
            def emit(e, o=out_ap, i0=in0_ap, sc=sc, i1=in1_ap, op0=op0, op1=op1):
                return e.scalar_tensor_tensor(out=o, in0=i0, scalar=sc, in1=i1,
                                              op0=op0, op1=op1)
            pr.op(eng, emit, reads=reads, writes=writes)

        def tt(eng, out_ap, a_ap, b_ap, alu, reads, writes):
            def emit(e, o=out_ap, a=a_ap, b=b_ap, alu=alu):
                return e.tensor_tensor(out=o, in0=a, in1=b, op=alu)
            pr.op(eng, emit, reads=reads, writes=writes)

        def cp(eng, out_ap, in_ap, reads, writes):
            if eng == "S":
                def emit(e, o=out_ap, i=in_ap):
                    return e.copy(out=o, in_=i)
            else:
                def emit(e, o=out_ap, i=in_ap):
                    return e.tensor_copy(out=o, in_=i)
            pr.op(eng, emit, reads=reads, writes=writes)

        def memset(eng, ap, val, reads, writes):
            def emit(e, a=ap, v=val):
                return e.memset(a, v)
            pr.op(eng, emit, reads=reads, writes=writes)

        def act(out_ap, in_ap, func, bias, scale, accum, reads, writes):
            def emit(e, o=out_ap, i=in_ap, f=func, b=bias, s=scale, a=accum):
                return e.activation(out=o, in_=i, func=f, bias=b, scale=s,
                                    accum_out=a)
            pr.op("S", emit, reads=reads, writes=writes)

        def mm(gkey, out_ap, lhsT, rhs, reads, start, stop, close, final, tp=None):
            def emit(e, o=out_ap, l=lhsT, r=rhs, st=start, sp=stop, tp=tp):
                return e.matmul(o, l, r, start=st, stop=sp, tile_position=tp)
            pr.gop("T", emit, reads=reads, gkey=gkey, first=start, close=close,
                   final=final)

        def tr(gkey, out_ap, in_ap, reads, first, close, final):
            def emit(e, o=out_ap, i=in_ap):
                return e.transpose(o, i, s_cid[:])
            pr.gop("T", emit, reads=tuple(reads) + (("cid",),), gkey=gkey,
                   first=first, close=close, final=final)

        groups = [(sq, g) for sq in range(n_seq) for g in range(n_grp)]
        assert len(groups) % 2 == 0
        npairs = len(groups) // 2

        def xnk(pos):
            return [("xn", pos, 0), ("xn", pos, 1)]
        xn_all = xnk(0) + xnk(1)
        a0keys = [("a0", m) for m in range(32)]
        r1024 = 1.0 / 1024.0

        pcnt = [0]

        def pre_consts():
            pr.dma("Q0", s_cid[:], cid_d, writes=[("cid",)])
            pr.dma("Q0", s_cidr[:], cidr_d, writes=[("cidr",)])
            pr.dma("Q0", s_msk[:], cmsk_d, writes=[("msk",)])
            pr.dma("Q0", s_msk0[:], cmsk0_d, writes=[("msk0",)])
            pr.dma("Q0", s_scal[:], scal_d, writes=[("scal",)])
            pr.dma("Q0", s_n12[:], cn12_d, writes=[("n12",)])
            ts("V", s_S0[:], s_scal[:, 0:1], 4.0, None, ALU.mult, ALU.bypass,
               reads=[("scal",)], writes=[("S0",)])
            ts("V", s_S1[:], s_scal[:, 1:2], 2.0, None, ALU.mult, ALU.bypass,
               reads=[("scal",)], writes=[("S1",)])
            ts("V", s_S2[:], s_scal[:, 2:3], 1.0, None, ALU.mult, ALU.bypass,
               reads=[("scal",)], writes=[("S2",)])
            pr.dma("Q0", s_k[:], czero_d, reads=[],
                   writes=[("kcar",)] + [("k", b) for b in range(4)])
            for p in ("hi", "lo"):
                memset("V", s_v[p][:, 2 * D:3 * D], 0.0, reads=[],
                       writes=[("v", p, 2, 0), ("v", p, 2, 1)])

        pre_tasks = []

        def task_d0(m, qh):
            def run():
                kt = ("pre", 0)
                kb = ("pre", 1)
                bt = s_pre[:, 0:512]
                bb = s_pre[:, 512:1024]
                r0 = m * 128
                pr.dma("Q1", bt, k0s_d[r0:r0 + 128, qh * 512:qh * 512 + 512],
                       reads=[kt], writes=[kt])
                pr.dma("Q1", bb, k0s_d[r0:r0 + 128,
                                       1024 + qh * 512:1024 + qh * 512 + 512],
                       reads=[kb], writes=[kb])
                tt("V", bt, bt, bb, ALU.subtract, reads=[kt, kb], writes=[kt])
                pr.dma("Q0", d0s_d[r0:r0 + 128, qh * 512:(qh + 1) * 512], bt,
                       reads=[kt], writes=[("d0sd", m)])
            return run

        def task_pm(src_d, dst_d, rc, c0, c1, wkey, eng):
            def run():
                kk = [("pre", 0), ("pre", 1)]
                buf = s_pre[:, 0:1024]
                pr.dma("Q1", buf, src_d[rc:rc + 128, c0:c1], reads=kk,
                       writes=kk)
                if eng == "S":
                    act(buf, buf, AF.Copy, -1.0, 2.0, None, reads=kk,
                        writes=kk)
                else:
                    ts("V", buf, buf, 2.0, -1.0, ALU.mult, ALU.add,
                       reads=kk, writes=kk)
                pr.dma("Q0", dst_d[rc:rc + 128, c0:c1], buf, reads=kk,
                       writes=[wkey])
            return run

        def build_pre_tasks():
            for m in range(32):
                for qh in range(2):
                    pre_tasks.append(task_d0(m, qh))
            for oc in range(32):
                pre_tasks.append(task_pm(k2_d, w2pm_d, oc * 128, 0, D,
                                         ("w2pmd", oc), ("V", "S")[oc % 2]))
            for m2 in range(32):
                for qf in range(4):
                    pre_tasks.append(task_pm(
                        k1s_d, w1pm_d, m2 * 128, qf * 1024, (qf + 1) * 1024,
                        ("w1pmd", m2, qf // 2), ("V", "S")[qf % 2]))

        def drain(n):
            for _ in range(n):
                if pre_tasks:
                    pre_tasks.pop(0)()

        def phase_X_ln(gi, cc):
            sq, g = groups[gi]
            gch = 2 * gi + cc
            slot = gch % 5
            xt = s_xc[:, slot * D:(slot + 1) * D]
            kx = ("xc", slot)
            r0 = sq * SEQ + g * GRP + cc * 128
            L = lambda i, cc=cc: ("ln", cc, i)
            lc = s_ln[:, cc * 8:(cc + 1) * 8]
            pr.dma("Q0", xt, x_d[r0:r0 + 128, :], reads=[kx], writes=[kx])
            def em_rs(e, o=lc[:, 0:1], i=xt):
                return e.reduce_sum(o, i, AX)
            pr.op("V", em_rs, reads=[kx], writes=[L(0)])
            ts("V", lc[:, 1:2], lc[:, 0:1], r1024, None, ALU.mult,
               ALU.bypass, reads=[L(0)], writes=[L(1)])
            act(s_xnt["hi"][:, 0:D], xt, AF.Square, 0.0, 1.0, lc[:, 2:3],
                reads=[kx], writes=[("xnt", "hi"), L(2)])
            ts("V", lc[:, 3:4], lc[:, 2:3], r1024, None, ALU.mult,
               ALU.bypass, reads=[L(2)], writes=[L(3)])
            tt("V", lc[:, 4:5], lc[:, 1:2], lc[:, 1:2], ALU.mult,
               reads=[L(1)], writes=[L(4)])
            tt("V", lc[:, 5:6], lc[:, 3:4], lc[:, 4:5], ALU.subtract,
               reads=[L(3), L(4)], writes=[L(5)])
            ts("V", lc[:, 5:6], lc[:, 5:6], 1.0, 1e-5, ALU.mult, ALU.add,
               reads=[L(5)], writes=[L(5)])
            def em_sq(e, o=lc[:, 6:7], i=lc[:, 5:6]):
                return e.sqrt(o, i)
            pr.op("S", em_sq, reads=[L(5)], writes=[L(6)])
            def em_rc(e, o=lc[:, 7:8], i=lc[:, 6:7]):
                return e.reciprocal(o, i)
            pr.op("V", em_rc, reads=[L(6)], writes=[L(7)])
            xnf = s_xnf[:, 0:D]
            ts("V", xnf, xt, lc[:, 1:2], None, ALU.subtract, ALU.bypass,
               reads=[kx, L(1)], writes=[("xnf",)])
            ts("V", s_xnt["hi"][:, 0:D], xnf, lc[:, 7:8], None, ALU.mult,
               ALU.bypass, reads=[("xnf",), L(7)], writes=[("xnt", "hi")])
            stt("V", s_xnt["lo"][:, 0:D], xnf, lc[:, 7:8],
                s_xnt["hi"][:, 0:D], ALU.mult, ALU.subtract,
                reads=[("xnf",), L(7), ("xnt", "hi")], writes=[("xnt", "lo")])

        def phase_X_tr(gi, cc):
            pos = gi % 2
            for ip, p in enumerate(("hi", "lo")):
                bk = ("pb", ip)
                for ic in range(8):
                    src_ = s_xnt[p][:, ic * 128:(ic + 1) * 128]
                    dst = psb[ip][:, ic * 128:(ic + 1) * 128]
                    tr(bk, dst, src_, [("xnt", p)],
                       first=(ic == 0), close=(ic == 7), final=(ic == 7))
            cp("V", s_pre[:], psb[1][:, 0:1024],
               reads=[("pb", 1)], writes=[("pre", 0), ("pre", 1)])
            dsts = bass.AP(s_xn, pos * 256 + cc * 128,
                           [[8 * 512, 128], [512, 8], [1, 128]])
            src0 = bass.AP(psb[0], 0, [[1024, 128], [128, 8], [1, 128]])
            src1 = bass.AP(s_pre, 0, [[1024, 128], [128, 8], [1, 128]])
            tt("V", dsts, src0, src1, ALU.add,
               reads=[("pb", 0), ("pb", 1), ("pre", 0), ("pre", 1)],
               writes=[("xn", pos, cc)])

        def xblock(gi, cc):
            phase_X_ln(gi, cc)
            phase_X_tr(gi, cc)

        def panel_load(nm, ic, hf):
            b = pcnt[0] % 4
            pcnt[0] += 1
            pr.dma("Q0", s_wbuf[b][:],
                   wt_d[nm][ic * 128:(ic + 1) * 128, hf * 512:(hf + 1) * 512],
                   reads=[("wbuf", b)], writes=[("wbuf", b)])
            return b

        def phase_P_qk(gi):
            pos = gi % 2
            for nm, okey in (("wq", "q"), ("wk", "k")):
                for ic in range(8):
                    for hf in range(2):
                        b = panel_load(nm, ic, hf)
                        for mi in range(4):
                            m = hf * 4 + mi
                            bank = m // 2
                            bk = ("ps", bank)
                            lh = s_wbuf[b][:, mi * 128:(mi + 1) * 128]
                            rh = s_xn[:, ic * 512 + pos * 256:
                                      ic * 512 + pos * 256 + 256]
                            out = ps[bank][:, (m % 2) * 256:(m % 2 + 1) * 256]
                            rds = ([("wbuf", b)] + xnk(pos)) if mi % 2 == 0 else []
                            mm(bk, out, lh, rh, rds,
                               start=(ic == 0 and m % 2 == 0),
                               stop=(ic == 7 and m % 2 == 1),
                               close=(m % 2 == 1),
                               final=(ic == 7 and m % 2 == 1))
                for bank in range(4):
                    bk = ("ps", bank)
                    if okey == "q":
                        dst = s_q[:, bank * 512:(bank + 1) * 512]
                        src = ps[bank][:, 0:512]
                    else:
                        dst = bass.AP(s_k, 2 * bank * 384 + 128,
                                      [[8 * 384, 128], [384, 2], [1, 256]])
                        src = bass.AP(ps[bank], 0,
                                      [[512, 128], [256, 2], [1, 256]])
                    cp("S", dst, src, reads=[bk], writes=[(okey, bank)])

        def phase_P_v(gi):
            pos = gi % 2
            vch0 = 2 * gi
            for ic in range(8):
                for hf in range(2):
                    b = panel_load("wv", ic, hf)
                    for tc in range(2):
                        bank = tc * 2 + hf
                        lh = s_xn[:, ic * 512 + pos * 256 + tc * 128:
                                  ic * 512 + pos * 256 + (tc + 1) * 128]
                        out = ps[bank][:, 0:512]
                        mm(("ps", bank), out, lh, s_wbuf[b][:],
                           [("wbuf", b)] + xnk(pos),
                           start=(ic == 0), stop=(ic == 7), close=True,
                           final=(ic == 7))
            for tc in range(2):
                sl = (vch0 + tc) % 3
                for hf in range(2):
                    bank = tc * 2 + hf
                    bk = ("ps", bank)
                    hi = s_v["hi"][:, sl * D + hf * 512:sl * D + (hf + 1) * 512]
                    lo = s_v["lo"][:, sl * D + hf * 512:sl * D + (hf + 1) * 512]
                    cp("S", hi, ps[bank][:, 0:512], reads=[bk],
                       writes=[("v", "hi", sl, hf)])
                    stt("V", lo, ps[bank][:, 0:512], 1.0, hi, ALU.mult,
                        ALU.subtract, reads=[bk, ("v", "hi", sl, hf)],
                        writes=[("v", "lo", sl, hf)])

        def phase_A(gi):
            sq, g = groups[gi]
            for cc in range(2):
                vch = 2 * gi + cc
                seq_start = (g == 0 and cc == 0)
                msk_s = s_msk0 if seq_start else s_msk
                mkey = ("msk0",) if seq_start else ("msk",)

                def vap(p, h, blk, vch=vch):
                    sl = (vch - 1 + blk) % 3
                    c0 = h * HD
                    return (s_v[p][:, sl * D + c0:sl * D + c0 + HD],
                            ("v", p, sl, c0 // 512))

                def s_wave(w, cc=cc, msk_s=msk_s, mkey=mkey):
                    sbank = 2 + (w % 4)
                    sbk = ("ps", sbank)
                    for hh in range(2):
                        h = 2 * w + hh
                        pb = 64 * (h % 2)
                        m = h // 2
                        tp = (pb, 0) if pb else None
                        qh = s_q[pb:pb + 64, m * 256 + cc * 128:
                                 m * 256 + (cc + 1) * 128]
                        kh = s_k[pb:pb + 64, m * 384 + cc * 128:
                                 m * 384 + cc * 128 + 256]
                        omsk = ps[sbank][:, hh * 256:hh * 256 + 256]
                        rds = [("q", m // 2), ("k", m // 2)]
                        if cc == 0:
                            rds.append(("kcar",))
                        mm(sbk, omsk, qh, kh, rds, start=(hh == 0),
                           stop=False, close=False, final=False, tp=tp)
                        mm(sbk, omsk, s_cidr[:], msk_s[:],
                           [mkey, ("cidr",)], start=False, stop=(hh == 1),
                           close=True, final=(hh == 1))

                def head_work(w, cc=cc, vap=vap):
                    sbank = 2 + (w % 4)
                    sbk = ("ps", sbank)
                    for hh in range(2):
                        h = 2 * w + hh
                        slot = h % 2
                        act(s_P0[:, slot * 256:(slot + 1) * 256],
                            ps[sbank][:, hh * 256:hh * 256 + 256],
                            AF.Exp, s_n12[:, 0:1], 0.125,
                            s_sums[:, h:h + 1],
                            reads=[sbk, ("n12",)],
                            writes=[("P0", slot), ("sums", h)])
                    def em_rcp(e, o=s_rcp[:, 2 * w:2 * w + 2],
                               i=s_sums[:, 2 * w:2 * w + 2]):
                        return e.reciprocal(o, i)
                    pr.op("V", em_rcp,
                          reads=[("sums", 2 * w), ("sums", 2 * w + 1)],
                          writes=[("rcp", w)])
                    for hh in range(2):
                        h = 2 * w + hh
                        slot = h % 2
                        p0 = s_P0[:, slot * 256:(slot + 1) * 256]
                        ph = s_Ph[:, slot * 256:(slot + 1) * 256]
                        pl = s_Pl[:, slot * 256:(slot + 1) * 256]
                        rc = s_rcp[:, h:h + 1]
                        ts("V", ph, p0, rc, None, ALU.mult, ALU.bypass,
                           reads=[("P0", slot), ("rcp", w)],
                           writes=[("Ph", slot)])
                        stt("V", pl, p0, rc, ph, ALU.mult, ALU.subtract,
                            reads=[("P0", slot), ("rcp", w), ("Ph", slot)],
                            writes=[("Pl", slot)])
                    for hh in range(2):
                        h = 2 * w + hh
                        slot = h % 2
                        ptk = ("pb", slot)
                        for idx in range(4):
                            ti, blk = divmod(idx, 2)
                            sap = (s_Ph if ti == 0 else s_Pl)[
                                :, slot * 256 + blk * 128:
                                slot * 256 + (blk + 1) * 128]
                            dst = psb[slot][:, idx * 128:(idx + 1) * 128]
                            tr(ptk, dst, sap,
                               [("Ph", slot), ("Pl", slot)],
                               first=(idx == 0), close=(idx == 3),
                               final=(idx == 3))
                    for hh in range(2):
                        h = 2 * w + hh
                        slot = h % 2
                        ptk = ("pb", slot)
                        dpt = s_PT[:, slot * 512:(slot + 1) * 512]
                        cp("V", dpt, psb[slot][:, 0:512], reads=[ptk],
                           writes=[("PT", slot)])
                    for hh in range(2):
                        h = 2 * w + hh
                        slot = h % 2
                        obank = w // 4
                        obk = ("ps", obank)
                        pb = 64 * (h % 2)
                        qq = w % 4
                        oout = ps[obank][pb:pb + 64, qq * 128:(qq + 1) * 128]
                        otp = (0, pb) if pb else None
                        seqm = []
                        for blk in range(2):
                            vh, vkh = vap("hi", h, blk)
                            vl, vkl = vap("lo", h, blk)
                            pthi = s_PT[:, slot * 512 + blk * 128:
                                        slot * 512 + (blk + 1) * 128]
                            ptlo = s_PT[:, slot * 512 + 256 + blk * 128:
                                        slot * 512 + 256 + (blk + 1) * 128]
                            seqm.append((vh, pthi, [vkh, ("PT", slot)]))
                            seqm.append((vh, ptlo, []))
                            seqm.append((vl, pthi, [vkl]))
                        for si, (lh, rr, rds) in enumerate(seqm):
                            mm(obk, oout, lh, rr, rds,
                               start=(si == 0), stop=(si == 5),
                               close=(si == 5),
                               final=(w % 4 == 3 and hh == 1 and si == 5),
                               tp=otp)

                for w0 in range(3):
                    s_wave(w0)
                for w in range(NH // 2):
                    if w + 3 < NH // 2:
                        s_wave(w + 3)
                    head_work(w)
                for ob in range(2):
                    obk = ("ps", ob)
                    dsts = bass.AP(s_O, ob * 4 * GRP + cc * 128,
                                   [[8 * GRP, 128], [GRP, 4], [1, 128]])
                    srcs = bass.AP(ps[ob], 0, [[512, 128], [128, 4], [1, 128]])
                    cp("S", dsts, srcs, reads=[obk], writes=[("O", ob, cc)])
            dstc = bass.AP(s_k, 0, [[8 * 384, 128], [384, 8], [1, 128]])
            srcc = bass.AP(s_k, 256, [[8 * 384, 128], [384, 8], [1, 128]])
            pr.dma("Q1", dstc, srcc,
                   reads=[("k", b) for b in range(4)] + [("kcar",)],
                   writes=[("kcar",)])

        def phase_Wo(gi):
            gi2 = 2 * gi
            for ic in range(8):
                for hf in range(2):
                    b = panel_load("wo", ic, hf)
                    for tc in range(2):
                        bank = tc * 2 + hf
                        lh = s_O[:, ic * 256 + tc * 128:ic * 256 + (tc + 1) * 128]
                        mm(("ps", bank), ps[bank][:, 0:512], lh, s_wbuf[b][:],
                           [("wbuf", b), ("O", ic // 4, tc)],
                           start=(ic == 0), stop=(ic == 7), close=True,
                           final=(ic == 7))
            for tc in range(2):
                slot = (gi2 + tc) % 5
                kx = ("xc", slot)
                for hf in range(2):
                    bank = tc * 2 + hf
                    bk = ("ps", bank)
                    xap = s_xc[:, slot * D + hf * 512:slot * D + (hf + 1) * 512]
                    tt("V", xap, ps[bank][:, 0:512], xap, ALU.add,
                       reads=[bk, kx], writes=[kx])

        def phase_F0(pj):
            def strip_dma(idx):
                if idx >= 64:
                    return
                m, h = divmod(idx, 2)
                b = idx % 6
                assert ("d0sd", m) in pr.bufs
                pr.dma("Q1", s_d0b[b][:],
                       d0s_d[m * 128:(m + 1) * 128, h * 512:(h + 1) * 512],
                       reads=[("d0b", b), ("d0sd", m)], writes=[("d0b", b)])
            drain(5)
            for idx in range(5):
                strip_dma(idx)
            for m in range(32):
                drain(2)
                bank = 4 + (m % 2)
                bk = ("ps", bank)
                out = ps[bank][:, 0:512]
                for ic in range(8):
                    h = ic // 4
                    b = (m * 2 + h) % 6
                    lh = s_d0b[b][:, (ic % 4) * 128:(ic % 4 + 1) * 128]
                    rh = s_xn[:, ic * 512:(ic + 1) * 512]
                    rds = [("d0b", b)] + (xn_all if ic == 0 else [])
                    mm(bk, out, lh, rh, rds, start=(ic == 0), stop=(ic == 7),
                       close=(ic % 4 == 3), final=(ic == 7))
                    if ic % 4 == 3:
                        strip_dma(m * 2 + h + 5)
                act(s_a0[:, m * 512:(m + 1) * 512], out, AF.Relu, 0.0,
                    s_S0[:, 0:1], None, reads=[bk, ("S0",)],
                    writes=[("a0", m)])

        def g1_mms(m2):
            def strip_dma(idx):
                if idx >= 256:
                    return
                m2_, e_ = divmod(idx, 8)
                b = idx % 6
                assert ("w1pmd", m2_, e_ // 4) in pr.bufs
                pr.dma("Q1", s_k1b[b][:],
                       w1pm_d[m2_ * 128:(m2_ + 1) * 128,
                              e_ * 512:(e_ + 1) * 512],
                       reads=[("k1b", b), ("w1pmd", m2_, e_ // 4)],
                       writes=[("k1b", b)])
            drain(4)
            if m2 == 0:
                for idx in range(5):
                    strip_dma(idx)
            sl = m2 % 2
            assert ("w2pmd", m2) in pr.bufs
            pr.dma("Q1", s_w2b[sl][:], w2pm_d[m2 * 128:(m2 + 1) * 128, :],
                   reads=[("w2b", sl), ("w2pmd", m2)], writes=[("w2b", sl)])
            bank = m2 % 2
            bk = ("ps", bank)
            out = ps[bank][:, 0:512]
            for kc in range(32):
                e = kc // 4
                b = (m2 * 8 + e) % 6
                lh = s_k1b[b][:, (kc % 4) * 128:(kc % 4 + 1) * 128]
                rh = s_a0[:, kc * 512:(kc + 1) * 512]
                rds = [("k1b", b)] + (a0keys if kc == 0 else [])
                mm(bk, out, lh, rh, rds, start=(kc == 0), stop=(kc == 31),
                   close=(kc % 4 == 3), final=(kc == 31))
                if kc % 4 == 3:
                    strip_dma(m2 * 8 + e + 5)
            act(s_a1[:, sl * 512:(sl + 1) * 512], out, AF.Relu, 0.0,
                s_S1[:, 0:1], None, reads=[bk, ("S1",)], writes=[("a1", sl)])

        def g2_batch(pj, bt):
            batch = range(2 * bt, 2 * bt + 2)
            akeys = [("a1", m2 % 2) for m2 in batch]
            wkeys = [("w2b", m2 % 2) for m2 in batch]
            for ti in range(8):
                tc, oh = divmod(ti, 2)
                bank = 2 + (ti % 4)
                gk = ("ps", bank)
                for j, m2 in enumerate(batch):
                    sl = m2 % 2
                    mm(gk, ps[bank][:, 0:512],
                       s_a1[:, sl * 512 + tc * 128:sl * 512 + (tc + 1) * 128],
                       s_w2b[sl][:, oh * 512:(oh + 1) * 512],
                       (akeys + wkeys) if j == 0 else [],
                       start=(j == 0), stop=(j == 1), close=(j == 1),
                       final=True)
                slot = (4 * pj + tc) % 5
                xap = s_xc[:, slot * D + oh * 512:slot * D + (oh + 1) * 512]
                stt("V", xap, ps[bank][:, 0:512], s_S2[:, 0:1], xap,
                    ALU.mult, ALU.add, reads=[gk, ("S2",), ("xc", slot)],
                    writes=[("xc", slot)])

        def phase_F12(pj):
            gi0 = 2 * pj
            nxtA = gi0 + 2 if gi0 + 2 < len(groups) else None

            drain(40)

            def ystore(gi, cc):
                sq, g = groups[gi]
                slot = (2 * gi + cc) % 5
                r0 = sq * SEQ + g * GRP + cc * 128
                pr.dma("Q1", y_d[r0:r0 + 128, :],
                       s_xc[:, slot * D:(slot + 1) * D],
                       reads=[("xc", slot)], writes=[("ydone", gi, cc)])

            for bt in range(16):
                for m2 in range(2 * bt, 2 * bt + 2):
                    g1_mms(m2)
                g2_batch(pj, bt)
                if nxtA is not None:
                    if bt == 2:
                        phase_X_ln(nxtA, 0)
                    elif bt == 4:
                        phase_X_tr(nxtA, 0)
            ystore(gi0, 0)
            if nxtA is not None:
                phase_X_ln(nxtA, 1)
                phase_X_tr(nxtA, 1)
            ystore(gi0, 1)
            ystore(gi0 + 1, 0)
            ystore(gi0 + 1, 1)

        pre_consts()
        build_pre_tasks()
        for cc in range(2):
            xblock(0, cc)
        for pj in range(npairs):
            gia, gib = 2 * pj, 2 * pj + 1
            phase_P_qk(gia)
            phase_P_v(gia)
            phase_A(gia)
            phase_X_ln(gib, 0)
            phase_X_tr(gib, 0)
            phase_Wo(gia)
            phase_X_ln(gib, 1)
            phase_X_tr(gib, 1)
            phase_P_qk(gib)
            phase_P_v(gib)
            phase_A(gib)
            phase_Wo(gib)
            phase_F0(pj)
            phase_F12(pj)
        assert not pre_tasks, len(pre_tasks)

        pr.finalize_queues()
        pr.render()
    return nc




_CACHE = {}


def _consts():
    i = np.arange(128)[:, None]
    j = np.arange(256)[None, :]
    band = (j >= i + 1) & (j <= i + 128)
    valid0 = band & (j >= 128)
    cmsk = np.where(band, 0.0, MASK_RAW).astype(np.float32)
    cmsk0 = np.where(valid0, 0.0, MASK_RAW).astype(np.float32)
    cid = np.eye(128, dtype=ml_dtypes.bfloat16)
    cidr = np.eye(128, dtype=np.float32)
    cn12 = np.full((128, 1), EXP_SHIFT, dtype=np.float32)
    return cmsk, cmsk0, cid, cidr, cn12


def make_inputs(L0_attn_x, L0_attn_Wq, L0_attn_Wk, L0_attn_Wv, L0_attn_Wo,
                L0_ffn_k0, L0_ffn_k1, L0_ffn_k2, L0_ffn_s0, L0_ffn_s1,
                L0_ffn_s2):
    x = np.ascontiguousarray(np.asarray(L0_attn_x, dtype=np.float32))
    wqt = np.ascontiguousarray(np.asarray(L0_attn_Wq, np.float32).T)
    wkt = np.ascontiguousarray(np.asarray(L0_attn_Wk, np.float32).T)
    wvt = np.ascontiguousarray(np.asarray(L0_attn_Wv, np.float32).T)
    wot = np.ascontiguousarray(np.asarray(L0_attn_Wo, np.float32).T)
    k0 = np.asarray(L0_ffn_k0).astype(np.uint8)
    k0s = np.ascontiguousarray(
        k0.reshape(16, 128, 32, 128).transpose(2, 1, 0, 3)).reshape(HID, 2 * D)
    k1 = np.asarray(L0_ffn_k1).astype(np.uint8)
    k1s = np.ascontiguousarray(
        k1.reshape(32, 128, 32, 128).transpose(2, 1, 0, 3)).reshape(HID, HID)
    k2 = np.ascontiguousarray(np.asarray(L0_ffn_k2).astype(np.uint8))
    scal = np.zeros((128, 4), np.float32)
    scal[:, 0] = np.float32(L0_ffn_s0)
    scal[:, 1] = np.float32(L0_ffn_s1)
    scal[:, 2] = np.float32(L0_ffn_s2)
    cmsk, cmsk0, cid, cidr, cn12 = _consts()
    czero = np.zeros((128, 8 * 384), np.float32)
    shared = dict(wqt=wqt, wkt=wkt, wvt=wvt, wot=wot, k0s=k0s, k1s=k1s, k2=k2,
                  scal=scal, cmsk=cmsk, cmsk0=cmsk0, cid=cid, cidr=cidr,
                  cn12=cn12, czero=czero)
    in_maps = []
    for c in range(8):
        m = dict(shared)
        m["x"] = np.ascontiguousarray(x[2 * c:2 * c + 2].reshape(2 * SEQ, D))
        in_maps.append(m)
    return in_maps


def kernel(L0_attn_x, L0_attn_Wq, L0_attn_Wk, L0_attn_Wv, L0_attn_Wo,
           L0_ffn_k0, L0_ffn_k1, L0_ffn_k2, L0_ffn_s0, L0_ffn_s1, L0_ffn_s2):
    in_maps = make_inputs(L0_attn_x, L0_attn_Wq, L0_attn_Wk, L0_attn_Wv,
                          L0_attn_Wo, L0_ffn_k0, L0_ffn_k1, L0_ffn_k2,
                          L0_ffn_s0, L0_ffn_s1, L0_ffn_s2)
    if "nc" not in _CACHE:
        _CACHE["nc"] = build_program()
    nc = _CACHE["nc"]
    res = run_bass_kernel_spmd(nc, in_maps, core_ids=list(range(8)))
    out = np.empty((16, SEQ, D), np.float32)
    for c in range(8):
        out[2 * c:2 * c + 2] = np.asarray(res.results[c]["y"]).reshape(2, SEQ, D)
    return out
